# version 07
# baseline (speedup 1.0000x reference)
import numpy as np
import ml_dtypes
from contextlib import ExitStack

import concourse.bass as bass
import concourse.mybir as mybir
from concourse.bass_utils import run_bass_kernel_spmd

F32 = mybir.dt.float32
BF16 = mybir.dt.bfloat16
I32 = mybir.dt.int32
AF = mybir.ActivationFunctionType
ALU = mybir.AluOpType
AX = mybir.AxisListType

NCORES = 8
B, T, D = 32, 2048, 1024
BPC = B // NCORES
NH, HS, NL = 16, 64, 64
H1, M1 = 64, 16
EPS = 1e-5

_NC_CACHE = {}


def _bf(x):
    return np.ascontiguousarray(x).astype(ml_dtypes.bfloat16)


class Chain:

    def __init__(self):
        self.ops = []

    def add(self, engine, fn, inc=1):
        self.ops.append((engine, fn, inc))

    def dma(self, engine, out, in_, **kw):
        self.add(engine, lambda e, o=out, i=in_, kw=kw: e.dma_start(out=o, in_=i, **kw), 16)

    def build(self, nc):
        pre = []
        c = 0
        for _, _, inc in self.ops:
            pre.append(c)
            c += inc
        with nc.semaphore("chain") as sem, nc.Block() as block:
            for engname in ("sync", "scalar", "vector", "tensor", "gpsimd"):
                idxs = [i for i, (e, _, _) in enumerate(self.ops) if e == engname]
                if not idxs:
                    continue

                def body(eng, idxs=idxs):
                    for i in idxs:
                        _, fn, inc = self.ops[i]
                        if pre[i] > 0:
                            eng.wait_ge(sem, pre[i])
                        last = fn(eng)
                        last.then_inc(sem, inc)

                getattr(block, engname)(body)
            total = c

            def tail(eng):
                eng.wait_ge(sem, total)
            block.sync(tail)


def _run(nc, in_maps):
    from concourse.aot_env import allow_ici_all_to_all
    with allow_ici_all_to_all():
        return run_bass_kernel_spmd(nc, in_maps, core_ids=list(range(NCORES)))



def _ln_tile(ch, v_eng, s_eng, xt, stats, xnb_out, n=D, extra_outs=()):
    nhalf = n // 2
    ch.add("vector", lambda e: e.bn_stats(out=stats[:, 0:6], in_=xt[:, 0:nhalf]))
    ch.add("vector", lambda e: e.bn_stats(out=stats[:, 6:12], in_=xt[:, nhalf:n]))
    ch.add("vector", lambda e: e.bn_aggr(out=stats[:, 12:14], in_=stats[:, 0:12]))
    ch.add("vector", lambda e: e.tensor_scalar(out=stats[:, 13:14], in0=stats[:, 13:14],
                                               scalar1=EPS, scalar2=None, op0=ALU.add))
    ch.add("scalar", lambda e: e.activation(out=stats[:, 14:15], in_=stats[:, 13:14],
                                            func=AF.Sqrt))
    ch.add("vector", lambda e: e.reciprocal(out=stats[:, 15:16], in_=stats[:, 14:15]))
    ch.add("vector", lambda e: e.tensor_scalar(out=stats[:, 12:13], in0=stats[:, 12:13],
                                               scalar1=-1.0, scalar2=None, op0=ALU.mult))
    for outap in (xnb_out,) + tuple(extra_outs):
        ch.add("vector", lambda e, o=outap: e.tensor_scalar(
            out=o, in0=xt, scalar1=stats[:, 12:13], scalar2=stats[:, 15:16],
            op0=ALU.add, op1=ALU.mult))


def build_ln0():
    nc = bass.Bass("TRN2", target_bir_lowering=False)
    x = nc.dram_tensor("x", [BPC * T // 128, 128, D], F32, kind="ExternalInput").ap()
    xn0 = nc.dram_tensor("xn0", [BPC * T // 128, 128, D], BF16, kind="ExternalOutput").ap()
    ntiles = BPC * T // 128
    with ExitStack() as ctx:
        e = ctx.enter_context
        xt = e(nc.sbuf_tensor("xt", [128, D], F32))
        xnb = e(nc.sbuf_tensor("xnb", [128, D], BF16))
        stats = e(nc.sbuf_tensor("stats", [128, 16], F32))
        ch = Chain()
        for i in range(ntiles):
            ch.dma("sync", xt[:], x[i])
            _ln_tile(ch, None, None, xt[:], stats, xnb[:])
            ch.dma("sync", xn0[i], xnb[:])
        ch.build(nc)
    return nc


def stage_ln0(x_full):
    nc = _NC_CACHE.setdefault("ln0", build_ln0())
    shards = x_full.reshape(NCORES, BPC * T // 128, 128, D)
    res = _run(nc, [{"x": np.ascontiguousarray(shards[c])} for c in range(NCORES)])
    out = np.stack([r["xn0"] for r in res.results])
    return out.reshape(B, T, D)



def build_attn0():
    nc = bass.Bass("TRN2", target_bir_lowering=False)
    xn0T = nc.dram_tensor("xn0T", [BPC, 128, 8, T], BF16, kind="ExternalInput").ap()
    wqk = nc.dram_tensor("wqk", [16, 128, 8, 128], F32, kind="ExternalInput").ap()
    wv = nc.dram_tensor("wv", [128, 8, D], F32, kind="ExternalInput").ap()
    wp = nc.dram_tensor("wp", [128, 8, D], F32, kind="ExternalInput").ap()
    eyes = nc.dram_tensor("eyes", [64, 3, 64], F32, kind="ExternalInput").ap()
    eye64 = nc.dram_tensor("eye64", [64, 64], F32, kind="ExternalInput").ap()
    onesf_i = nc.dram_tensor("onesf", [128, 128], F32, kind="ExternalInput").ap()
    onesbf_i = nc.dram_tensor("onesbf", [128, 8], BF16, kind="ExternalInput").ap()
    s0_o = nc.dram_tensor("s0", [BPC, 16, 128, D], F32, kind="ExternalOutput").ap()

    with ExitStack() as ctx:
        e = ctx.enter_context
        qT = e(nc.sbuf_tensor("qT", [128, 8, T], BF16))
        kT = e(nc.sbuf_tensor("kT", [128, 8, T], BF16))
        vb = e(nc.sbuf_tensor("vb", [128, 16, D], BF16))
        yT = e(nc.sbuf_tensor("yT", [128, 8, T], BF16))
        xb = e(nc.sbuf_tensor("xb", [128, 8, T], BF16))
        wres = e(nc.sbuf_tensor("wres", [128, 8, D], BF16))
        wstrip = e(nc.sbuf_tensor("wstrip", [128, 8, 128], BF16))
        expNT = e(nc.sbuf_tensor("expNT", [128, 16, 64], BF16))
        expLT = e(nc.sbuf_tensor("expLT", [64, T], F32))
        repsb = e(nc.sbuf_tensor("repsb", [128, 512], F32))
        rzlsb = e(nc.sbuf_tensor("rzlsb", [128, 512], F32))
        eyesb = e(nc.sbuf_tensor("eyesb", [64, 3, 64], F32))
        eye64b = e(nc.sbuf_tensor("eye64b", [64, 64], F32))
        onesf = e(nc.sbuf_tensor("onesfb", [128, 128], F32))
        onesbf = e(nc.sbuf_tensor("onesbfb", [128, 8], BF16))
        sc_ql = e(nc.sbuf_tensor("sc_ql", [128, 64], F32))
        sc_kl = e(nc.sbuf_tensor("sc_kl", [128, 64], F32))
        sc_qlbf = e(nc.sbuf_tensor("sc_qlbf", [128, 64], BF16))
        sc_klbf = e(nc.sbuf_tensor("sc_klbf", [128, 64], BF16))
        Ksb = e(nc.sbuf_tensor("Ksb", [64, 64], F32))
        Kt_sb = e(nc.sbuf_tensor("Kt_sb", [64, 64], F32))
        V_sb = e(nc.sbuf_tensor("V_sb", [64, 64], F32))
        W_sb = e(nc.sbuf_tensor("W_sb", [64, 64], F32))
        T_sb = e(nc.sbuf_tensor("T_sb", [64, 64], F32))
        KVt_sb = e(nc.sbuf_tensor("KVt_sb", [64, 64], F32))
        z1sb = e(nc.sbuf_tensor("z1sb", [64, 64], F32))
        z2aug = e(nc.sbuf_tensor("z2aug", [64, 65], F32))
        smalls = e(nc.sbuf_tensor("smalls", [64, 8], F32))
        s0t = e(nc.sbuf_tensor("s0t", [128, D], F32))
        P0 = e(nc.psum_tensor("P0", [128, 512], F32))
        P1 = e(nc.psum_tensor("P1", [128, 512], F32))
        P2 = e(nc.psum_tensor("P2", [128, 512], F32))
        PBIG = e(nc.psum_tensor("PBIG", [128, 1024], F32))

        ch = Chain()
        ch.dma("sync", eyesb[:], eyes)
        ch.dma("sync", eye64b[:], eye64)
        ch.dma("sync", onesf[:], onesf_i)
        ch.dma("sync", onesbf[:], onesbf_i)
        ch.add("vector", lambda e: e.memset(z2aug[:, 64:65], 1.0))

        def mm(ps, lhsT, rhs, start=True, stop=True):
            return lambda e: e.matmul(out=ps, lhsT=lhsT, rhs=rhs, start=start, stop=stop)

        for b in range(BPC):
            ch.dma("gpsimd", xb[:], xn0T[b])
            for part, dst in ((0, qT), (1, kT)):
                for cc in range(8):
                    ch.dma("gpsimd", wstrip[:], wqk[part * 8 + cc])
                    for tc in range(4):
                        def qmm(e, cc=cc, tc=tc, dst=dst):
                            for dc in range(8):
                                r = e.matmul(out=P0[:, 0:512],
                                             lhsT=wstrip[:, dc, :],
                                             rhs=xb[:, dc, tc * 512:(tc + 1) * 512],
                                             start=(dc == 0), stop=(dc == 7))
                            return r
                        ch.add("tensor", qmm)
                        ch.add("scalar", lambda e, cc=cc, tc=tc, dst=dst: e.copy(
                            out=dst[:, cc, tc * 512:(tc + 1) * 512], in_=P0[:, 0:512]))
            ch.dma("gpsimd", wres[:], wv)
            for tt in range(16):
                for fh in range(2):
                    def vmm(e, tt=tt, fh=fh):
                        for dc in range(8):
                            r = e.matmul(out=P0[:, 0:512],
                                         lhsT=xb[:, dc, tt * 128:(tt + 1) * 128],
                                         rhs=wres[:, dc, fh * 512:(fh + 1) * 512],
                                         start=(dc == 0), stop=(dc == 7))
                        return r
                    ch.add("tensor", vmm)
                    ch.add("scalar", lambda e, tt=tt, fh=fh: e.copy(
                        out=vb[:, tt, fh * 512:(fh + 1) * 512], in_=P0[:, 0:512]))

            for h in range(NH):
                bb = 64 * (h % 2)
                cc = h // 2
                qTh = qT[bb:bb + 64, cc, :]
                kTh = kT[bb:bb + 64, cc, :]
                ch.add("vector", lambda e, qTh=qTh, bb=bb: e.tensor_reduce(
                    out=sc_ql[bb:bb + 64, :], in_=qTh.rearrange("p (l g) -> p l g", g=32),
                    axis=AX.X, op=ALU.add))
                ch.add("vector", lambda e, kTh=kTh, bb=bb: e.tensor_reduce(
                    out=sc_kl[bb:bb + 64, :], in_=kTh.rearrange("p (l g) -> p l g", g=32),
                    axis=AX.X, op=ALU.add))
                ch.add("scalar", lambda e, bb=bb: e.copy(out=sc_qlbf[bb:bb + 64, :], in_=sc_ql[bb:bb + 64, :]))
                ch.add("scalar", lambda e, bb=bb: e.copy(out=sc_klbf[bb:bb + 64, :], in_=sc_kl[bb:bb + 64, :]))
                ch.add("tensor", mm(P0[0:64, 0:64], sc_ql[bb:bb + 64, :], sc_kl[bb:bb + 64, :]))
                ch.add("scalar", lambda e: e.activation(out=Ksb[:, :], in_=P0[0:64, 0:64],
                                                        func=AF.Exp, scale=1.0 / 1024.0))
                ch.add("vector", lambda e: e.tensor_reduce(out=smalls[:, 0:1], in_=Ksb[:, :],
                                                           axis=AX.X, op=ALU.add))
                ch.add("vector", lambda e: e.reciprocal(out=smalls[:, 1:2], in_=smalls[:, 0:1]))
                ch.add("vector", lambda e: e.tensor_scalar(out=Ksb[:, :], in0=Ksb[:, :],
                                                           scalar1=smalls[:, 1:2], scalar2=None, op0=ALU.mult))
                ch.add("tensor", mm(P1[0:1, 0:64], onesf[0:64, 0:1], Ksb[:, :]))
                ch.add("vector", lambda e: e.reduce_max(out=smalls[0:1, 2:3], in_=P1[0:1, 0:64], axis=AX.X))
                ch.add("vector", lambda e: e.reciprocal(out=smalls[0:1, 3:4], in_=smalls[0:1, 2:3]))
                ch.add("tensor", lambda e: e.matmul(out=P2[0:64, 0:1], lhsT=onesf[0:1, 0:64],
                                                    rhs=smalls[0:1, 3:4], start=True, stop=True))
                ch.add("tensor", lambda e: e.transpose(out=P0[0:64, 0:64], in_=Ksb[:, :], identity=eye64b[:]))
                ch.add("vector", lambda e: e.tensor_scalar(out=V_sb[:], in0=P0[0:64, 0:64],
                                                           scalar1=P2[0:64, 0:1], scalar2=None, op0=ALU.mult))
                ch.add("vector", lambda e: e.tensor_scalar(out=W_sb[:], in0=Ksb[:, :],
                                                           scalar1=P2[0:64, 0:1], scalar2=None, op0=ALU.mult))
                ch.add("scalar", lambda e: e.copy(out=Kt_sb[:], in_=P0[0:64, 0:64]))
                for _ in range(6):
                    ch.add("tensor", mm(P0[0:64, 0:64], Kt_sb[:], V_sb[:]))
                    ch.add("vector", lambda e: e.tensor_tensor(out=T_sb[:], in0=eyesb[:, 0, :],
                                                               in1=P0[0:64, 0:64], op=ALU.subtract))
                    ch.add("tensor", mm(P1[0:64, 0:64], V_sb[:], Kt_sb[:]))
                    ch.add("scalar", lambda e: e.copy(out=KVt_sb[:], in_=P1[0:64, 0:64]))
                    ch.add("tensor", mm(P0[0:64, 0:64], KVt_sb[:], T_sb[:]))
                    ch.add("vector", lambda e: e.tensor_tensor(out=T_sb[:], in0=eyesb[:, 1, :],
                                                               in1=P0[0:64, 0:64], op=ALU.subtract))
                    ch.add("tensor", mm(P0[0:64, 0:64], KVt_sb[:], T_sb[:]))
                    ch.add("vector", lambda e: e.tensor_tensor(out=T_sb[:], in0=eyesb[:, 2, :],
                                                               in1=P0[0:64, 0:64], op=ALU.subtract))
                    ch.add("tensor", mm(P1[0:64, 0:64], W_sb[:], T_sb[:]))
                    ch.add("tensor", mm(P2[0:64, 0:64], T_sb[:], W_sb[:]))
                    ch.add("scalar", lambda e: e.mul(V_sb[:], P1[0:64, 0:64], 0.25))
                    ch.add("scalar", lambda e: e.mul(W_sb[:], P2[0:64, 0:64], 0.25))
                for tc in range(16):
                    ch.add("tensor", lambda e, tc=tc, bb=bb, cc=cc: e.matmul(
                        out=P0[0:128, 0:64], lhsT=kT[bb:bb + 64, cc, tc * 128:(tc + 1) * 128],
                        rhs=sc_qlbf[bb:bb + 64, :], start=True, stop=True))
                    ch.add("scalar", lambda e, tc=tc: e.activation(
                        out=expNT[:, tc, :], in_=P0[0:128, 0:64], func=AF.Exp, scale=1.0 / 32.0))
                def z1mm(e, h=h):
                    for tc in range(16):
                        e.matmul(out=P2[0:64, 0:64], lhsT=expNT[:, tc, :],
                                 rhs=vb[:, tc, h * 64:(h + 1) * 64],
                                 start=(tc == 0), stop=(tc == 15))
                        r = e.matmul(out=P1[0:64, 0:1], lhsT=expNT[:, tc, :],
                                     rhs=onesbf[:, 0:1], start=(tc == 0), stop=(tc == 15))
                    return r
                ch.add("tensor", z1mm)
                ch.add("vector", lambda e: e.reciprocal(out=smalls[:, 4:5], in_=P1[0:64, 0:1]))
                ch.add("vector", lambda e: e.tensor_scalar(out=z1sb[:], in0=P2[0:64, 0:64],
                                                           scalar1=smalls[:, 4:5], scalar2=None, op0=ALU.mult))
                ch.add("tensor", mm(P0[0:64, 0:64], W_sb[:], z1sb[:]))
                ch.add("scalar", lambda e: e.copy(out=z2aug[:, 0:64], in_=P0[0:64, 0:64]))
                for tc in range(4):
                    ch.add("tensor", lambda e, tc=tc, bb=bb, cc=cc: e.matmul(
                        out=P1[0:64, 0:512], lhsT=sc_klbf[bb:bb + 64, :],
                        rhs=qT[bb:bb + 64, cc, tc * 512:(tc + 1) * 512], start=True, stop=True))
                    ch.add("scalar", lambda e, tc=tc: e.activation(
                        out=expLT[:, tc * 512:(tc + 1) * 512], in_=P1[0:64, 0:512],
                        func=AF.Exp, scale=1.0 / 32.0))
                for tc in range(4):
                    ch.add("tensor", lambda e, tc=tc: e.matmul(
                        out=PBIG[0:65, 0:512], lhsT=z2aug[:, :],
                        rhs=expLT[:, tc * 512:(tc + 1) * 512], start=True, stop=True))
                    ch.add("vector", lambda e: e.reciprocal(out=rzlsb[64:65, 0:512], in_=PBIG[64:65, 0:512]))
                    ch.add("tensor", lambda e: e.matmul(out=P2[0:64, 0:512], lhsT=onesf[64:65, 0:64],
                                                        rhs=rzlsb[64:65, 0:512], start=True, stop=True))
                    ch.add("scalar", lambda e: e.copy(out=repsb[0:64, 0:512], in_=P2[0:64, 0:512]))
                    ch.add("vector", lambda e, tc=tc, bb=bb, cc=cc: e.tensor_tensor(
                        out=yT[bb:bb + 64, cc, tc * 512:(tc + 1) * 512],
                        in0=PBIG[0:64, 0:512], in1=repsb[0:64, 0:512], op=ALU.mult))

            ch.dma("gpsimd", wres[:], wp)
            for tt in range(16):
                for fh in range(2):
                    def pmm(e, tt=tt, fh=fh):
                        for dc in range(8):
                            r = e.matmul(out=P0[:, 0:512],
                                         lhsT=yT[:, dc, tt * 128:(tt + 1) * 128],
                                         rhs=wres[:, dc, fh * 512:(fh + 1) * 512],
                                         start=(dc == 0), stop=(dc == 7))
                        return r
                    ch.add("tensor", pmm)
                    ch.add("scalar", lambda e, tt=tt, fh=fh: e.copy(
                        out=s0t[:, fh * 512:(fh + 1) * 512], in_=P0[:, 0:512]))
                ch.dma("sync", s0_o[b, tt], s0t[:])
        ch.build(nc)
    return nc


def stage_attn0(xn0, W_attn, W_proj):
    nc = _NC_CACHE.setdefault("attn0", build_attn0())
    wqk = np.ascontiguousarray(
        W_attn[:, :2048].reshape(8, 128, 16, 128).transpose(2, 1, 0, 3))
    wv = np.ascontiguousarray(W_attn[:, 2048:].reshape(8, 128, 1024).transpose(1, 0, 2))
    wp = np.ascontiguousarray(W_proj.reshape(8, 128, 1024).transpose(1, 0, 2))
    eyes1 = np.ascontiguousarray(np.stack(
        [c * np.eye(64, dtype=np.float32) for c in (7.0, 15.0, 13.0)]).transpose(1, 0, 2))
    eyes = np.concatenate([eyes1, eyes1], axis=0)
    eye64 = np.eye(64, dtype=np.float32)
    onesf = np.ones((128, 128), np.float32)
    onesbf = _bf(np.ones((128, 8), np.float32))
    in_maps = []
    for c in range(NCORES):
        xcT = np.ascontiguousarray(
            np.asarray(xn0[c * BPC:(c + 1) * BPC]).transpose(0, 2, 1))
        in_maps.append({
            "xn0T": np.ascontiguousarray(xcT.reshape(BPC, 8, 128, T).transpose(0, 2, 1, 3)),
            "wqk": wqk, "wv": wv, "wp": wp, "eyes": eyes, "eye64": eye64,
            "onesf": onesf, "onesbf": onesbf,
        })
    res = _run(nc, in_maps)
    s0 = np.stack([r["s0"] for r in res.results])
    return s0.reshape(B, T, D)



def build_ffn0():
    nc = bass.Bass("TRN2", target_bir_lowering=False)
    gxnT = nc.dram_tensor("gxnT", [128, 8, 2048], BF16, kind="ExternalInput").ap()
    w1 = nc.dram_tensor("w1", [128, 32, 8, 128], F32, kind="ExternalInput").ap()
    w2 = nc.dram_tensor("w2", [8, 128, 32, 128], F32, kind="ExternalInput").ap()
    b1i = nc.dram_tensor("b1", [128, 32], F32, kind="ExternalInput").ap()
    b2i = nc.dram_tensor("b2", [128, 8], F32, kind="ExternalInput").ap()
    wvec = nc.dram_tensor("wvec", [1, 2048], F32, kind="ExternalInput").ap()
    onesf_i = nc.dram_tensor("onesf", [128, 128], F32, kind="ExternalInput").ap()
    wT_o = nc.dram_tensor("wT", [8, 128, 2048], BF16, kind="ExternalOutput").ap()

    C_GELU = 0.7978845608028654
    with ExitStack() as ctx:
        e = ctx.enter_context
        gx = e(nc.sbuf_tensor("gx", [128, 8, 2048], BF16))
        w1sb = e(nc.sbuf_tensor("w1sb", [128, 32, 8, 128], BF16))
        w2sb = e(nc.sbuf_tensor("w2sb", [128, 32, 128], BF16))
        b1sb = e(nc.sbuf_tensor("b1sb", [128, 32], F32))
        b2sb = e(nc.sbuf_tensor("b2sb", [128, 8], F32))
        onesf = e(nc.sbuf_tensor("onesfb", [128, 128], F32))
        wrow = e(nc.sbuf_tensor("wrow", [1, 2048], F32))
        wrep = e(nc.sbuf_tensor("wrep", [128, 2048], F32))
        hTq = e(nc.sbuf_tensor("hTq", [128, 32, 512], BF16))
        ga = e(nc.sbuf_tensor("ga", [128, 512], F32))
        gb = e(nc.sbuf_tensor("gb", [128, 512], F32))
        gc = e(nc.sbuf_tensor("gc", [128, 512], F32))
        wout = e(nc.sbuf_tensor("wout", [128, 512], BF16))
        PH = e(nc.psum_tensor("PH", [128, 512], F32))
        PO = e(nc.psum_tensor("PO", [128, 512], F32))
        PR = e(nc.psum_tensor("PR", [128, 512], F32))

        ch = Chain()
        ch.dma("gpsimd", gx[:], gxnT)
        ch.dma("gpsimd", w1sb[:], w1)
        ch.dma("sync", b1sb[:], b1i)
        ch.dma("sync", b2sb[:], b2i)
        ch.dma("sync", onesf[:], onesf_i)
        ch.dma("sync", wrow[:], wvec)
        for tc in range(4):
            ch.add("tensor", lambda e, tc=tc: e.matmul(
                out=PR[:, 0:512], lhsT=onesf[0:1, 0:128],
                rhs=wrow[0:1, tc * 512:(tc + 1) * 512], start=True, stop=True))
            ch.add("scalar", lambda e, tc=tc: e.copy(
                out=wrep[:, tc * 512:(tc + 1) * 512], in_=PR[:, 0:512]))

        for tq in range(4):
            for ffc in range(32):
                def hmm_(e, tq=tq, ffc=ffc):
                    for dc in range(8):
                        r = e.matmul(out=PH[:, 0:512], lhsT=w1sb[:, ffc, dc, :],
                                     rhs=gx[:, dc, tq * 512:(tq + 1) * 512],
                                     start=(dc == 0), stop=(dc == 7))
                    return r
                ch.add("tensor", hmm_)
                ch.add("vector", lambda e, ffc=ffc: e.tensor_scalar(
                    out=ga[:], in0=PH[:, 0:512], scalar1=b1sb[:, ffc:ffc + 1],
                    scalar2=None, op0=ALU.add))
                ch.add("scalar", lambda e: e.activation(out=gb[:], in_=ga[:], func=AF.Square))
                ch.add("vector", lambda e: e.tensor_scalar(
                    out=gb[:], in0=gb[:], scalar1=0.044715, scalar2=1.0,
                    op0=ALU.mult, op1=ALU.add))
                ch.add("vector", lambda e: e.tensor_tensor(out=gc[:], in0=gb[:], in1=ga[:], op=ALU.mult))
                ch.add("scalar", lambda e: e.activation(out=gc[:], in_=gc[:], func=AF.Tanh, scale=C_GELU))
                ch.add("vector", lambda e: e.tensor_scalar(
                    out=gc[:], in0=gc[:], scalar1=1.0, scalar2=0.5, op0=ALU.add, op1=ALU.mult))
                ch.add("vector", lambda e, ffc=ffc: e.tensor_tensor(
                    out=hTq[:, ffc, :], in0=gc[:], in1=ga[:], op=ALU.mult))
            for dc in range(8):
                ch.dma("gpsimd", w2sb[:], w2[dc])
                def omm(e):
                    for ffc in range(32):
                        r = e.matmul(out=PO[:, 0:512], lhsT=w2sb[:, ffc, :],
                                     rhs=hTq[:, ffc, :], start=(ffc == 0), stop=(ffc == 31))
                    return r
                ch.add("tensor", omm)
                ch.add("vector", lambda e, dc=dc: e.tensor_scalar(
                    out=ga[:], in0=PO[:, 0:512], scalar1=b2sb[:, dc:dc + 1],
                    scalar2=None, op0=ALU.add))
                ch.add("vector", lambda e, tq=tq: e.tensor_tensor(
                    out=wout[:], in0=ga[:], in1=wrep[:, tq * 512:(tq + 1) * 512], op=ALU.mult))
                ch.dma("sync", wT_o[dc, :, tq * 512:(tq + 1) * 512], wout[:])
        ch.build(nc)
    return nc


def stage_ffn0(xn0_bf, top_idx, combine_weights, W1, b1, W2, b2):
    nc = _NC_CACHE.setdefault("ffn0", build_ffn0())
    tok = np.asarray(top_idx).T
    xf = np.asarray(xn0_bf).reshape(B * T, D)
    in_maps = []
    for c in range(NCORES):
        gathered = xf[tok[c]]
        gxnT = np.ascontiguousarray(gathered.T).reshape(8, 128, 2048).transpose(1, 0, 2)
        w1r = W1[c].reshape(8, 128, 32, 128).transpose(1, 2, 0, 3)
        w2r = W2[c].reshape(32, 128, 8, 128).transpose(2, 1, 0, 3)
        b1r = b1[c].reshape(32, 128).T
        b2r = b2[c].reshape(8, 128).T
        wv = combine_weights[tok[c], c].reshape(1, 2048)
        in_maps.append({
            "gxnT": np.ascontiguousarray(gxnT), "w1": np.ascontiguousarray(w1r),
            "w2": np.ascontiguousarray(w2r), "b1": np.ascontiguousarray(b1r),
            "b2": np.ascontiguousarray(b2r), "wvec": np.ascontiguousarray(wv),
            "onesf": np.ones((128, 128), np.float32),
        })
    res = _run(nc, in_maps)
    wT = np.stack([r["wT"] for r in res.results])
    weighted = wT.reshape(NCORES, 8, 128, 2048).transpose(0, 3, 1, 2).reshape(NCORES, 2048, 1024)
    return np.ascontiguousarray(weighted)



def build_scatter():
    nc = bass.Bass("TRN2", target_bir_lowering=False)
    wall = nc.dram_tensor("wall", [8, 16, 128, 1024], BF16, kind="ExternalInput").ap()
    idxi = nc.dram_tensor("idxi", [128, 8, 16], I32, kind="ExternalInput").ap()
    rbase = nc.dram_tensor("rbase", [128, 1], F32, kind="ExternalInput").ap()
    iota_i = nc.dram_tensor("iota", [128, 1024], F32, kind="ExternalInput").ap()
    osc = nc.dram_tensor("osc", [8, 128, 1024], F32, kind="ExternalOutput").ap()

    with ExitStack() as ctx:
        e = ctx.enter_context
        idxsb = e(nc.sbuf_tensor("idxsb", [128, 128], I32))
        idxf = e(nc.sbuf_tensor("idxf", [128, 128], F32))
        iotasb = e(nc.sbuf_tensor("iotasb", [128, 1024], F32))
        rbasesb = e(nc.sbuf_tensor("rbasesb", [128, 1], F32))
        ohsb = e(nc.sbuf_tensor("ohsb", [128, 128], BF16))
        wchunk = e(nc.sbuf_tensor("wchunk", [128, 1024], BF16))
        outsb = e(nc.sbuf_tensor("outsb", [128, 1024], F32))
        PR0 = e(nc.psum_tensor("PR0", [128, 512], F32))
        PR1 = e(nc.psum_tensor("PR1", [128, 512], F32))

        ch = Chain()
        ch.dma("sync", idxsb[:].rearrange("p (e s) -> p e s", e=8), idxi)
        ch.dma("sync", iotasb[:], iota_i)
        ch.dma("sync", rbasesb[:], rbase)
        ch.add("scalar", lambda e: e.copy(out=idxf[:], in_=idxsb[:]))
        ch.add("vector", lambda e: e.tensor_scalar(
            out=idxf[:], in0=idxf[:], scalar1=rbasesb[:, 0:1], scalar2=None,
            op0=ALU.subtract))
        for rt in range(8):
            first = True
            for ee in range(8):
                for sc in range(16):
                    ch.dma("gpsimd", wchunk[:], wall[ee, sc])
                    ch.add("vector", lambda e, rt=rt, ee=ee, sc=sc: e.tensor_scalar(
                        out=ohsb[:], in0=iotasb[:, rt * 128:(rt + 1) * 128],
                        scalar1=idxf[:, ee * 16 + sc:ee * 16 + sc + 1],
                        scalar2=None, op0=ALU.is_equal))
                    ch.add("tensor", lambda e, first=first: (
                        e.matmul(out=PR0[:, 0:512], lhsT=ohsb[:], rhs=wchunk[:, 0:512],
                                 start=first, stop=False),
                        e.matmul(out=PR1[:, 0:512], lhsT=ohsb[:], rhs=wchunk[:, 512:1024],
                                 start=first, stop=False))[-1])
                    first = False
            ch.add("scalar", lambda e: e.copy(out=outsb[:, 0:512], in_=PR0[:, 0:512]))
            ch.add("scalar", lambda e: e.copy(out=outsb[:, 512:1024], in_=PR1[:, 0:512]))
            ch.dma("sync", osc[rt], outsb[:])
        ch.build(nc)
    return nc


def stage_scatter(weighted, top_idx):
    nc = _NC_CACHE.setdefault("scatter", build_scatter())
    tok = np.asarray(top_idx).T.astype(np.int32)
    wall = np.ascontiguousarray(weighted.reshape(8, 16, 128, 1024))
    idxi = np.ascontiguousarray(tok.reshape(8, 16, 128).transpose(2, 0, 1))
    iota = np.broadcast_to(np.arange(1024, dtype=np.float32), (128, 1024)).copy()
    in_maps = []
    for c in range(NCORES):
        in_maps.append({
            "wall": wall, "idxi": idxi,
            "rbase": np.full((128, 1), 1024.0 * c, np.float32),
            "iota": iota,
        })
    res = _run(nc, in_maps)
    osc = np.stack([r["osc"] for r in res.results])
    return osc.reshape(8192, 1024)



def build_combine0():
    nc = bass.Bass("TRN2", target_bir_lowering=False)
    NTIL = BPC * T // 128
    x_i = nc.dram_tensor("x", [NTIL, 128, D], F32, kind="ExternalInput").ap()
    s0_i = nc.dram_tensor("s0", [NTIL, 128, D], F32, kind="ExternalInput").ap()
    g0_i = nc.dram_tensor("g0pre", [NTIL, 128, D], F32, kind="ExternalInput").ap()
    nw_i = nc.dram_tensor("nw", [1, D], F32, kind="ExternalInput").ap()
    wr_i = nc.dram_tensor("wr", [128, 8, 8], F32, kind="ExternalInput").ap()
    br_i = nc.dram_tensor("br", [1, 8], F32, kind="ExternalInput").ap()
    eye128 = nc.dram_tensor("eye128", [128, 128], F32, kind="ExternalInput").ap()
    eye4r_i = nc.dram_tensor("eye4r", [1, BPC * BPC], F32, kind="ExternalInput").ap()
    onesf_i = nc.dram_tensor("onesf", [128, 128], F32, kind="ExternalInput").ap()
    iota8_i = nc.dram_tensor("iota8", [128, 8], F32, kind="ExternalInput").ap()
    x1_o = nc.dram_tensor("x1", [NTIL, 128, D], F32, kind="ExternalOutput").ap()
    xn1_o = nc.dram_tensor("xn1", [NTIL, 128, D], F32, kind="ExternalOutput").ap()
    xn1b_o = nc.dram_tensor("xn1b", [NTIL, 128, D], BF16, kind="ExternalOutput").ap()
    hb_o = nc.dram_tensor("hb", [NTIL, 128, D], BF16, kind="ExternalOutput").ap()
    idx_o = nc.dram_tensor("idx", [BPC, 2], F32, kind="ExternalOutput").ap()

    with ExitStack() as ctx:
        e = ctx.enter_context
        xt = e(nc.sbuf_tensor("xt", [128, D], F32))
        s0t = e(nc.sbuf_tensor("s0t", [128, D], F32))
        g0t = e(nc.sbuf_tensor("g0t", [128, D], F32))
        x1t = e(nc.sbuf_tensor("x1t", [128, D], F32))
        xn1t = e(nc.sbuf_tensor("xn1t", [128, D], F32))
        xn1bt = e(nc.sbuf_tensor("xn1bt", [128, D], BF16))
        hbt = e(nc.sbuf_tensor("hbt", [128, D], BF16))
        sq = e(nc.sbuf_tensor("sq", [128, D], F32))
        stats = e(nc.sbuf_tensor("stats", [128, 16], F32))
        nwrep = e(nc.sbuf_tensor("nwrep", [128, D], F32))
        nwrow = e(nc.sbuf_tensor("nwrow", [1, D], F32))
        onesf = e(nc.sbuf_tensor("onesfb", [128, 128], F32))
        eyesb = e(nc.sbuf_tensor("eyesb", [128, 128], F32))
        eye4r = e(nc.sbuf_tensor("eye4rsb", [1, BPC * BPC], F32))
        wrsb = e(nc.sbuf_tensor("wrsb", [128, 8, 8], F32))
        brsb = e(nc.sbuf_tensor("brsb", [1, 8], F32))
        iota8 = e(nc.sbuf_tensor("iota8sb", [128, 8], F32))
        xrows = e(nc.sbuf_tensor("xrows", [1, BPC, D], F32))
        xrT = e(nc.sbuf_tensor("xrT", [128, 8, BPC], F32))
        scsb = e(nc.sbuf_tensor("scsb", [BPC, 8], F32))
        m1 = e(nc.sbuf_tensor("m1", [BPC, 8], F32))
        m2 = e(nc.sbuf_tensor("m2", [BPC, 8], F32))
        sm = e(nc.sbuf_tensor("sm", [BPC, 8], F32))
        idxsb = e(nc.sbuf_tensor("idxsb", [BPC, 2], F32))
        P0 = e(nc.psum_tensor("P0", [128, 512], F32))
        P1 = e(nc.psum_tensor("P1", [128, 512], F32))

        ch = Chain()
        ch.dma("sync", nwrow[:], nw_i)
        ch.dma("sync", onesf[:], onesf_i)
        ch.dma("sync", eyesb[:], eye128)
        ch.dma("sync", eye4r[:], eye4r_i)
        ch.dma("sync", wrsb[:], wr_i)
        ch.dma("sync", brsb[:], br_i)
        ch.dma("sync", iota8[:], iota8_i)
        for fh in range(2):
            ch.add("tensor", lambda e, fh=fh: e.matmul(
                out=P0[:, 0:512], lhsT=onesf[0:1, 0:128],
                rhs=nwrow[0:1, fh * 512:(fh + 1) * 512], start=True, stop=True))
            ch.add("scalar", lambda e, fh=fh: e.copy(
                out=nwrep[:, fh * 512:(fh + 1) * 512], in_=P0[:, 0:512]))

        NTIL = BPC * T // 128
        for i in range(NTIL):
            ch.dma("sync", xt[:], x_i[i])
            ch.dma("sync", s0t[:], s0_i[i])
            ch.dma("sync", g0t[:], g0_i[i])
            ch.add("scalar", lambda e: e.activation(out=g0t[:], in_=g0t[:], func=AF.Silu))
            ch.add("vector", lambda e: e.tensor_tensor(out=s0t[:], in0=s0t[:], in1=g0t[:], op=ALU.mult))
            ch.add("vector", lambda e: e.tensor_tensor(out=x1t[:], in0=xt[:], in1=s0t[:], op=ALU.add))
            ch.dma("sync", x1_o[i], x1t[:])
            _ln_tile(ch, None, None, x1t[:], stats, xn1t[:], extra_outs=(xn1bt[:],))
            ch.dma("sync", xn1_o[i], xn1t[:])
            ch.dma("sync", xn1b_o[i], xn1bt[:])
            ch.add("scalar", lambda e: e.activation(out=sq[:], in_=xn1t[:], func=AF.Square))
            ch.add("vector", lambda e: e.tensor_reduce(out=stats[:, 0:1], in_=sq[:], axis=AX.X, op=ALU.add))
            ch.add("vector", lambda e: e.tensor_scalar(
                out=stats[:, 0:1], in0=stats[:, 0:1], scalar1=1.0 / D, scalar2=EPS,
                op0=ALU.mult, op1=ALU.add))
            ch.add("scalar", lambda e: e.activation(out=stats[:, 1:2], in_=stats[:, 0:1], func=AF.Sqrt))
            ch.add("vector", lambda e: e.reciprocal(out=stats[:, 2:3], in_=stats[:, 1:2]))
            ch.add("vector", lambda e: e.tensor_scalar(
                out=sq[:], in0=xn1t[:], scalar1=stats[:, 2:3], scalar2=None, op0=ALU.mult))
            ch.add("vector", lambda e: e.tensor_tensor(out=hbt[:], in0=sq[:], in1=nwrep[:], op=ALU.mult))
            ch.dma("sync", hb_o[i], hbt[:])
            if i % 16 == 0:
                bidx = i // 16
                ch.add("vector", lambda e, bidx=bidx: e.tensor_copy(
                    out=xrows[0:1, bidx, :], in_=xn1t[0:1, :]))
        for dc in range(8):
            def trx(e, dc=dc):
                for bb in range(BPC):
                    r = e.matmul(out=P0[0:128, 0:BPC],
                                 lhsT=xrows[0:1, bb, dc * 128:(dc + 1) * 128],
                                 rhs=eye4r[0:1, bb * BPC:(bb + 1) * BPC],
                                 start=(bb == 0), stop=(bb == BPC - 1))
                return r
            ch.add("tensor", trx)
            ch.add("scalar", lambda e, dc=dc: e.copy(out=xrT[:, dc, :], in_=P0[0:128, 0:BPC]))
        def smm(e):
            for dc in range(8):
                e.matmul(out=P1[0:BPC, 0:8], lhsT=xrT[:, dc, :], rhs=wrsb[:, dc, :],
                         start=(dc == 0), stop=False)
            return e.matmul(out=P1[0:BPC, 0:8], lhsT=onesf[0:1, 0:BPC], rhs=brsb[:],
                            start=False, stop=True)
        ch.add("tensor", smm)
        ch.add("scalar", lambda e: e.copy(out=scsb[:], in_=P1[0:BPC, 0:8]))
        ch.add("vector", lambda e: e.reduce_max(out=sm[:, 0:1], in_=scsb[:], axis=AX.X))
        ch.add("vector", lambda e: e.tensor_scalar(
            out=m1[:], in0=scsb[:], scalar1=sm[:, 0:1], scalar2=None, op0=ALU.is_ge))
        ch.add("vector", lambda e: e.tensor_scalar(
            out=m2[:], in0=m1[:], scalar1=1e30, scalar2=None, op0=ALU.mult))
        ch.add("vector", lambda e: e.tensor_tensor(out=m2[:], in0=scsb[:], in1=m2[:], op=ALU.subtract))
        ch.add("vector", lambda e: e.reduce_max(out=sm[:, 1:2], in_=m2[:], axis=AX.X))
        ch.add("vector", lambda e: e.tensor_scalar(
            out=m2[:], in0=m2[:], scalar1=sm[:, 1:2], scalar2=None, op0=ALU.is_ge))
        ch.add("vector", lambda e: e.tensor_tensor(out=m1[:], in0=m1[:], in1=iota8[0:BPC, :], op=ALU.mult))
        ch.add("vector", lambda e: e.tensor_tensor(out=m2[:], in0=m2[:], in1=iota8[0:BPC, :], op=ALU.mult))
        ch.add("vector", lambda e: e.tensor_reduce(out=idxsb[:, 0:1], in_=m1[:], axis=AX.X, op=ALU.add))
        ch.add("vector", lambda e: e.tensor_reduce(out=idxsb[:, 1:2], in_=m2[:], axis=AX.X, op=ALU.add))
        ch.dma("sync", idx_o[:, :], idxsb[:])
        ch.build(nc)
    return nc


def stage_combine0(x_full, s0, scat_region, norm_w, Wr, br):
    nc = _NC_CACHE.setdefault("combine0", build_combine0())
    NTIL = BPC * T // 128
    g0pre = np.zeros((B * T, D), np.float32)
    g0pre[:8192] = scat_region
    g0pre = g0pre.reshape(NCORES, NTIL, 128, D)
    xs = x_full.reshape(NCORES, NTIL, 128, D)
    s0s = s0.reshape(NCORES, NTIL, 128, D)
    wr_r = np.ascontiguousarray(Wr.reshape(8, 128, 8).transpose(1, 0, 2))
    in_maps = []
    for c in range(NCORES):
        in_maps.append({
            "x": np.ascontiguousarray(xs[c]), "s0": np.ascontiguousarray(s0s[c]),
            "g0pre": np.ascontiguousarray(g0pre[c]),
            "nw": norm_w.reshape(1, D), "wr": wr_r, "br": br.reshape(1, 8),
            "eye128": np.eye(128, dtype=np.float32),
            "eye4r": np.eye(BPC, dtype=np.float32).reshape(1, BPC * BPC),
            "onesf": np.ones((128, 128), np.float32),
            "iota8": np.broadcast_to(np.arange(8, dtype=np.float32), (128, 8)).copy(),
        })
    res = _run(nc, in_maps)
    x1 = np.stack([r["x1"] for r in res.results]).reshape(B, T, D)
    xn1 = np.stack([r["xn1"] for r in res.results]).reshape(B, T, D)
    xn1b = np.stack([r["xn1b"] for r in res.results]).reshape(B, T, D)
    hb = np.stack([r["hb"] for r in res.results]).reshape(B, T, D)
    idx = np.stack([r["idx"] for r in res.results]).reshape(B, 2).astype(np.int64)
    return x1, xn1, xn1b, hb, idx



def build_gates():
    nc = bass.Bass("TRN2", target_bir_lowering=False)
    hT_i = nc.dram_tensor("hT", [128, 128, 8, 512], BF16, kind="ExternalInput").ap()
    wc_i = nc.dram_tensor("wc", [128, 8, 2304], F32, kind="ExternalInput").ap()
    gn_o = nc.dram_tensor("gn", [512, 128, 2176], BF16, kind="ExternalOutput").ap()
    v_o = nc.dram_tensor("v", [512, 128, 128], F32, kind="ExternalOutput").ap()

    with ExitStack() as ctx:
        e = ctx.enter_context
        wcsb = e(nc.sbuf_tensor("wcsb", [128, 8, 2304], BF16))
        strip = e(nc.sbuf_tensor("strip", [128, 8, 512], BF16))
        egl = e(nc.sbuf_tensor("egl", [128, 2176], F32))
        vsb = e(nc.sbuf_tensor("vsb", [128, 128], F32))
        rz = e(nc.sbuf_tensor("rz", [128, 128], F32))
        gnt = e(nc.sbuf_tensor("gnt", [128, 2176], BF16))
        PT = [e(nc.psum_tensor(f"PT{i}", [128, 512], F32)) for i in range(4)]
        PT4 = e(nc.psum_tensor("PT4", [128, 256], F32))

        ch = Chain()
        ch.dma("gpsimd", wcsb[:], wc_i)
        for tt in range(512):
            if tt % 4 == 0:
                ch.dma("gpsimd", strip[:], hT_i[tt // 4])
            ts = (tt % 4) * 128
            for p5 in range(5):
                off = p5 * 512
                sz = 256 if p5 == 4 else 512
                ps = PT4[:, 0:256] if p5 == 4 else PT[p5][:, 0:512]
                def gmm(e, ts=ts, off=off, sz=sz, ps=ps):
                    for dc in range(8):
                        r = e.matmul(out=ps, lhsT=strip[:, dc, ts:ts + 128],
                                     rhs=wcsb[:, dc, off:off + sz],
                                     start=(dc == 0), stop=(dc == 7))
                    return r
                ch.add("tensor", gmm)
            ch.add("scalar", lambda e: e.copy(out=vsb[:], in_=PT[0][:, 0:128]))
            ch.add("scalar", lambda e: e.activation(out=egl[:, 0:384], in_=PT[0][:, 128:512], func=AF.Exp))
            for p5 in range(1, 4):
                ch.add("scalar", lambda e, p5=p5: e.activation(
                    out=egl[:, p5 * 512 - 128:p5 * 512 + 384], in_=PT[p5][:, 0:512], func=AF.Exp))
            ch.add("scalar", lambda e: e.activation(out=egl[:, 1920:2176], in_=PT4[:, 0:256], func=AF.Exp))
            ch.add("vector", lambda e: e.tensor_reduce(
                out=rz[:], in_=egl[:].rearrange("p (m g) -> p m g", g=17), axis=AX.X, op=ALU.add))
            ch.add("vector", lambda e: e.reciprocal(out=rz[:], in_=rz[:]))
            ch.add("vector", lambda e: e.tensor_tensor(
                out=gnt[:].rearrange("p (m g) -> p m g", g=17),
                in0=egl[:].rearrange("p (m g) -> p m g", g=17),
                in1=rz[:].unsqueeze(2).broadcast_to((128, 128, 17)), op=ALU.mult))
            ch.dma("sync", gn_o[tt], gnt[:])
            ch.dma("sync", v_o[tt], vsb[:])
        ch.build(nc)
    return nc


def stage_gates(hb, W_v, W_a):
    nc = _NC_CACHE.setdefault("gates", build_gates())
    hT = np.ascontiguousarray(np.asarray(hb).reshape(B * T, D).T)
    hTr = np.ascontiguousarray(
        hT.reshape(8, 128, 128, 512).transpose(2, 1, 0, 3))
    Wa_r = W_a.reshape(H1, M1, M1 + 1, D)
    in_maps = []
    for c in range(NCORES):
        heads = range(8 * c, 8 * c + 8)
        wc = np.empty((D, 8, 16, 18), np.float32)
        for hh, hg in enumerate(heads):
            wc[:, hh, :, 17] = 0.0
        wcv = np.stack([W_v[hg * 16:(hg + 1) * 16].T for hg in heads], 1)
        wc_full = np.empty((D, 2304), np.float32)
        for hh, hg in enumerate(heads):
            wc_full[:, hh * 16:(hh + 1) * 16] = W_v[hg * 16:(hg + 1) * 16].T
        for hh, hg in enumerate(heads):
            blk = Wa_r[hg]
            wc_full[:, 128 + hh * 272:128 + (hh + 1) * 272] = \
                blk.transpose(2, 0, 1).reshape(D, 272)
        wcr = np.ascontiguousarray(wc_full.reshape(8, 128, 2304).transpose(1, 0, 2))
        in_maps.append({"hT": hTr, "wc": wcr})
    res = _run(nc, in_maps)
    gn = np.stack([r["gn"] for r in res.results])
    vv = np.stack([r["v"] for r in res.results])
    return gn, vv



def build_scan():
    nc = bass.Bass("TRN2", target_bir_lowering=False)
    CH = 64
    NCH = T // CH
    g_i = nc.dram_tensor("g", [NCH, 128, CH, 2, 16, 18], BF16, kind="ExternalInput").ap()
    v_i = nc.dram_tensor("v", [NCH, 128, CH, 2, 16], F32, kind="ExternalInput").ap()
    ys_o = nc.dram_tensor("ys", [NCH, 128, CH, 2, 16], F32, kind="ExternalOutput").ap()

    with ExitStack() as ctx:
        e = ctx.enter_context
        gb = e(nc.sbuf_tensor("gb", [128, CH, 2, 16, 18], BF16))
        vbuf = e(nc.sbuf_tensor("vbuf", [128, CH, 2, 16], F32))
        ub = e(nc.sbuf_tensor("ub", [128, CH, 2, 16], F32))
        ysb = [e(nc.sbuf_tensor(f"ysb{i}", [128, CH, 2, 16], F32)) for i in range(2)]
        st0 = e(nc.sbuf_tensor("st0", [128, 2, 16], F32))
        tmpb = e(nc.sbuf_tensor("tmpb", [128, 2, 16, 16], F32))
        red = e(nc.sbuf_tensor("red", [128, 2, 16], F32))

        ch = Chain()
        ch.add("vector", lambda e: e.memset(st0[:], 0.0))
        for ci in range(NCH):
            ys = ysb[ci % 2]
            ysp = ysb[(ci + 1) % 2]
            ch.dma("sync", gb[:], g_i[ci])
            ch.dma("sync", vbuf[:], v_i[ci])
            ch.add("vector", lambda e: e.tensor_tensor(
                out=ub[:], in0=vbuf[:], in1=gb[:, :, :, :, 16], op=ALU.mult))
            for t in range(CH):
                if ci == 0 and t == 0:
                    state = st0[:]
                elif t == 0:
                    state = ysp[:, CH - 1, :, :]
                else:
                    state = ys[:, t - 1, :, :]
                sb = state.unsqueeze(2).broadcast_to((128, 2, 16, 16))
                A = gb[:, t, :, :, 0:16]
                ch.add("vector", lambda e, A=A, sb=sb: e.tensor_tensor(
                    out=tmpb[:], in0=A, in1=sb, op=ALU.mult))
                ch.add("vector", lambda e: e.tensor_reduce(
                    out=red[:], in_=tmpb[:], axis=AX.X, op=ALU.add))
                ch.add("vector", lambda e, ys=ys, t=t: e.tensor_tensor(
                    out=ys[:, t, :, :], in0=red[:], in1=ub[:, t, :, :], op=ALU.add))
            ch.dma("sync", ys_o[ci], ys[:])
        ch.build(nc)
    return nc


def stage_scan(gn, vv):
    nc = _NC_CACHE.setdefault("scan", build_scan())
    A_all = np.asarray(gn).reshape(8, B * T, 8, 16, 17)
    v_all = np.asarray(vv).reshape(8, B * T, 8, 16)
    CH = 64
    NCH = T // CH
    in_maps = []
    seqs = np.arange(B * H1)
    bs, hgs = seqs // H1, seqs % H1
    gcs, hhs = hgs // 8, hgs % 8
    g_seq = np.zeros((B * H1, T, 16, 18), ml_dtypes.bfloat16)
    taus = bs[:, None] * T + np.arange(T)[None, :]
    for s in range(B * H1):
        blk = A_all[gcs[s], taus[s], hhs[s]]
        g_seq[s, :, :, 0:16] = blk[:, :, 1:17]
        g_seq[s, :, :, 16] = blk[:, :, 0]
    v_seq = np.empty((B * H1, T, 16), np.float32)
    for s in range(B * H1):
        v_seq[s] = v_all[gcs[s], taus[s], hhs[s]]
    for c in range(NCORES):
        sl = slice(256 * c, 256 * (c + 1))
        gss = g_seq[sl].reshape(128, 2, T, 16, 18).transpose(0, 2, 1, 3, 4)
        vss = v_seq[sl].reshape(128, 2, T, 16).transpose(0, 2, 1, 3)
        gss = np.ascontiguousarray(gss.reshape(128, NCH, CH, 2, 16, 18).transpose(1, 0, 2, 3, 4, 5))
        vss = np.ascontiguousarray(vss.reshape(128, NCH, CH, 2, 16).transpose(1, 0, 2, 3, 4))
        in_maps.append({"g": gss, "v": vss})
    res = _run(nc, in_maps)
    ys = np.stack([r["ys"] for r in res.results])
    ys = ys.transpose(0, 2, 4, 1, 3, 5).reshape(B * H1, T, 16)
    h_out = ys.reshape(B, H1, T, 16).transpose(0, 2, 1, 3).reshape(B, T, D)
    return np.ascontiguousarray(h_out)



def build_final():
    nc = bass.Bass("TRN2", target_bir_lowering=False)
    NTIL = T // 128
    x1_i = nc.dram_tensor("x1", [BPC, NTIL, 128, D], F32, kind="ExternalInput").ap()
    xn1_i = nc.dram_tensor("xn1", [BPC, NTIL, 128, D], F32, kind="ExternalInput").ap()
    xn1T_i = nc.dram_tensor("xn1T", [BPC, 128, 8, T], BF16, kind="ExternalInput").ap()
    ysT_i = nc.dram_tensor("ysT", [BPC, 128, 8, T], F32, kind="ExternalInput").ap()
    woutT_i = nc.dram_tensor("woutT", [128, 8, D], F32, kind="ExternalInput").ap()
    w1g_i = nc.dram_tensor("w1g", [BPC, 2, 8, 128, 8, 128], F32, kind="ExternalInput").ap()
    b1g_i = nc.dram_tensor("b1g", [128, BPC, 2, 8], F32, kind="ExternalInput").ap()
    w2g_i = nc.dram_tensor("w2g", [BPC, 128, 2, 8, 1024], F32, kind="ExternalInput").ap()
    b2g_i = nc.dram_tensor("b2g", [1, BPC, 2, 1024], F32, kind="ExternalInput").ap()
    onesf_i = nc.dram_tensor("onesf", [128, 128], F32, kind="ExternalInput").ap()
    out_o = nc.dram_tensor("out", [BPC, NTIL, 128, D], F32, kind="ExternalOutput").ap()

    with ExitStack() as ctx:
        e = ctx.enter_context
        xn1T = e(nc.sbuf_tensor("xn1Tb", [128, 8, T], BF16))
        ysT = e(nc.sbuf_tensor("ysTb", [128, 8, T], BF16))
        woutsb = e(nc.sbuf_tensor("woutsb", [128, 8, D], BF16))
        h1T = e(nc.sbuf_tensor("h1T", [128, 2, 8, T], BF16))
        w2sb = e(nc.sbuf_tensor("w2sb", [128, 2, 8, D], BF16))
        w1strip = e(nc.sbuf_tensor("w1strip", [128, 8, 128], BF16))
        b1sb = e(nc.sbuf_tensor("b1sb", [128, BPC, 2, 8], F32))
        b2row = e(nc.sbuf_tensor("b2row", [1, 2, 1024], F32))
        b2sum = e(nc.sbuf_tensor("b2sum", [1, 1024], F32))
        b2rep = e(nc.sbuf_tensor("b2rep", [128, 1024], F32))
        onesf = e(nc.sbuf_tensor("onesfb", [128, 128], F32))
        x1t = e(nc.sbuf_tensor("x1t", [128, D], F32))
        sg = e(nc.sbuf_tensor("sg", [128, D], F32))
        g1t = e(nc.sbuf_tensor("g1t", [128, D], F32))
        PS = [e(nc.psum_tensor(f"PS{i}", [128, 512], F32)) for i in range(2)]
        PO = [e(nc.psum_tensor(f"PO{i}", [128, 512], F32)) for i in range(2)]

        ch = Chain()
        ch.dma("gpsimd", woutsb[:], woutT_i)
        ch.dma("sync", b1sb[:], b1g_i)
        ch.dma("sync", onesf[:], onesf_i)
        for b in range(BPC):
            ch.dma("sync", b2row[:], b2g_i[0:1, b])
            ch.dma("gpsimd", xn1T[:], xn1T_i[b])
            ch.dma("gpsimd", ysT[:], ysT_i[b])
            ch.dma("gpsimd", w2sb[:], w2g_i[b])
            ch.add("vector", lambda e: e.tensor_tensor(
                out=b2sum[:], in0=b2row[0:1, 0, :], in1=b2row[0:1, 1, :], op=ALU.add))
            for fh in range(2):
                ch.add("tensor", lambda e, fh=fh: e.matmul(
                    out=PS[0][:, 0:512], lhsT=onesf[0:1, 0:128],
                    rhs=b2sum[0:1, fh * 512:(fh + 1) * 512], start=True, stop=True))
                ch.add("scalar", lambda e, fh=fh: e.copy(
                    out=b2rep[:, fh * 512:(fh + 1) * 512], in_=PS[0][:, 0:512]))
            for k in range(2):
                for ffc in range(8):
                    ch.dma("gpsimd", w1strip[:], w1g_i[b, k, ffc])
                    for tc in range(4):
                        def hmm_(e, tc=tc):
                            for dc in range(8):
                                r = e.matmul(out=PS[0][:, 0:512], lhsT=w1strip[:, dc, :],
                                             rhs=xn1T[:, dc, tc * 512:(tc + 1) * 512],
                                             start=(dc == 0), stop=(dc == 7))
                            return r
                        ch.add("tensor", hmm_)
                        ch.add("scalar", lambda e, k=k, ffc=ffc, tc=tc, b=b: e.activation(
                            out=h1T[:, k, ffc, tc * 512:(tc + 1) * 512], in_=PS[0][:, 0:512],
                            func=AF.Gelu, bias=b1sb[:, b, k, ffc:ffc + 1]))
            for tt in range(16):
                def smm(e, tt=tt):
                    for fh in range(2):
                        for dc in range(8):
                            r = e.matmul(out=PS[fh][:, 0:512],
                                         lhsT=ysT[:, dc, tt * 128:(tt + 1) * 128],
                                         rhs=woutsb[:, dc, fh * 512:(fh + 1) * 512],
                                         start=(dc == 0), stop=(dc == 7))
                    return r
                ch.add("tensor", smm)
                ch.dma("sync", g1t[:], xn1_i[b, tt])
                ch.dma("sync", x1t[:], x1_i[b, tt])
                for fh in range(2):
                    ch.add("vector", lambda e, fh=fh: e.tensor_tensor(
                        out=sg[:, fh * 512:(fh + 1) * 512], in0=PS[fh][:, 0:512],
                        in1=g1t[:, fh * 512:(fh + 1) * 512], op=ALU.add))
                def omm(e, tt=tt):
                    for fh in range(2):
                        n = 0
                        for k in range(2):
                            for ffc in range(8):
                                r = e.matmul(out=PO[fh][:, 0:512],
                                             lhsT=h1T[:, k, ffc, tt * 128:(tt + 1) * 128],
                                             rhs=w2sb[:, k, ffc, fh * 512:(fh + 1) * 512],
                                             start=(n == 0), stop=(n == 15))
                                n += 1
                    return r
                ch.add("tensor", omm)
                for fh in range(2):
                    ch.add("vector", lambda e, fh=fh: e.tensor_tensor(
                        out=g1t[:, fh * 512:(fh + 1) * 512], in0=PO[fh][:, 0:512],
                        in1=b2rep[:, fh * 512:(fh + 1) * 512], op=ALU.add))
                ch.add("scalar", lambda e: e.activation(out=g1t[:], in_=g1t[:], func=AF.Silu, scale=0.5))
                ch.add("vector", lambda e: e.tensor_tensor(out=sg[:], in0=sg[:], in1=g1t[:], op=ALU.mult))
                ch.add("vector", lambda e: e.tensor_tensor(out=x1t[:], in0=x1t[:], in1=sg[:], op=ALU.add))
                ch.dma("sync", out_o[b, tt], x1t[:])
        ch.build(nc)
    return nc


def stage_final(x1, xn1, xn1b, h_out, idx, W_out, W1f, b1f, W2f, b2f):
    nc = _NC_CACHE.setdefault("final", build_final())
    NTIL = T // 128
    woutT = np.ascontiguousarray(W_out.T.reshape(8, 128, D).transpose(1, 0, 2))
    in_maps = []
    for c in range(NCORES):
        bsl = slice(c * BPC, (c + 1) * BPC)
        xn1Tc = np.asarray(xn1b[bsl]).transpose(0, 2, 1)
        ysTc = np.ascontiguousarray(h_out[bsl].transpose(0, 2, 1))
        idxc = idx[bsl]
        w1g = W1f[idxc]
        b1g = b1f[idxc]
        w2g = W2f[idxc]
        b2g = b2f[idxc]
        in_maps.append({
            "x1": np.ascontiguousarray(x1[bsl].reshape(BPC, NTIL, 128, D)),
            "xn1": np.ascontiguousarray(xn1[bsl].reshape(BPC, NTIL, 128, D)),
            "xn1T": np.ascontiguousarray(xn1Tc.reshape(BPC, 8, 128, T).transpose(0, 2, 1, 3)),
            "ysT": np.ascontiguousarray(ysTc.reshape(BPC, 8, 128, T).transpose(0, 2, 1, 3)),
            "woutT": woutT,
            "w1g": np.ascontiguousarray(
                w1g.reshape(BPC, 2, 8, 128, 8, 128).transpose(0, 1, 4, 3, 2, 5)),
            "b1g": np.ascontiguousarray(
                b1g.reshape(BPC, 2, 8, 128).transpose(3, 0, 1, 2)),
            "w2g": np.ascontiguousarray(
                w2g.reshape(BPC, 2, 8, 128, 1024).transpose(0, 3, 1, 2, 4)),
            "b2g": b2g.reshape(1, BPC, 2, 1024),
            "onesf": np.ones((128, 128), np.float32),
        })
    res = _run(nc, in_maps)
    out = np.stack([r["out"] for r in res.results])
    return out.reshape(B, T, D)



def kernel_staged(**inputs):
    x = np.asarray(inputs["L0_attn_x"], np.float32)
    xn0 = stage_ln0(x)
    s0 = stage_attn0(xn0, inputs["L0_attn_W_attn"], inputs["L0_attn_W_proj"])
    weighted = stage_ffn0(xn0, inputs["L0_ffn_top_idx"], inputs["L0_ffn_combine_weights"],
                          inputs["L0_ffn_W1"], inputs["L0_ffn_b1"],
                          inputs["L0_ffn_W2"], inputs["L0_ffn_b2"])
    scat = stage_scatter(weighted, inputs["L0_ffn_top_idx"])
    x1, xn1, xn1b, hb, idx = stage_combine0(x, s0, scat, inputs["L1_attn_norm_w"],
                                            inputs["L1_ffn_Wr"], inputs["L1_ffn_br"])
    gn, vv = stage_gates(hb, inputs["L1_attn_W_v"], inputs["L1_attn_W_a"])
    h_out = stage_scan(gn, vv)
    out = stage_final(x1, xn1, xn1b, h_out, idx, inputs["L1_attn_W_out"],
                      inputs["L1_ffn_W1"], inputs["L1_ffn_b1"],
                      inputs["L1_ffn_W2"], inputs["L1_ffn_b2"])
    return out



class ChainSeg(Chain):
    SEG = 50000

    def __init__(self):
        self.groups = []

    def add(self, engine, fn, inc=1):
        self.groups.append([(engine, fn, inc)])

    def add_parallel(self, ops):
        self.groups.append(list(ops))

    def dma(self, engine, out, in_, **kw):
        self.add(engine, lambda e, o=out, i=in_, kw=kw: e.dma_start(out=o, in_=i, **kw), 16)

    def build(self, nc):
        segs = []
        cur = []
        c = 0
        for g in self.groups:
            ginc = sum(o[2] for o in g)
            if c + ginc > self.SEG and cur:
                segs.append(cur)
                cur = []
                c = 0
            cur.append(g)
            c += ginc
        if cur:
            segs.append(cur)
        with ExitStack() as st:
            sems = [st.enter_context(nc.semaphore(f"chseg{i}")) for i in range(len(segs))]
            block = st.enter_context(nc.Block())
            seg_totals = []
            plans = {}
            for si, seg in enumerate(segs):
                c = 0
                for g in seg:
                    for engname, fn, inc in g:
                        plans.setdefault(engname, []).append((si, c, fn, inc))
                    c += sum(o[2] for o in g)
                seg_totals.append(c)
            for engname, plan in plans.items():
                def body(eng, plan=plan):
                    cur_seg = 0
                    for si, pre, fn, inc in plan:
                        while cur_seg < si:
                            eng.wait_ge(sems[cur_seg], seg_totals[cur_seg])
                            cur_seg += 1
                        if pre > 0:
                            eng.wait_ge(sems[si], pre)
                        last = fn(eng)
                        last.then_inc(sems[si], inc)
                getattr(block, engname)(body)

            def tail(eng):
                for si in range(len(segs)):
                    eng.wait_ge(sems[si], seg_totals[si])
            block.sync(tail)


class ChainP:
    NSEM = 6

    def __init__(self):
        self.ops = []
        self.counts = [0] * self.NSEM
        self.cur = 0
        self.tile_no = -1
        self.tile_ends = {}
        self.pending_waits = []

    def begin_tile(self):
        if self.tile_no >= 0:
            self.tile_ends[self.tile_no] = (self.cur, self.counts[self.cur])
        self.tile_no += 1
        self.cur = self.tile_no % self.NSEM
        return self.tile_no

    def wait_tile_end(self, k):
        if k >= 0 and k in self.tile_ends:
            self.pending_waits.append(self.tile_ends[k])

    def mark(self):
        return (self.cur, self.counts[self.cur])

    def wait_mark(self, m):
        if m is not None:
            self.pending_waits.append(m)

    def add(self, engine, fn, inc=1, raw=False):
        pre = self.counts[self.cur]
        ew = self.pending_waits
        self.pending_waits = []
        self.ops.append((engine, fn, inc, self.cur, pre, ew, raw))
        self.counts[self.cur] += inc

    def dma(self, engine, out, in_, **kw):
        self.add(engine, lambda e, o=out, i=in_, kw=kw: e.dma_start(out=o, in_=i, **kw), 16)

    def finish(self):
        if self.tile_no >= 0:
            self.tile_ends[self.tile_no] = (self.cur, self.counts[self.cur])

    def emit(self, ch, nc, name):
        self.finish()
        sems = [nc.semaphore(f"{name}{i}").__enter__() for i in range(self.NSEM)]
        totals = list(self.counts)
        by_engine = {}
        for op in self.ops:
            by_engine.setdefault(op[0], []).append(op)

        def make_region(engine_ops, sems=sems):
            def region(eng):
                last = None
                for _, fn, inc, sidx, pre, ew, raw in engine_ops:
                    for (wsem, wcount) in ew:
                        if wcount > 0:
                            eng.wait_ge(sems[wsem], wcount)
                    if pre > 0:
                        eng.wait_ge(sems[sidx], pre)
                    if raw:
                        last = fn(eng, sems[sidx])
                    else:
                        last = fn(eng)
                        last.then_inc(sems[sidx], inc)
                return last
            return region

        engines = list(by_engine)
        group = []
        for engname in engines:
            def fn(eng, engname=engname):
                make_region(by_engine[engname])(eng)
                for i in range(self.NSEM):
                    if totals[i] > 0:
                        eng.wait_ge(sems[i], totals[i])
                return eng.nop()
            group.append((engname, fn, 1))
        ch.add_parallel(group)


def _collective_issue(ch, kind, op, ins, outs, sem_name, nc):
    cc_sem = nc.semaphore(sem_name).__enter__()

    def fn(e):
        from concourse.aot_env import allow_ici_all_to_all
        from contextlib import nullcontext
        cm = allow_ici_all_to_all() if kind == "AllToAll" else nullcontext()
        with cm:
            e.collective_compute(kind, op, replica_groups=[list(range(NCORES))],
                                 ins=ins, outs=outs).then_inc(cc_sem)
        return e.nop()
    ch.add("gpsimd", fn)
    return cc_sem


def _collective_wait(ch, cc_sem):
    def fn(e):
        e.wait_ge(cc_sem, 1)
        return e.nop()
    ch.add("gpsimd", fn)


def _collective(ch, kind, op, ins, outs, sem_name, nc):
    cc_sem = nc.semaphore(sem_name).__enter__()

    def fn(e):
        from concourse.aot_env import allow_ici_all_to_all
        from contextlib import nullcontext
        cm = allow_ici_all_to_all() if kind == "AllToAll" else nullcontext()
        with cm:
            e.collective_compute(kind, op, replica_groups=[list(range(NCORES))],
                                 ins=ins, outs=outs).then_inc(cc_sem)
        e.wait_ge(cc_sem, 1)
        return e.nop()
    ch.add("gpsimd", fn)


def build_fused():
    nc = bass.Bass("TRN2", target_bir_lowering=False)
    NTIL = BPC * T // 128
    x_i = nc.dram_tensor("x", [NTIL, 128, D], F32, kind="ExternalInput").ap()
    gxr_i = nc.dram_tensor("gxr", [16, 128, D], F32, kind="ExternalInput").ap()
    wqk_i = nc.dram_tensor("wqk", [16, 128, 8, 128], F32, kind="ExternalInput").ap()
    wv_i = nc.dram_tensor("wv", [128, 8, D], F32, kind="ExternalInput").ap()
    wp_i = nc.dram_tensor("wp", [128, 8, D], F32, kind="ExternalInput").ap()
    w1_i = nc.dram_tensor("w1", [128, 32, 8, 128], F32, kind="ExternalInput").ap()
    w2_i = nc.dram_tensor("w2", [8, 128, 32, 128], F32, kind="ExternalInput").ap()
    b1_i = nc.dram_tensor("b1", [128, 32], F32, kind="ExternalInput").ap()
    b2_i = nc.dram_tensor("b2", [128, 8], F32, kind="ExternalInput").ap()
    wvec_i = nc.dram_tensor("wvec", [1, 2048], F32, kind="ExternalInput").ap()
    idxi_i = nc.dram_tensor("idxi", [128, 8, 16], I32, kind="ExternalInput").ap()
    nw_i = nc.dram_tensor("nw", [1, D], F32, kind="ExternalInput").ap()
    wr_i = nc.dram_tensor("wr", [128, 8, 8], F32, kind="ExternalInput").ap()
    br_i = nc.dram_tensor("br", [1, 8], F32, kind="ExternalInput").ap()
    wc_i = nc.dram_tensor("wc", [128, 8, 2304], F32, kind="ExternalInput").ap()
    woutT_i = nc.dram_tensor("woutT", [128, 8, D], F32, kind="ExternalInput").ap()
    w1f_i = nc.dram_tensor("w1f", [8192, D], F32, kind="ExternalInput").ap()
    w2f_i = nc.dram_tensor("w2f", [8192, D], F32, kind="ExternalInput").ap()
    b1f_i = nc.dram_tensor("b1f", [8, D], F32, kind="ExternalInput").ap()
    b2f_i = nc.dram_tensor("b2f", [8, D], F32, kind="ExternalInput").ap()
    eyes_i = nc.dram_tensor("eyes", [128, 3, 64], F32, kind="ExternalInput").ap()
    eye64_i = nc.dram_tensor("eye64", [128, 64], F32, kind="ExternalInput").ap()
    onesf_i = nc.dram_tensor("onesf", [128, 128], F32, kind="ExternalInput").ap()
    onesbf_i = nc.dram_tensor("onesbf", [128, 512], BF16, kind="ExternalInput").ap()
    iota1024_i = nc.dram_tensor("iota1024", [128, 1024], F32, kind="ExternalInput").ap()
    iotacol_i = nc.dram_tensor("iotacol", [128, 1], F32, kind="ExternalInput").ap()
    rbase_i = nc.dram_tensor("rbase", [128, 1], F32, kind="ExternalInput").ap()
    iscore_i = nc.dram_tensor("iscore", [128, 1], F32, kind="ExternalInput").ap()
    out_o = nc.dram_tensor("out", [BPC, 16, 128, D], F32, kind="ExternalOutput").ap()

    d_xn0T = nc.dram_tensor("d_xn0T", [BPC, 128, 8, T], BF16)
    d_gxnT = nc.dram_tensor("d_gxnT", [128, 8, 2048], BF16)
    d_s0 = nc.dram_tensor("d_s0", [BPC, 16, 128, D], F32)
    agw_in = nc.dram_tensor("agw_in", [2048, D], BF16)
    agw_out = nc.dram_tensor("agw_out", [8, 2048, D], BF16)
    agsc_in = nc.dram_tensor("agsc_in", [8, 128, D], F32)
    agsc_out = nc.dram_tensor("agsc_out", [64, 128, D], F32)
    d_x1 = nc.dram_tensor("d_x1", [BPC, 16, 128, D], F32)
    d_xn1 = nc.dram_tensor("d_xn1", [BPC, 16, 128, D], F32)
    d_xn1T = nc.dram_tensor("d_xn1T", [BPC, 128, 8, T], BF16)
    agh_inB = [nc.dram_tensor(f"agh_in{b}", [D, T], BF16) for b in range(BPC)]
    agh_outB = [nc.dram_tensor(f"agh_out{b}", [8, D, T], BF16) for b in range(BPC)]
    d_idx = nc.dram_tensor("d_idx", [BPC, 2], F32)
    d_w1g = nc.dram_tensor("d_w1g", [BPC, 2, 8, 128, D], BF16)
    d_w2g = nc.dram_tensor("d_w2g", [BPC, 2, 8, 128, D], BF16)
    d_b1g = nc.dram_tensor("d_b1g", [BPC, 2, D], BF16)
    d_b2s = nc.dram_tensor("d_b2s", [BPC, D], F32)
    d_gseqH = [nc.dram_tensor(f"d_gseq{i}", [256, T // 2, 19, 16], BF16) for i in range(2)]
    d_ys = nc.dram_tensor("d_ys", [256, 16, T], BF16)
    a2y_in = nc.dram_tensor("a2y_in", [8, 8, 4, 16, T], BF16)
    a2y_out = nc.dram_tensor("a2y_out", [8, 8, 4, 16, T], BF16)

    ch = ChainSeg()
    C_GELU = 0.7978845608028654

    def mmop(ps, lhsT, rhs, start=True, stop=True):
        return lambda e: e.matmul(out=ps, lhsT=lhsT, rhs=rhs, start=start, stop=stop)

    with ExitStack() as ctx:
        e_ = ctx.enter_context
        onesf = e_(nc.sbuf_tensor("onesfb", [128, 128], F32))
        onesbf = e_(nc.sbuf_tensor("onesbfb", [128, 512], BF16))
        eyesb = e_(nc.sbuf_tensor("eyesb", [128, 3, 64], F32))
        eye64b = e_(nc.sbuf_tensor("eye64b", [128, 64], F32))
        iotacol = e_(nc.sbuf_tensor("iotacolb", [128, 1], F32))
        ch.dma("sync", onesf[:], onesf_i)
        ch.dma("sync", onesbf[:], onesbf_i)
        ch.dma("sync", eyesb[:], eyes_i)
        ch.dma("sync", eye64b[:], eye64_i)
        ch.dma("sync", iotacol[:], iotacol_i)

        def _ph():
            with ExitStack() as pa:
                ea = pa.enter_context
                xts = [ea(nc.sbuf_tensor(f"A_xt{i}", [128, D], F32)) for i in range(2)]
                xnbs = [ea(nc.sbuf_tensor(f"A_xnb{i}", [128, D], BF16)) for i in range(2)]
                xTts = [ea(nc.sbuf_tensor(f"A_xTt{i}", [128, 8, 128], BF16)) for i in range(2)]
                statss = [ea(nc.sbuf_tensor(f"A_stats{i}", [128, 16], F32)) for i in range(2)]
                cp = ChainP()
                for i in range(NTIL + 16):
                    tno = cp.begin_tile()
                    cp.wait_tile_end(tno - 2)
                    cp.wait_tile_end(tno - 3)
                    si = i % 2
                    xt, xnb, xTt, stats = xts[si], xnbs[si], xTts[si], statss[si]
                    if i < NTIL:
                        cp.add("sync", lambda e, i=i, xt=xt: e.dma_start(out=xt[:], in_=x_i[i]), 16)
                    else:
                        cp.add("sync", lambda e, i=i, xt=xt: e.dma_start(out=xt[:], in_=gxr_i[i - NTIL]), 16)
                    _ln_tile(cp, None, None, xt[:], stats, xnb[:])
                    def xbars(e, sem, xnb=xnb, xTt=xTt):
                        for dc in range(8):
                            r = e.dma_start_transpose(
                                xTt[:, dc, :], xnb[:, dc * 128:(dc + 1) * 128]).then_inc(sem, 16)
                        return r
                    cp.add("scalar" if i % 2 == 0 else "sync", xbars, 128, raw=True)
                    if i < NTIL:
                        b, tl = i // 16, i % 16
                        cp.add("sync", lambda e, b=b, tl=tl, xTt=xTt: e.dma_start(
                            out=d_xn0T.ap()[b, :, :, tl * 128:(tl + 1) * 128], in_=xTt[:]), 16)
                    else:
                        j = i - NTIL
                        cp.add("sync", lambda e, j=j, xTt=xTt: e.dma_start(
                            out=d_gxnT.ap()[:, :, j * 128:(j + 1) * 128], in_=xTt[:]), 16)
                cp.emit(ch, nc, "Ap")
        _ph()
        def _ph():
            with ExitStack() as pb:
                eb = pb.enter_context
                qT = eb(nc.sbuf_tensor("qT", [128, 8, T], BF16))
                kT = eb(nc.sbuf_tensor("kT", [128, 8, T], BF16))
                vb = eb(nc.sbuf_tensor("vb", [128, 16, D], BF16))
                yT = eb(nc.sbuf_tensor("yT", [128, 8, T], BF16))
                xb = eb(nc.sbuf_tensor("xb", [128, 8, T], BF16))
                wres = eb(nc.sbuf_tensor("wres", [128, 8, D], BF16))
                wstrips = [eb(nc.sbuf_tensor(f"wstrip{i}", [128, 8, 128], BF16)) for i in range(2)]
                expNT = eb(nc.sbuf_tensor("expNT", [128, 16, 128], BF16))
                expLT = eb(nc.sbuf_tensor("expLT", [128, T], BF16))
                repsb = eb(nc.sbuf_tensor("repsb", [128, 512], F32))
                rzlsb = eb(nc.sbuf_tensor("rzlsb", [1, 1024], F32))
                sc_ql = eb(nc.sbuf_tensor("sc_ql", [128, 64], F32))
                sc_kl = eb(nc.sbuf_tensor("sc_kl", [128, 64], F32))
                sc_qlbf = eb(nc.sbuf_tensor("sc_qlbf", [128, 64], BF16))
                sc_klbf = eb(nc.sbuf_tensor("sc_klbf", [128, 64], BF16))
                Ksb = eb(nc.sbuf_tensor("Ksb", [128, 64], F32))
                Kt_sb = eb(nc.sbuf_tensor("Kt_sb", [128, 64], F32))
                V_sb = eb(nc.sbuf_tensor("V_sb", [128, 64], F32))
                W_sb = eb(nc.sbuf_tensor("W_sb", [128, 64], F32))
                T_sb = eb(nc.sbuf_tensor("T_sb", [128, 64], F32))
                KVt_sb = eb(nc.sbuf_tensor("KVt_sb", [128, 64], F32))
                z1sb = eb(nc.sbuf_tensor("z1sb", [128, 64], F32))
                z2aug = eb(nc.sbuf_tensor("z2aug", [128, 65], BF16))
                smalls = eb(nc.sbuf_tensor("smalls", [128, 8], F32))
                s0ts = [eb(nc.sbuf_tensor(f"s0t{i}", [128, D], F32)) for i in range(2)]
                P0s = [eb(nc.psum_tensor(f"P0{i}", [128, 512], F32)) for i in range(2)]
                P1s = [eb(nc.psum_tensor(f"P1{i}", [128, 512], F32)) for i in range(2)]
                P2s = [eb(nc.psum_tensor(f"P2{i}", [128, 512], F32)) for i in range(2)]
                PBs = [eb(nc.psum_tensor(f"PB{i}", [128, 512], F32)) for i in range(2)]
                cp = ChainP()
                cp.begin_tile()
                cp.add("vector", lambda e: e.memset(z2aug[:, 64:65], 1.0))
                prev_b_end = 0
                for b in range(BPC):
                    t_xb = cp.begin_tile()
                    cp.wait_tile_end(prev_b_end)
                    cp.add("gpsimd", lambda e, b=b: e.dma_start(out=xb[:], in_=d_xn0T.ap()[b]), 16)
                    k_mm = 0
                    for part, dst in ((0, qT), (1, kT)):
                        for cc in range(8):
                            for tc in range(4):
                                tno = cp.begin_tile()
                                cp.wait_tile_end(tno - 2)
                                cp.wait_tile_end(tno - 3)
                                if part == 0 and cc == 0 and tc == 0:
                                    cp.wait_tile_end(t_xb)
                                ws = wstrips[(part * 8 + cc) % 2]
                                if tc == 0:
                                    cp.add("gpsimd", lambda e, part=part, cc=cc, ws=ws: e.dma_start(
                                        out=ws[:], in_=wqk_i[part * 8 + cc]), 16)
                                ps = P0s[k_mm % 2]
                                def qmm(e, cc=cc, tc=tc, ws=ws, ps=ps):
                                    for dc in range(8):
                                        r = e.matmul(out=ps[:, 0:512], lhsT=ws[:, dc, :],
                                                     rhs=xb[:, dc, tc * 512:(tc + 1) * 512],
                                                     start=(dc == 0), stop=(dc == 7))
                                    return r
                                cp.add("tensor", qmm)
                                cp.add("scalar", lambda e, cc=cc, tc=tc, dst=dst, ps=ps: e.copy(
                                    out=dst[:, cc, tc * 512:(tc + 1) * 512], in_=ps[:, 0:512]))
                                k_mm += 1
                    t_wv = cp.begin_tile()
                    cp.wait_tile_end(t_wv - 2)
                    cp.wait_tile_end(t_wv - 3)
                    cp.add("gpsimd", lambda e: e.dma_start(out=wres[:], in_=wv_i), 16)
                    last_v = t_wv
                    for tt in range(16):
                        for fh in range(2):
                            tno = cp.begin_tile()
                            cp.wait_tile_end(tno - 2)
                            cp.wait_tile_end(tno - 3)
                            if tt == 0 and fh == 0:
                                cp.wait_tile_end(t_wv)
                            ps = P0s[k_mm % 2]
                            def vmm(e, tt=tt, fh=fh, ps=ps):
                                for dc in range(8):
                                    r = e.matmul(out=ps[:, 0:512],
                                                 lhsT=xb[:, dc, tt * 128:(tt + 1) * 128],
                                                 rhs=wres[:, dc, fh * 512:(fh + 1) * 512],
                                                 start=(dc == 0), stop=(dc == 7))
                                return r
                            cp.add("tensor", vmm)
                            cp.add("scalar", lambda e, tt=tt, fh=fh, ps=ps: e.copy(
                                out=vb[:, tt, fh * 512:(fh + 1) * 512], in_=ps[:, 0:512]))
                            last_v = tno
                            k_mm += 1
                    last_head = last_v
                    for h in range(NH):
                        s = h % 2
                        bb = 64 * s
                        cc = h // 2
                        B0, B1 = bb, bb + 64
                        tno = cp.begin_tile()
                        cp.wait_tile_end(tno - 2)
                        cp.wait_tile_end(tno - 3)
                        if h == 0:
                            cp.wait_tile_end(last_v)
                        P0, P1, P2, PB = P0s[s], P1s[s], P2s[s], PBs[s]
                        qTh = qT[B0:B1, cc, :]
                        kTh = kT[B0:B1, cc, :]
                        cp.add("vector", lambda e, qTh=qTh, B0=B0, B1=B1: e.tensor_reduce(
                            out=sc_ql[B0:B1, :], in_=qTh.rearrange("p (l g) -> p l g", g=32),
                            axis=AX.X, op=ALU.add))
                        cp.add("vector", lambda e, kTh=kTh, B0=B0, B1=B1: e.tensor_reduce(
                            out=sc_kl[B0:B1, :], in_=kTh.rearrange("p (l g) -> p l g", g=32),
                            axis=AX.X, op=ALU.add))
                        cp.add("scalar", lambda e, B0=B0, B1=B1: e.copy(out=sc_qlbf[B0:B1, :], in_=sc_ql[B0:B1, :]))
                        cp.add("scalar", lambda e, B0=B0, B1=B1: e.copy(out=sc_klbf[B0:B1, :], in_=sc_kl[B0:B1, :]))
                        cp.add("tensor", lambda e, B0=B0, B1=B1, P0=P0: e.matmul(
                            out=P0[B0:B1, 0:64], lhsT=sc_ql[B0:B1, :],
                            rhs=sc_kl[B0:B1, :], start=True, stop=True))
                        cp.add("scalar", lambda e, B0=B0, B1=B1, P0=P0: e.activation(
                            out=Ksb[B0:B1, :], in_=P0[B0:B1, 0:64], func=AF.Exp, scale=1.0 / 1024.0))
                        cp.add("vector", lambda e, B0=B0, B1=B1: e.tensor_reduce(
                            out=smalls[B0:B1, 0:1], in_=Ksb[B0:B1, :], axis=AX.X, op=ALU.add))
                        cp.add("vector", lambda e, B0=B0, B1=B1: e.reciprocal(
                            out=smalls[B0:B1, 1:2], in_=smalls[B0:B1, 0:1]))
                        cp.add("vector", lambda e, B0=B0, B1=B1: e.tensor_scalar(
                            out=Ksb[B0:B1, :], in0=Ksb[B0:B1, :],
                            scalar1=smalls[B0:B1, 1:2], scalar2=None, op0=ALU.mult))
                        cp.add("tensor", lambda e, B0=B0, B1=B1, P1=P1: e.matmul(
                            out=P1[0:1, 0:64], lhsT=onesf[B0:B1, 0:1], rhs=Ksb[B0:B1, :],
                            start=True, stop=True))
                        cp.add("vector", lambda e, s=s, P1=P1: e.reduce_max(
                            out=smalls[0:1, 2 + 4 * s:3 + 4 * s], in_=P1[0:1, 0:64], axis=AX.X))
                        cp.add("vector", lambda e, s=s: e.reciprocal(
                            out=smalls[0:1, 3 + 4 * s:4 + 4 * s], in_=smalls[0:1, 2 + 4 * s:3 + 4 * s]))
                        cp.add("tensor", lambda e, B0=B0, B1=B1, s=s, P2=P2: e.matmul(
                            out=P2[B0:B1, 0:1], lhsT=onesf[0:1, B0:B1],
                            rhs=smalls[0:1, 3 + 4 * s:4 + 4 * s], start=True, stop=True))
                        cp.add("tensor", lambda e, B0=B0, B1=B1, P0=P0: e.matmul(
                            out=P0[B0:B1, 0:64], lhsT=Ksb[B0:B1, :], rhs=eye64b[B0:B1, :],
                            start=True, stop=True))
                        cp.add("vector", lambda e, B0=B0, B1=B1, P0=P0, P2=P2: e.tensor_scalar(
                            out=V_sb[B0:B1, :], in0=P0[B0:B1, 0:64],
                            scalar1=P2[B0:B1, 0:1], scalar2=None, op0=ALU.mult))
                        cp.add("vector", lambda e, B0=B0, B1=B1, P2=P2: e.tensor_scalar(
                            out=W_sb[B0:B1, :], in0=Ksb[B0:B1, :],
                            scalar1=P2[B0:B1, 0:1], scalar2=None, op0=ALU.mult))
                        cp.add("scalar", lambda e, B0=B0, B1=B1, P0=P0: e.copy(
                            out=Kt_sb[B0:B1, :], in_=P0[B0:B1, 0:64]))
                        for _ in range(6):
                            cp.add("tensor", lambda e, B0=B0, B1=B1, P0=P0: e.matmul(
                                out=P0[B0:B1, 0:64], lhsT=Kt_sb[B0:B1, :], rhs=V_sb[B0:B1, :],
                                start=True, stop=True))
                            cp.add("vector", lambda e, B0=B0, B1=B1, P0=P0: e.tensor_tensor(
                                out=T_sb[B0:B1, :], in0=eyesb[B0:B1, 0, :],
                                in1=P0[B0:B1, 0:64], op=ALU.subtract))
                            cp.add("tensor", lambda e, B0=B0, B1=B1, P1=P1: e.matmul(
                                out=P1[B0:B1, 0:64], lhsT=V_sb[B0:B1, :], rhs=Kt_sb[B0:B1, :],
                                start=True, stop=True))
                            cp.add("scalar", lambda e, B0=B0, B1=B1, P1=P1: e.copy(
                                out=KVt_sb[B0:B1, :], in_=P1[B0:B1, 0:64]))
                            cp.add("tensor", lambda e, B0=B0, B1=B1, P0=P0: e.matmul(
                                out=P0[B0:B1, 0:64], lhsT=KVt_sb[B0:B1, :], rhs=T_sb[B0:B1, :],
                                start=True, stop=True))
                            cp.add("vector", lambda e, B0=B0, B1=B1, P0=P0: e.tensor_tensor(
                                out=T_sb[B0:B1, :], in0=eyesb[B0:B1, 1, :],
                                in1=P0[B0:B1, 0:64], op=ALU.subtract))
                            cp.add("tensor", lambda e, B0=B0, B1=B1, P0=P0: e.matmul(
                                out=P0[B0:B1, 0:64], lhsT=KVt_sb[B0:B1, :], rhs=T_sb[B0:B1, :],
                                start=True, stop=True))
                            cp.add("vector", lambda e, B0=B0, B1=B1, P0=P0: e.tensor_tensor(
                                out=T_sb[B0:B1, :], in0=eyesb[B0:B1, 2, :],
                                in1=P0[B0:B1, 0:64], op=ALU.subtract))
                            cp.add("tensor", lambda e, B0=B0, B1=B1, P1=P1: e.matmul(
                                out=P1[B0:B1, 0:64], lhsT=W_sb[B0:B1, :], rhs=T_sb[B0:B1, :],
                                start=True, stop=True))
                            cp.add("tensor", lambda e, B0=B0, B1=B1, P2=P2: e.matmul(
                                out=P2[B0:B1, 0:64], lhsT=T_sb[B0:B1, :], rhs=W_sb[B0:B1, :],
                                start=True, stop=True))
                            cp.add("scalar", lambda e, B0=B0, B1=B1, P1=P1: e.mul(
                                V_sb[B0:B1, :], P1[B0:B1, 0:64], 0.25))
                            cp.add("scalar", lambda e, B0=B0, B1=B1, P2=P2: e.mul(
                                W_sb[B0:B1, :], P2[B0:B1, 0:64], 0.25))
                        for tc in range(16):
                            cp.add("tensor", lambda e, tc=tc, B0=B0, B1=B1, cc=cc, P0=P0: e.matmul(
                                out=P0[0:128, 0:64], lhsT=kT[B0:B1, cc, tc * 128:(tc + 1) * 128],
                                rhs=sc_qlbf[B0:B1, :], start=True, stop=True))
                            cp.add("scalar", lambda e, tc=tc, s=s, P0=P0: e.activation(
                                out=expNT[:, tc, s * 64:(s + 1) * 64], in_=P0[0:128, 0:64],
                                func=AF.Exp, scale=1.0 / 32.0))
                        def z1mm(e, h=h, s=s, B0=B0, B1=B1, P1=P1, P2=P2):
                            for tc in range(16):
                                e.matmul(out=P2[B0:B1, 0:64], lhsT=expNT[:, tc, s * 64:(s + 1) * 64],
                                         rhs=vb[:, tc, h * 64:(h + 1) * 64],
                                         start=(tc == 0), stop=(tc == 15))
                                r = e.matmul(out=P1[B0:B1, 0:1], lhsT=expNT[:, tc, s * 64:(s + 1) * 64],
                                             rhs=onesbf[:, 0:1], start=(tc == 0), stop=(tc == 15))
                            return r
                        cp.add("tensor", z1mm)
                        cp.add("vector", lambda e, B0=B0, B1=B1, P1=P1: e.reciprocal(
                            out=smalls[B0:B1, 4:5], in_=P1[B0:B1, 0:1]))
                        cp.add("vector", lambda e, B0=B0, B1=B1, P2=P2: e.tensor_scalar(
                            out=z1sb[B0:B1, :], in0=P2[B0:B1, 0:64],
                            scalar1=smalls[B0:B1, 4:5], scalar2=None, op0=ALU.mult))
                        cp.add("tensor", lambda e, B0=B0, B1=B1, P0=P0: e.matmul(
                            out=P0[B0:B1, 0:64], lhsT=W_sb[B0:B1, :], rhs=z1sb[B0:B1, :],
                            start=True, stop=True))
                        cp.add("scalar", lambda e, B0=B0, B1=B1, P0=P0: e.copy(
                            out=z2aug[B0:B1, 0:64], in_=P0[B0:B1, 0:64]))
                        for tc in range(4):
                            cp.add("tensor", lambda e, tc=tc, B0=B0, B1=B1, cc=cc, P1=P1: e.matmul(
                                out=P1[B0:B1, 0:512], lhsT=sc_klbf[B0:B1, :],
                                rhs=qT[B0:B1, cc, tc * 512:(tc + 1) * 512], start=True, stop=True))
                            cp.add("scalar", lambda e, tc=tc, B0=B0, B1=B1, P1=P1: e.activation(
                                out=expLT[B0:B1, tc * 512:(tc + 1) * 512], in_=P1[B0:B1, 0:512],
                                func=AF.Exp, scale=1.0 / 32.0))
                        for tc in range(4):
                            def ymm(e, tc=tc, B0=B0, B1=B1, PB=PB, P1=P1):
                                e.matmul(out=PB[B0:B1, 0:512], lhsT=z2aug[B0:B1, 0:64],
                                         rhs=expLT[B0:B1, tc * 512:(tc + 1) * 512],
                                         start=True, stop=True)
                                return e.matmul(out=P1[0:1, 0:512], lhsT=z2aug[B0:B1, 64:65],
                                                rhs=expLT[B0:B1, tc * 512:(tc + 1) * 512],
                                                start=True, stop=True)
                            cp.add("tensor", ymm)
                            cp.add("vector", lambda e, s=s, P1=P1: e.reciprocal(
                                out=rzlsb[0:1, s * 512:(s + 1) * 512], in_=P1[0:1, 0:512]))
                            cp.add("tensor", lambda e, B0=B0, B1=B1, s=s, P2=P2: e.matmul(
                                out=P2[B0:B1, 0:512], lhsT=onesf[0:1, B0:B1],
                                rhs=rzlsb[0:1, s * 512:(s + 1) * 512], start=True, stop=True))
                            cp.add("scalar", lambda e, B0=B0, B1=B1, P2=P2: e.copy(
                                out=repsb[B0:B1, 0:512], in_=P2[B0:B1, 0:512]))
                            cp.add("vector", lambda e, tc=tc, B0=B0, B1=B1, cc=cc, PB=PB: e.tensor_tensor(
                                out=yT[B0:B1, cc, tc * 512:(tc + 1) * 512],
                                in0=PB[B0:B1, 0:512], in1=repsb[B0:B1, 0:512],
                                op=ALU.mult))
                        last_head = tno
                    t_wp = cp.begin_tile()
                    cp.wait_tile_end(t_wp - 2)
                    cp.wait_tile_end(t_wp - 3)
                    cp.wait_tile_end(last_head)
                    cp.add("gpsimd", lambda e: e.dma_start(out=wres[:], in_=wp_i), 16)
                    for tt in range(16):
                        tno = cp.begin_tile()
                        cp.wait_tile_end(tno - 2)
                        cp.wait_tile_end(tno - 3)
                        if tt == 0:
                            cp.wait_tile_end(t_wp)
                        s0t = s0ts[tt % 2]
                        for fh in range(2):
                            ps = (P0s if fh == 0 else P1s)[tt % 2]
                            def pmm(e, tt=tt, fh=fh, ps=ps):
                                for dc in range(8):
                                    r = e.matmul(out=ps[:, 0:512],
                                                 lhsT=yT[:, dc, tt * 128:(tt + 1) * 128],
                                                 rhs=wres[:, dc, fh * 512:(fh + 1) * 512],
                                                 start=(dc == 0), stop=(dc == 7))
                                return r
                            cp.add("tensor", pmm)
                            cp.add("scalar", lambda e, fh=fh, ps=ps, s0t=s0t: e.copy(
                                out=s0t[:, fh * 512:(fh + 1) * 512], in_=ps[:, 0:512]))
                        cp.add("sync", lambda e, b=b, tt=tt, s0t=s0t: e.dma_start(
                            out=d_s0.ap()[b, tt], in_=s0t[:]), 16)
                        prev_b_end = tno
                cp.emit(ch, nc, "Bp")
        _ph()
        def _ph():
            with ExitStack() as pc:
                ec = pc.enter_context
                gx = ec(nc.sbuf_tensor("C_gx", [128, 8, 2048], BF16))
                w1sb = ec(nc.sbuf_tensor("C_w1sb", [128, 32, 8, 128], BF16))
                w2sbs = [ec(nc.sbuf_tensor(f"C_w2sb{i}", [128, 32, 128], BF16)) for i in range(2)]
                b1sb = ec(nc.sbuf_tensor("C_b1sb", [128, 32], F32))
                b2sb = ec(nc.sbuf_tensor("C_b2sb", [128, 8], F32))
                wrow = ec(nc.sbuf_tensor("C_wrow", [1, 2048], F32))
                wrep = ec(nc.sbuf_tensor("C_wrep", [128, 2048], F32))
                hTq = ec(nc.sbuf_tensor("C_hTq", [128, 32, 512], BF16))
                gas = [ec(nc.sbuf_tensor(f"C_ga{i}", [128, 512], F32)) for i in range(2)]
                gbs = [ec(nc.sbuf_tensor(f"C_gb{i}", [128, 512], F32)) for i in range(2)]
                gcs = [ec(nc.sbuf_tensor(f"C_gc{i}", [128, 512], F32)) for i in range(2)]
                wouts = [ec(nc.sbuf_tensor(f"C_wout{i}", [128, 512], BF16)) for i in range(2)]
                wtps = [ec(nc.sbuf_tensor(f"C_wtp{i}", [128, 128], BF16)) for i in range(2)]
                PHs = [ec(nc.psum_tensor(f"PH{i}", [128, 512], F32)) for i in range(2)]
                POs = [ec(nc.psum_tensor(f"PO{i}", [128, 512], F32)) for i in range(2)]
                ch.dma("gpsimd", gx[:], d_gxnT.ap())
                ch.dma("gpsimd", w1sb[:], w1_i)
                ch.dma("sync", b1sb[:], b1_i)
                ch.dma("sync", b2sb[:], b2_i)
                ch.dma("sync", wrow[:], wvec_i)
                cp = ChainP()
                t0_ = cp.begin_tile()
                for tc in range(4):
                    cp.add("tensor", lambda e, tc=tc: e.matmul(
                        out=POs[0][:, 0:512], lhsT=onesf[0:1, 0:128],
                        rhs=wrow[0:1, tc * 512:(tc + 1) * 512], start=True, stop=True))
                    cp.add("scalar", lambda e, tc=tc: e.copy(
                        out=wrep[:, tc * 512:(tc + 1) * 512], in_=POs[0][:, 0:512]))
                kt = 0
                last_omm = -1
                for tq in range(4):
                    for ffc in range(32):
                        tno = cp.begin_tile()
                        cp.wait_tile_end(tno - 2)
                        cp.wait_tile_end(tno - 3)
                        if ffc == 0 and tq > 0:
                            cp.wait_tile_end(last_omm)
                        s = kt % 2
                        PH, ga, gb_, gc_ = PHs[s], gas[s], gbs[s], gcs[s]
                        def hmm_(e, tq=tq, ffc=ffc, PH=PH):
                            for dc in range(8):
                                r = e.matmul(out=PH[:, 0:512], lhsT=w1sb[:, ffc, dc, :],
                                             rhs=gx[:, dc, tq * 512:(tq + 1) * 512],
                                             start=(dc == 0), stop=(dc == 7))
                            return r
                        cp.add("tensor", hmm_)
                        cp.add("vector", lambda e, ffc=ffc, PH=PH, ga=ga: e.tensor_scalar(
                            out=ga[:], in0=PH[:, 0:512], scalar1=b1sb[:, ffc:ffc + 1],
                            scalar2=None, op0=ALU.add))
                        cp.add("scalar", lambda e, ga=ga, gb_=gb_: e.activation(out=gb_[:], in_=ga[:], func=AF.Square))
                        cp.add("vector", lambda e, gb_=gb_: e.tensor_scalar(
                            out=gb_[:], in0=gb_[:], scalar1=0.044715, scalar2=1.0,
                            op0=ALU.mult, op1=ALU.add))
                        cp.add("vector", lambda e, gb_=gb_, gc_=gc_, ga=ga: e.tensor_tensor(
                            out=gc_[:], in0=gb_[:], in1=ga[:], op=ALU.mult))
                        cp.add("scalar", lambda e, gc_=gc_: e.activation(out=gc_[:], in_=gc_[:], func=AF.Tanh, scale=C_GELU))
                        cp.add("vector", lambda e, gc_=gc_: e.tensor_scalar(
                            out=gc_[:], in0=gc_[:], scalar1=1.0, scalar2=0.5, op0=ALU.add, op1=ALU.mult))
                        cp.add("vector", lambda e, ffc=ffc, gc_=gc_, ga=ga: e.tensor_tensor(
                            out=hTq[:, ffc, :], in0=gc_[:], in1=ga[:], op=ALU.mult))
                        kt += 1
                        last_h = tno
                    for dc in range(8):
                        tno = cp.begin_tile()
                        cp.wait_tile_end(tno - 2)
                        cp.wait_tile_end(tno - 3)
                        if dc == 0:
                            cp.wait_tile_end(last_h)
                        s = kt % 2
                        w2sb, ga, wout, wtp = w2sbs[s], gas[s], wouts[s], gcs[s]
                        PO = POs[s]
                        cp.add("gpsimd", lambda e, dc=dc, w2sb=w2sb: e.dma_start(
                            out=w2sb[:], in_=w2_i[dc]), 16)
                        def omm(e, PO=PO, w2sb=w2sb):
                            for ffc in range(32):
                                r = e.matmul(out=PO[:, 0:512], lhsT=w2sb[:, ffc, :],
                                             rhs=hTq[:, ffc, :], start=(ffc == 0), stop=(ffc == 31))
                            return r
                        cp.add("tensor", omm)
                        cp.add("vector", lambda e, dc=dc, PO=PO, ga=ga: e.tensor_scalar(
                            out=ga[:], in0=PO[:, 0:512], scalar1=b2sb[:, dc:dc + 1],
                            scalar2=None, op0=ALU.add))
                        cp.add("vector", lambda e, tq=tq, ga=ga, wout=wout: e.tensor_tensor(
                            out=wout[:], in0=ga[:], in1=wrep[:, tq * 512:(tq + 1) * 512], op=ALU.mult))
                        for sc in range(4):
                            wtpb = wtps[sc % 2]
                            cp.add("scalar", lambda e, sc=sc, wout=wout, wtpb=wtpb: e.dma_start_transpose(
                                wtpb[:], wout[:, sc * 128:(sc + 1) * 128]), 16)
                            cp.add("sync", lambda e, tq=tq, sc=sc, dc=dc, wtpb=wtpb: e.dma_start(
                                out=agw_in.ap()[tq * 512 + sc * 128:tq * 512 + (sc + 1) * 128,
                                                dc * 128:(dc + 1) * 128], in_=wtpb[:]), 16)
                        kt += 1
                        last_omm = tno
                cp.emit(ch, nc, "Cp")
            _collective(ch, "AllGather", ALU.bypass, [agw_in.ap()], [agw_out.ap()],
                        "ccw", nc)
        _ph()
        def _ph():
            with ExitStack() as pd:
                ed = pd.enter_context
                idxsb = ed(nc.sbuf_tensor("D_idxsb", [128, 128], I32))
                idxf = ed(nc.sbuf_tensor("D_idxf", [128, 128], F32))
                iotasb = ed(nc.sbuf_tensor("D_iotasb", [128, 1024], F32))
                rbasesb = ed(nc.sbuf_tensor("D_rbasesb", [128, 1], F32))
                ohsb = ed(nc.sbuf_tensor("D_ohsb", [128, 1024], BF16))
                wchunk = ed(nc.sbuf_tensor("D_wchunk", [128, 512], BF16))
                outsb = ed(nc.sbuf_tensor("D_outsb", [128, 512], F32))
                PRs = [ed(nc.psum_tensor(f"PR{i}", [128, 512], F32)) for i in range(8)]
                ch.dma("sync", idxsb[:].rearrange("p (e s) -> p e s", e=8), idxi_i)
                ch.dma("sync", iotasb[:], iota1024_i)
                ch.dma("sync", rbasesb[:], rbase_i)
                ch.add("scalar", lambda e: e.copy(out=idxf[:], in_=idxsb[:]))
                ch.add("vector", lambda e: e.tensor_scalar(
                    out=idxf[:], in0=idxf[:], scalar1=rbasesb[:, 0:1], scalar2=None,
                    op0=ALU.subtract))
                for dh in range(2):
                    n = 0
                    for ee2 in range(8):
                        for sc in range(16):
                            def wld(e, ee2=ee2, sc=sc, dh=dh):
                                ap = bass.AP(agw_out, ee2 * 2048 * D + sc * 128 * D + dh * 512,
                                             [[D, 128], [1, 512]])
                                return e.dma_start(out=wchunk[:], in_=ap)
                            ch.add("gpsimd", wld, 16)
                            ch.add("vector", lambda e, ee2=ee2, sc=sc: e.tensor_scalar(
                                out=ohsb[:], in0=iotasb[:],
                                scalar1=idxf[:, ee2 * 16 + sc:ee2 * 16 + sc + 1],
                                scalar2=None, op0=ALU.is_equal))
                            def dmm(e, n=n):
                                for rt in range(8):
                                    r = e.matmul(out=PRs[rt][:, 0:512],
                                                 lhsT=ohsb[:, rt * 128:(rt + 1) * 128],
                                                 rhs=wchunk[:], start=(n == 0),
                                                 stop=(n == 127))
                                return r
                            ch.add("tensor", dmm)
                            n += 1
                    for rt in range(8):
                        ch.add("scalar", lambda e, rt=rt: e.copy(out=outsb[:], in_=PRs[rt][:, 0:512]))
                        ch.dma("sync", agsc_in.ap()[rt, :, dh * 512:(dh + 1) * 512], outsb[:])
                _collective(ch, "AllGather", ALU.bypass, [agsc_in.ap()], [agsc_out.ap()],
                            "ccs", nc)
        _ph()
        agh_sems = []
        def _ph():
            with ExitStack() as pe:
                ee = pe.enter_context
                xts = [ee(nc.sbuf_tensor(f"E_xt{i}", [128, D], F32)) for i in range(2)]
                s0ts = [ee(nc.sbuf_tensor(f"E_s0t{i}", [128, D], F32)) for i in range(2)]
                g0ts = [ee(nc.sbuf_tensor(f"E_g0t{i}", [128, D], F32)) for i in range(2)]
                x1ts = [ee(nc.sbuf_tensor(f"E_x1t{i}", [128, D], F32)) for i in range(2)]
                xn1ts = [ee(nc.sbuf_tensor(f"E_xn1t{i}", [128, D], F32)) for i in range(2)]
                xn1bts = [ee(nc.sbuf_tensor(f"E_xn1bt{i}", [128, D], BF16)) for i in range(2)]
                hbts = [ee(nc.sbuf_tensor(f"E_hbt{i}", [128, D], BF16)) for i in range(2)]
                sqs = [ee(nc.sbuf_tensor(f"E_sq{i}", [128, D], F32)) for i in range(2)]
                statss = [ee(nc.sbuf_tensor(f"E_stats{i}", [128, 16], F32)) for i in range(2)]
                xTts = [ee(nc.sbuf_tensor(f"E_xTt{i}", [128, 8, 128], BF16)) for i in range(2)]
                nwrep = ee(nc.sbuf_tensor("E_nwrep", [128, D], F32))
                nwrow = ee(nc.sbuf_tensor("E_nwrow", [1, D], F32))
                iscoresb = ee(nc.sbuf_tensor("E_iscore", [128, 1], F32))
                wrsb = ee(nc.sbuf_tensor("E_wrsb", [128, 8, 8], F32))
                brsb = ee(nc.sbuf_tensor("E_brsb", [1, 8], F32))
                iota8 = ee(nc.sbuf_tensor("E_iota8", [128, 8], F32))
                xrows = ee(nc.sbuf_tensor("E_xrows", [1, BPC, D], F32))
                eye4r = ee(nc.sbuf_tensor("E_eye4r", [1, BPC * BPC], F32))
                xrT = ee(nc.sbuf_tensor("E_xrT", [128, 8, BPC], F32))
                scsb = ee(nc.sbuf_tensor("E_scsb", [BPC, 8], F32))
                m1 = ee(nc.sbuf_tensor("E_m1", [BPC, 8], F32))
                m2 = ee(nc.sbuf_tensor("E_m2", [BPC, 8], F32))
                sm = ee(nc.sbuf_tensor("E_sm", [BPC, 8], F32))
                idxsb = ee(nc.sbuf_tensor("E_idxsb", [BPC, 2], F32))
                idxflat = ee(nc.sbuf_tensor("E_idxflat", [1, 2 * BPC], F32))
                basecol = ee(nc.sbuf_tensor("E_basecol", [128, 1], F32))
                base1024 = ee(nc.sbuf_tensor("E_base1024", [128, 1], F32))
                u32col = ee(nc.sbuf_tensor("E_u32col", [128, 1], mybir.dt.uint32))
                wgsb = ee(nc.sbuf_tensor("E_wgsb", [128, D], BF16))
                b1fsb = ee(nc.sbuf_tensor("E_b1fsb", [8, D], F32))
                b2fsb = ee(nc.sbuf_tensor("E_b2fsb", [8, D], F32))
                mcol = ee(nc.sbuf_tensor("E_mcol", [8, 1], F32))
                brow = ee(nc.sbuf_tensor("E_brow", [1, D], BF16))
                b2acc = ee(nc.sbuf_tensor("E_b2acc", [1, D], F32))
                P0e = ee(nc.psum_tensor("P0e", [128, 512], F32))
                P1e = ee(nc.psum_tensor("P1e", [128, 512], F32))
                ch.dma("sync", nwrow[:], nw_i)
                ch.dma("sync", iscoresb[:], iscore_i)
                ch.dma("sync", wrsb[:], wr_i)
                ch.dma("sync", brsb[:], br_i)
                ch.dma("sync", eye4r[:],
                       nc.dram_tensor("eye4r", [1, BPC * BPC], F32, kind="ExternalInput").ap())
                ch.dma("sync", iota8[:],
                       nc.dram_tensor("iota8", [128, 8], F32, kind="ExternalInput").ap())
                ch.dma("sync", b1fsb[:], b1f_i)
                ch.dma("sync", b2fsb[:], b2f_i)
                for fh in range(2):
                    ch.add("tensor", lambda e, fh=fh: e.matmul(
                        out=P0e[:, 0:512], lhsT=onesf[0:1, 0:128],
                        rhs=nwrow[0:1, fh * 512:(fh + 1) * 512], start=True, stop=True))
                    ch.add("scalar", lambda e, fh=fh: e.copy(
                        out=nwrep[:, fh * 512:(fh + 1) * 512], in_=P0e[:, 0:512]))
                cp = ChainP()
                for i in range(NTIL):
                    b, tl = i // 16, i % 16
                    tno = cp.begin_tile()
                    cp.wait_tile_end(tno - 2)
                    cp.wait_tile_end(tno - 3)
                    si = i % 2
                    xt, s0t, g0t, x1t = xts[si], s0ts[si], g0ts[si], x1ts[si]
                    xn1t, xn1bt, hbt, sq = xn1ts[si], xn1bts[si], hbts[si], sqs[si]
                    stats, xTt = statss[si], xTts[si]
                    def lds(e, sem, i=i, b=b, tl=tl, xt=xt, s0t=s0t, g0t=g0t):
                        e.dma_start(out=xt[:], in_=x_i[i]).then_inc(sem, 16)
                        e.dma_start(out=s0t[:], in_=d_s0.ap()[b, tl]).then_inc(sem, 16)
                        return e.dma_start(out=g0t[:], in_=agsc_out.ap()[i]).then_inc(sem, 16)
                    cp.add("sync", lds, 48, raw=True)
                    cp.add("vector", lambda e, g0t=g0t: e.tensor_scalar(
                        out=g0t[:], in0=g0t[:], scalar1=iscoresb[:, 0:1], scalar2=None, op0=ALU.mult))
                    cp.add("scalar", lambda e, g0t=g0t: e.activation(out=g0t[:], in_=g0t[:], func=AF.Silu))
                    cp.add("vector", lambda e, s0t=s0t, g0t=g0t: e.tensor_tensor(
                        out=s0t[:], in0=s0t[:], in1=g0t[:], op=ALU.mult))
                    cp.add("vector", lambda e, x1t=x1t, xt=xt, s0t=s0t: e.tensor_tensor(
                        out=x1t[:], in0=xt[:], in1=s0t[:], op=ALU.add))
                    cp.add("sync", lambda e, b=b, tl=tl, x1t=x1t: e.dma_start(
                        out=d_x1.ap()[b, tl], in_=x1t[:]), 16)
                    _ln_tile(cp, None, None, x1t[:], stats, xn1t[:], extra_outs=(xn1bt[:],))
                    cp.add("sync", lambda e, b=b, tl=tl, xn1t=xn1t: e.dma_start(
                        out=d_xn1.ap()[b, tl], in_=xn1t[:]), 16)
                    def xbars1(e, sem, xn1bt=xn1bt, xTt=xTt):
                        for dc in range(8):
                            r = e.dma_start_transpose(
                                xTt[:, dc, :], xn1bt[:, dc * 128:(dc + 1) * 128]).then_inc(sem, 16)
                        return r
                    cp.add("scalar" if i % 2 == 0 else "sync", xbars1, 128, raw=True)
                    cp.add("sync", lambda e, b=b, tl=tl, xTt=xTt: e.dma_start(
                        out=d_xn1T.ap()[b, :, :, tl * 128:(tl + 1) * 128], in_=xTt[:]), 16)
                    cp.add("scalar", lambda e, xn1t=xn1t, sq=sq: e.activation(out=sq[:], in_=xn1t[:], func=AF.Square))
                    cp.add("vector", lambda e, sq=sq, stats=stats: e.tensor_reduce(
                        out=stats[:, 0:1], in_=sq[:], axis=AX.X, op=ALU.add))
                    cp.add("vector", lambda e, stats=stats: e.tensor_scalar(
                        out=stats[:, 0:1], in0=stats[:, 0:1], scalar1=1.0 / D, scalar2=EPS,
                        op0=ALU.mult, op1=ALU.add))
                    cp.add("scalar", lambda e, stats=stats: e.activation(out=stats[:, 1:2], in_=stats[:, 0:1], func=AF.Sqrt))
                    cp.add("vector", lambda e, stats=stats: e.reciprocal(out=stats[:, 2:3], in_=stats[:, 1:2]))
                    cp.add("vector", lambda e, sq=sq, xn1t=xn1t, stats=stats: e.tensor_scalar(
                        out=sq[:], in0=xn1t[:], scalar1=stats[:, 2:3], scalar2=None, op0=ALU.mult))
                    cp.add("vector", lambda e, hbt=hbt, sq=sq: e.tensor_tensor(
                        out=hbt[:], in0=sq[:], in1=nwrep[:], op=ALU.mult))
                    def xbars2(e, sem, hbt=hbt, xTt=xTt):
                        for dc in range(8):
                            r = e.dma_start_transpose(
                                xTt[:, dc, :], hbt[:, dc * 128:(dc + 1) * 128]).then_inc(sem, 16)
                        return r
                    cp.add("sync" if i % 2 == 0 else "scalar", xbars2, 128, raw=True)
                    def hdma(e, b=b, tl=tl, xTt=xTt):
                        ap = bass.AP(agh_inB[b], tl * 128,
                                     [[T, 128], [128 * T, 8], [1, 128]])
                        return e.dma_start(out=ap, in_=xTt[:])
                    cp.add("sync", hdma, 16)
                    if tl == 15:
                        cp.wait_tile_end(tno - 1)
                        agh_sems.append(_collective_issue(
                            cp, "AllGather", ALU.bypass, [agh_inB[b].ap()],
                            [agh_outB[b].ap()], f"cch{b}", nc))
                    if tl == 0:
                        cp.add("vector", lambda e, b=b, xn1t=xn1t: e.tensor_copy(
                            out=xrows[0:1, b, :], in_=xn1t[0:1, :]))
                cp.emit(ch, nc, "Ep")
                for dc in range(8):
                    def trx(e, dc=dc):
                        for bb in range(BPC):
                            r = e.matmul(out=P0e[0:128, 0:BPC],
                                         lhsT=xrows[0:1, bb, dc * 128:(dc + 1) * 128],
                                         rhs=eye4r[0:1, bb * BPC:(bb + 1) * BPC],
                                         start=(bb == 0), stop=(bb == BPC - 1))
                        return r
                    ch.add("tensor", trx)
                    ch.add("scalar", lambda e, dc=dc: e.copy(out=xrT[:, dc, :], in_=P0e[0:128, 0:BPC]))
                def smm(e):
                    for dc in range(8):
                        e.matmul(out=P1e[0:BPC, 0:8], lhsT=xrT[:, dc, :], rhs=wrsb[:, dc, :],
                                 start=(dc == 0), stop=False)
                    return e.matmul(out=P1e[0:BPC, 0:8], lhsT=onesf[0:1, 0:BPC], rhs=brsb[:],
                                    start=False, stop=True)
                ch.add("tensor", smm)
                ch.add("scalar", lambda e: e.copy(out=scsb[:], in_=P1e[0:BPC, 0:8]))
                ch.add("vector", lambda e: e.reduce_max(out=sm[:, 0:1], in_=scsb[:], axis=AX.X))
                ch.add("vector", lambda e: e.tensor_scalar(
                    out=m1[:], in0=scsb[:], scalar1=sm[:, 0:1], scalar2=None, op0=ALU.is_ge))
                ch.add("vector", lambda e: e.tensor_scalar(
                    out=m2[:], in0=m1[:], scalar1=1e30, scalar2=None, op0=ALU.mult))
                ch.add("vector", lambda e: e.tensor_tensor(out=m2[:], in0=scsb[:], in1=m2[:], op=ALU.subtract))
                ch.add("vector", lambda e: e.reduce_max(out=sm[:, 1:2], in_=m2[:], axis=AX.X))
                ch.add("vector", lambda e: e.tensor_scalar(
                    out=m2[:], in0=m2[:], scalar1=sm[:, 1:2], scalar2=None, op0=ALU.is_ge))
                ch.add("vector", lambda e: e.tensor_tensor(out=m1[:], in0=m1[:], in1=iota8[0:BPC, :], op=ALU.mult))
                ch.add("vector", lambda e: e.tensor_tensor(out=m2[:], in0=m2[:], in1=iota8[0:BPC, :], op=ALU.mult))
                ch.add("vector", lambda e: e.tensor_reduce(out=idxsb[:, 0:1], in_=m1[:], axis=AX.X, op=ALU.add))
                ch.add("vector", lambda e: e.tensor_reduce(out=idxsb[:, 1:2], in_=m2[:], axis=AX.X, op=ALU.add))
                ch.dma("sync", d_idx.ap()[:, :], idxsb[:])
                ch.dma("sync", idxflat[:], d_idx.ap().rearrange("b k -> (b k)").unsqueeze(0))
                for s in range(2 * BPC):
                    b, k = s // 2, s % 2
                    ch.add("tensor", lambda e, s=s: e.matmul(
                        out=P0e[0:128, 0:1], lhsT=onesf[0:1, 0:128],
                        rhs=idxflat[0:1, s:s + 1], start=True, stop=True))
                    ch.add("scalar", lambda e: e.copy(out=basecol[:], in_=P0e[0:128, 0:1]))
                    ch.add("vector", lambda e: e.tensor_scalar(
                        out=base1024[:], in0=basecol[:], scalar1=1024.0, scalar2=None, op0=ALU.mult))
                    for c8 in range(8):
                        ch.add("vector", lambda e, c8=c8: e.tensor_scalar(
                            out=u32col[:], in0=iotacol[:], scalar1=base1024[:, 0:1],
                            scalar2=float(128 * c8), op0=ALU.add, op1=ALU.add))
                        for w_src, w_dst in ((w1f_i, d_w1g), (w2f_i, d_w2g)):
                            ch.add("gpsimd", lambda e, w_src=w_src: e.indirect_dma_start(
                                out=wgsb[:], out_offset=None, in_=w_src,
                                in_offset=bass.IndirectOffsetOnAxis(ap=u32col[:], axis=0)), 16)
                            ch.dma("sync", w_dst.ap()[b, k, c8], wgsb[:])
                for b in range(BPC):
                    for k in range(2):
                        s = b * 2 + k
                        ch.add("tensor", lambda e, s=s: e.matmul(
                            out=P0e[0:8, 0:1], lhsT=onesf[0:1, 0:8],
                            rhs=idxflat[0:1, s:s + 1], start=True, stop=True))
                        ch.add("vector", lambda e: e.tensor_scalar(
                            out=mcol[:], in0=iotacol[0:8, 0:1], scalar1=P0e[0:8, 0:1],
                            scalar2=None, op0=ALU.is_equal))
                        def b1mm(e):
                            e.matmul(out=P0e[0:1, 0:512], lhsT=mcol[:], rhs=b1fsb[:, 0:512],
                                     start=True, stop=True)
                            return e.matmul(out=P1e[0:1, 0:512], lhsT=mcol[:], rhs=b1fsb[:, 512:1024],
                                            start=True, stop=True)
                        ch.add("tensor", b1mm)
                        ch.add("scalar", lambda e: e.copy(out=brow[0:1, 0:512], in_=P0e[0:1, 0:512]))
                        ch.add("scalar", lambda e: e.copy(out=brow[0:1, 512:1024], in_=P1e[0:1, 0:512]))
                        ch.dma("sync", d_b1g.ap()[b, k].unsqueeze(0), brow[:])
                        def b2mm(e):
                            e.matmul(out=P0e[0:1, 0:512], lhsT=mcol[:], rhs=b2fsb[:, 0:512],
                                     start=True, stop=True)
                            return e.matmul(out=P1e[0:1, 0:512], lhsT=mcol[:], rhs=b2fsb[:, 512:1024],
                                            start=True, stop=True)
                        ch.add("tensor", b2mm)
                        if k == 0:
                            ch.add("scalar", lambda e: e.copy(out=b2acc[0:1, 0:512], in_=P0e[0:1, 0:512]))
                            ch.add("scalar", lambda e: e.copy(out=b2acc[0:1, 512:1024], in_=P1e[0:1, 0:512]))
                        else:
                            ch.add("vector", lambda e: e.tensor_tensor(
                                out=b2acc[0:1, 0:512], in0=b2acc[0:1, 0:512],
                                in1=P0e[0:1, 0:512], op=ALU.add))
                            ch.add("vector", lambda e: e.tensor_tensor(
                                out=b2acc[0:1, 512:1024], in0=b2acc[0:1, 512:1024],
                                in1=P1e[0:1, 0:512], op=ALU.add))
                    ch.dma("sync", d_b2s.ap()[b].unsqueeze(0), b2acc[:])
        _ph()
        def _ph():
            with ExitStack() as pf:
                ef = pf.enter_context
                wcsb = ef(nc.sbuf_tensor("F_wcsb", [128, 8, 2304], BF16))
                strips = [ef(nc.sbuf_tensor(f"F_strip{i}", [128, 8, 512], BF16)) for i in range(2)]
                egls = [ef(nc.sbuf_tensor(f"F_egl{i}", [128, 2176], F32)) for i in range(2)]
                rzs = [ef(nc.sbuf_tensor(f"F_rz{i}", [128, 128], F32)) for i in range(2)]
                gnts = [ef(nc.sbuf_tensor(f"F_gnt{i}", [128, 2432], BF16)) for i in range(2)]
                PG = ef(nc.psum_tensor("PG", [128, 2304], F32))
                ch.dma("gpsimd", wcsb[:], wc_i)
                for s_ in agh_sems:
                    _collective_wait(ch, s_)
                cp = ChainP()
                pg_free = None
                for tt in range(512):
                    b, tl = tt // 16, tt % 16
                    tno = cp.begin_tile()
                    cp.wait_tile_end(tno - 3)
                    strip = strips[(tt // 4) % 2]
                    if tt % 4 == 0:
                        g = tt // 4
                        r, bp, t0 = g // 16, (g // 4) % 4, (g % 4) * 512
                        cp.wait_tile_end(4 * (g - 2) + 3)
                        def sdma(e, r=r, bp=bp, t0=t0, strip=strip):
                            ap = bass.AP(agh_outB[bp], r * D * T + t0,
                                         [[T, 128], [128 * T, 8], [1, 512]])
                            return e.dma_start(out=strip[:], in_=ap)
                        cp.add("sync", sdma, 16)
                    egl = egls[tt % 2]
                    rz = rzs[tt % 2]
                    gnt = gnts[tt % 2]
                    cp.wait_tile_end(tno - 2)
                    cp.wait_mark(pg_free)
                    ts = (tt % 4) * 128
                    def gmm(e, ts=ts, strip=strip):
                        for dc in range(8):
                            for p5 in range(5):
                                off = p5 * 512
                                sz = 256 if p5 == 4 else 512
                                r = e.matmul(out=PG[:, off:off + sz],
                                             lhsT=strip[:, dc, ts:ts + 128],
                                             rhs=wcsb[:, dc, off:off + sz],
                                             start=(dc == 0), stop=(dc == 7))
                        return r
                    cp.add("tensor", gmm)
                    pg_g = bass.AP(PG, 0, [[2304, 128], [288, 8], [16, 17], [1, 16]])
                    cp.add("scalar", lambda e, pg_g=pg_g, egl=egl: e.activation(
                        out=egl[:].rearrange("p (h c i) -> p h c i", h=8, c=17),
                        in_=pg_g, func=AF.Exp))
                    pg_v = bass.AP(PG, 272, [[2304, 128], [288, 8], [1, 16]])
                    gnt_vhi = bass.AP(gnt[:].tensor, 272, [[2432, 128], [304, 8], [1, 16]])
                    gnt_vlo = bass.AP(gnt[:].tensor, 288, [[2432, 128], [304, 8], [1, 16]])
                    cp.add("scalar", lambda e, pg_v=pg_v, gnt_vhi=gnt_vhi: e.copy(
                        out=gnt_vhi, in_=pg_v))
                    cp.add("vector", lambda e, pg_v=pg_v, gnt_vhi=gnt_vhi, gnt_vlo=gnt_vlo:
                           e.tensor_tensor(out=gnt_vlo, in0=pg_v, in1=gnt_vhi, op=ALU.subtract))
                    pg_free = cp.mark()
                    cp.add("vector", lambda e, egl=egl, rz=rz: e.tensor_reduce(
                        out=rz[:].rearrange("p (h i) -> p h i", h=8),
                        in_=bass.AP(egl[:].tensor, 0, [[2176, 128], [272, 8], [1, 16], [16, 17]]),
                        axis=AX.X, op=ALU.add))
                    cp.add("vector", lambda e, rz=rz: e.reciprocal(out=rz[:], in_=rz[:]))
                    gnt_A = bass.AP(gnt[:].tensor, 0, [[2432, 128], [304, 8], [16, 17], [1, 16]])
                    rz_b = bass.AP(rz[:].tensor, 0, [[128, 128], [16, 8], [0, 17], [1, 16]])
                    egl_v = bass.AP(egl[:].tensor, 0, [[2176, 128], [272, 8], [16, 17], [1, 16]])
                    cp.add("vector", lambda e, gnt_A=gnt_A, egl_v=egl_v, rz_b=rz_b: e.tensor_tensor(
                        out=gnt_A, in0=egl_v, in1=rz_b, op=ALU.mult))
                    def gdma(e, b=b, tl=tl, gnt=gnt):
                        TH = T // 2
                        half = tl // 8
                        tl2 = tl % 8
                        ap = bass.AP(d_gseqH[half], b * TH * 304 + tl2 * 128 * 304,
                                     [[304, 128], [32 * TH * 304, 8], [1, 304]])
                        return e.dma_start(out=ap, in_=gnt[:].rearrange("p (h x) -> p h x", h=8))
                    cp.add("sync", gdma, 16)
                cp.emit(ch, nc, "Fp")
        _ph()
        def _ph():
            with ExitStack() as pg_:
                eg = pg_.enter_context
                CH = 64
                NCH = T // CH
                gb = eg(nc.sbuf_tensor("G_gb", [128, 2, CH, 304], BF16))
                ub = eg(nc.sbuf_tensor("G_ub", [128, 2, 16, CH], F32))
                ysb = [eg(nc.sbuf_tensor(f"G_ysb{i}", [128, 2, 16, CH], F32)) for i in range(2)]
                ysbf = eg(nc.sbuf_tensor("G_ysbf", [128, 2, 16, CH], BF16))
                st0 = eg(nc.sbuf_tensor("G_st0", [128, 2, 16], F32))
                tmpc = eg(nc.sbuf_tensor("G_tmpc", [128, 16, 2, 16, 17], F32))
                hB = 2 * CH * 304
                uB = 2 * 16 * CH
                gT = None
                cp = ChainP()
                t0_ = cp.begin_tile()
                cp.add("vector", lambda e: e.memset(st0[:], 0.0))
                last_tile = t0_
                for ci in range(NCH):
                    ys = ysb[ci % 2]
                    ysp = ysb[(ci + 1) % 2]
                    tld = cp.begin_tile()
                    cp.wait_tile_end(last_tile)
                    def ldg(e, ci=ci):
                        TH = T // 2
                        half = (ci * CH) // TH
                        ci2 = ci - half * (TH // CH)
                        ap = bass.AP(d_gseqH[half], ci2 * CH * 304,
                                     [[2 * TH * 304, 128], [TH * 304, 2], [304, CH], [1, 304]])
                        return e.dma_start(out=gb[:], in_=ap)
                    cp.add("sync", ldg, 16)
                    hT_ = gb[:].tensor
                    for hp in range(2):
                        cp.begin_tile()
                        cp.wait_tile_end(tld)
                        vhi = bass.AP(hT_, hp * CH * 304 + 272,
                                      [[hB, 128], [1, 16], [304, CH]])
                        vlo = bass.AP(hT_, hp * CH * 304 + 288,
                                      [[hB, 128], [1, 16], [304, CH]])
                        a0 = bass.AP(hT_, hp * CH * 304 + 256,
                                     [[hB, 128], [1, 16], [304, CH]])
                        ubs = bass.AP(ub[:].tensor, hp * 16 * CH,
                                      [[uB, 128], [CH, 16], [1, CH]])
                        cp.add("vector", lambda e, vhi=vhi, vlo=vlo, ubs=ubs: e.tensor_tensor(
                            out=ubs, in0=vhi, in1=vlo, op=ALU.add))
                        cp.add("vector", lambda e, a0=a0, ubs=ubs: e.tensor_tensor(
                            out=ubs, in0=ubs, in1=a0, op=ALU.mult))
                    for t in range(CH):
                        for hp in range(2):
                            tno = cp.begin_tile()
                            cp.wait_tile_end(tno - 2)
                            cp.wait_tile_end(tno - 3)
                            t16 = t % 16
                            if t16 == 0:
                                usl = bass.AP(ub[:].tensor, hp * 16 * CH + t,
                                              [[uB, 128], [CH, 16], [1, 16]])
                                ucol = bass.AP(tmpc[:].tensor, hp * 272 + 16,
                                               [[16 * 544, 128], [17, 16], [544, 16]])
                                cp.add("vector", lambda e, usl=usl, ucol=ucol: e.tensor_copy(
                                    out=ucol, in_=usl))
                            if ci == 0 and t == 0:
                                sb = bass.AP(st0[:].tensor, hp * 16,
                                             [[32, 128], [0, 16], [1, 16]])
                            else:
                                yprev = ys if t > 0 else ysp
                                tp = t - 1 if t > 0 else CH - 1
                                sb = bass.AP(yprev[:].tensor, hp * 16 * CH + tp,
                                             [[uB, 128], [0, 16], [CH, 16]])
                            A = bass.AP(hT_, hp * CH * 304 + t * 304,
                                        [[hB, 128], [1, 16], [16, 16]])
                            tout = bass.AP(tmpc[:].tensor, t16 * 544 + hp * 272,
                                           [[16 * 544, 128], [17, 16], [1, 16]])
                            cp.add("vector", lambda e, A=A, sb=sb, tout=tout: e.tensor_tensor(
                                out=tout, in0=A, in1=sb, op=ALU.mult))
                            tin = bass.AP(tmpc[:].tensor, t16 * 544 + hp * 272,
                                          [[16 * 544, 128], [17, 16], [1, 17]])
                            yout = bass.AP(ys[:].tensor, hp * 16 * CH + t,
                                           [[uB, 128], [CH, 16]])
                            cp.add("vector", lambda e, tin=tin, yout=yout: e.tensor_reduce(
                                out=yout, in_=tin, axis=AX.X, op=ALU.add))
                    tst = cp.begin_tile()
                    cp.wait_tile_end(tst - 1)
                    cp.wait_tile_end(tst - 2)
                    cp.add("vector", lambda e, ys=ys: e.tensor_copy(out=ysbf[:], in_=ys[:]))
                    def sty(e, ci=ci):
                        ap = bass.AP(d_ys, ci * CH,
                                     [[2 * 16 * T, 128], [16 * T, 2], [T, 16], [1, CH]])
                        return e.dma_start(out=ap, in_=ysbf[:])
                    cp.add("sync", sty, 16)
                    last_tile = tst
                cp.emit(ch, nc, "Gp")
                for r_ in range(8):
                    for hh_ in range(8):
                        def rshp(e, r_=r_, hh_=hh_):
                            src_ap = bass.AP(d_ys, (hh_ * 32 + 4 * r_) * 16 * T,
                                             [[T, 16], [16 * T, 4], [1, T]])
                            dst_ap = bass.AP(a2y_in, r_ * 8 * 16 * 4 * T + hh_ * 16 * 4 * T,
                                             [[4 * T, 16], [T, 4], [1, T]])
                            return e.dma_start(out=dst_ap, in_=src_ap)
                        ch.add("sync", rshp, 16)
                _collective(ch, "AllToAll", ALU.bypass, [a2y_in.ap()], [a2y_out.ap()],
                            "ccy", nc)
        _ph()
        def _ph():
            with ExitStack() as ph:
                eh = ph.enter_context
                xn1T = eh(nc.sbuf_tensor("H_xn1T", [128, 8, T], BF16))
                ysTs = [eh(nc.sbuf_tensor(f"H_ysT{i}", [128, 8, 128], BF16)) for i in range(2)]
                woutsb = eh(nc.sbuf_tensor("H_woutsb", [128, 8, D], BF16))
                h1T = eh(nc.sbuf_tensor("H_h1T", [128, 2, 8, T], BF16))
                w2sb = eh(nc.sbuf_tensor("H_w2sb", [128, 2, 8, D], BF16))
                w1strips = [eh(nc.sbuf_tensor(f"H_w1strip{i}", [128, 8, 128], BF16)) for i in range(2)]
                b1rows = eh(nc.sbuf_tensor("H_b1rows", [1, 2, D], BF16))
                b2sum = eh(nc.sbuf_tensor("H_b2sum", [1, D], F32))
                b2rep = eh(nc.sbuf_tensor("H_b2rep", [128, D], F32))
                x1ts = [eh(nc.sbuf_tensor(f"H_x1t{i}", [128, D], F32)) for i in range(2)]
                sgs = [eh(nc.sbuf_tensor(f"H_sg{i}", [128, D], F32)) for i in range(2)]
                g1ts = [eh(nc.sbuf_tensor(f"H_g1t{i}", [128, D], F32)) for i in range(2)]
                PS = [eh(nc.psum_tensor(f"HPS{i}", [128, 512], F32)) for i in range(2)]
                PO = [eh(nc.psum_tensor(f"HPO{i}", [128, 512], F32)) for i in range(2)]
                ch.dma("gpsimd", woutsb[:], woutT_i)
                cp = ChainP()
                prev_b_end = -1
                for b in range(BPC):
                    t_ld = cp.begin_tile()
                    cp.wait_tile_end(prev_b_end)
                    def lds(e, sem, b=b):
                        e.dma_start(out=xn1T[:], in_=d_xn1T.ap()[b]).then_inc(sem, 16)
                        ap = bass.AP(d_w2g, b * 2 * 8 * 128 * D,
                                     [[D, 128], [8 * 128 * D, 2], [128 * D, 8], [1, D]])
                        e.dma_start(out=w2sb[:], in_=ap).then_inc(sem, 16)
                        e.dma_start(out=b1rows[:], in_=d_b1g.ap()[b].unsqueeze(0)).then_inc(sem, 16)
                        return e.dma_start(out=b2sum[:], in_=d_b2s.ap()[b].unsqueeze(0)).then_inc(sem, 16)
                    cp.add("gpsimd", lds, 64, raw=True)
                    for fh in range(2):
                        cp.add("tensor", lambda e, fh=fh: e.matmul(
                            out=PS[0][:, 0:512], lhsT=onesf[0:1, 0:128],
                            rhs=b2sum[0:1, fh * 512:(fh + 1) * 512], start=True, stop=True))
                        cp.add("scalar", lambda e, fh=fh: e.copy(
                            out=b2rep[:, fh * 512:(fh + 1) * 512], in_=PS[0][:, 0:512]))
                    kt = 0
                    for k in range(2):
                        for ffc in range(8):
                            tno = cp.begin_tile()
                            cp.wait_tile_end(tno - 2)
                            cp.wait_tile_end(tno - 3)
                            if kt == 0:
                                cp.wait_tile_end(t_ld)
                            ws = w1strips[kt % 2]
                            def ldw1(e, b=b, k=k, ffc=ffc, ws=ws):
                                ap = bass.AP(d_w1g, ((b * 2 + k) * 8) * 128 * D + ffc * 128,
                                             [[D, 128], [128 * D, 8], [1, 128]])
                                return e.dma_start(out=ws[:], in_=ap)
                            cp.add("sync", ldw1, 16)
                            for tc in range(4):
                                ps = (PS if kt % 2 == 0 else PO)[tc % 2]
                                def hmm_(e, tc=tc, k=k, ffc=ffc, ws=ws, ps=ps):
                                    for dc in range(8):
                                        e.matmul(out=ps[:, 0:512], lhsT=ws[:, dc, :],
                                                 rhs=xn1T[:, dc, tc * 512:(tc + 1) * 512],
                                                 start=(dc == 0), stop=False)
                                    return e.matmul(out=ps[:, 0:512],
                                                    lhsT=b1rows[0:1, k, ffc * 128:(ffc + 1) * 128],
                                                    rhs=onesbf[0:1, 0:512], start=False, stop=True)
                                cp.add("tensor", hmm_)
                                cp.add("scalar", lambda e, k=k, ffc=ffc, tc=tc, ps=ps: e.activation(
                                    out=h1T[:, k, ffc, tc * 512:(tc + 1) * 512], in_=ps[:, 0:512],
                                    func=AF.Gelu))
                            kt += 1
                            last_h1 = tno
                    for tt in range(16):
                        tno = cp.begin_tile()
                        cp.wait_tile_end(tno - 2)
                        cp.wait_tile_end(tno - 3)
                        if tt == 0:
                            cp.wait_tile_end(last_h1)
                        si = tt % 2
                        ps, po = PS[si], PO[si]
                        x1t, sg, g1t = x1ts[si], sgs[si], g1ts[si]
                        ysT = ysTs[si]
                        def ldy(e, b=b, tt=tt, ysT=ysT):
                            ap2 = bass.AP(a2y_out, b * T + tt * 128,
                                          [[4 * T, 128], [8 * 16 * 4 * T, 8], [1, 128]])
                            return e.dma_start(out=ysT[:], in_=ap2)
                        cp.add("sync", ldy, 16)
                        def smm2(e, tt=tt, ps=ps, po=po, ysT=ysT):
                            for dc in range(8):
                                e.matmul(out=ps[:, 0:512],
                                         lhsT=ysT[:, dc, :],
                                         rhs=woutsb[:, dc, 0:512],
                                         start=(dc == 0), stop=(dc == 7))
                            for dc in range(8):
                                r = e.matmul(out=po[:, 0:512],
                                             lhsT=ysT[:, dc, :],
                                             rhs=woutsb[:, dc, 512:1024],
                                             start=(dc == 0), stop=(dc == 7))
                            return r
                        cp.add("tensor", smm2)
                        def lds2(e, sem, b=b, tt=tt, g1t=g1t, x1t=x1t):
                            e.dma_start(out=g1t[:], in_=d_xn1.ap()[b, tt]).then_inc(sem, 16)
                            return e.dma_start(out=x1t[:], in_=d_x1.ap()[b, tt]).then_inc(sem, 16)
                        cp.add("sync", lds2, 32, raw=True)
                        cp.add("vector", lambda e, sg=sg, g1t=g1t, ps=ps: e.tensor_tensor(
                            out=sg[:, 0:512], in0=ps[:, 0:512], in1=g1t[:, 0:512], op=ALU.add))
                        cp.add("vector", lambda e, sg=sg, g1t=g1t, po=po: e.tensor_tensor(
                            out=sg[:, 512:1024], in0=po[:, 0:512], in1=g1t[:, 512:1024], op=ALU.add))
                        def omm2(e, tt=tt, ps=ps, po=po):
                            for fh in range(2):
                                dst = ps if fh == 0 else po
                                n = 0
                                for k in range(2):
                                    for ffc in range(8):
                                        r = e.matmul(out=dst[:, 0:512],
                                                     lhsT=h1T[:, k, ffc, tt * 128:(tt + 1) * 128],
                                                     rhs=w2sb[:, k, ffc, fh * 512:(fh + 1) * 512],
                                                     start=(n == 0), stop=(n == 15))
                                        n += 1
                            return r
                        cp.add("tensor", omm2)
                        cp.add("vector", lambda e, g1t=g1t, ps=ps: e.tensor_tensor(
                            out=g1t[:, 0:512], in0=ps[:, 0:512],
                            in1=b2rep[:, 0:512], op=ALU.add))
                        cp.add("vector", lambda e, g1t=g1t, po=po: e.tensor_tensor(
                            out=g1t[:, 512:1024], in0=po[:, 0:512],
                            in1=b2rep[:, 512:1024], op=ALU.add))
                        cp.add("scalar", lambda e, g1t=g1t: e.activation(out=g1t[:], in_=g1t[:], func=AF.Silu, scale=0.5))
                        cp.add("vector", lambda e, sg=sg, g1t=g1t: e.tensor_tensor(out=sg[:], in0=sg[:], in1=g1t[:], op=ALU.mult))
                        cp.add("vector", lambda e, x1t=x1t, sg=sg: e.tensor_tensor(out=x1t[:], in0=x1t[:], in1=sg[:], op=ALU.add))
                        cp.add("sync", lambda e, b=b, tt=tt, x1t=x1t: e.dma_start(
                            out=out_o[b, tt], in_=x1t[:]), 16)
                        prev_b_end = tno
                cp.emit(ch, nc, "Hp")
        _ph()
        ch.build(nc)
    return nc


def kernel_fused(**inputs):
    nc = _NC_CACHE.setdefault("fused", build_fused())
    x = np.asarray(inputs["L0_attn_x"], np.float32)
    W_attn = inputs["L0_attn_W_attn"]
    W_proj = inputs["L0_attn_W_proj"]
    top_idx = np.asarray(inputs["L0_ffn_top_idx"])
    cw = inputs["L0_ffn_combine_weights"]
    W1 = inputs["L0_ffn_W1"]; b1 = inputs["L0_ffn_b1"]
    W2 = inputs["L0_ffn_W2"]; b2 = inputs["L0_ffn_b2"]
    nw = inputs["L1_attn_norm_w"]; W_v = inputs["L1_attn_W_v"]
    W_a = inputs["L1_attn_W_a"]; W_out = inputs["L1_attn_W_out"]
    Wr = inputs["L1_ffn_Wr"]; br = inputs["L1_ffn_br"]
    W1f = inputs["L1_ffn_W1"]; b1f = inputs["L1_ffn_b1"]
    W2f = inputs["L1_ffn_W2"]; b2f = inputs["L1_ffn_b2"]

    tok = top_idx.T.astype(np.int32)
    xf = x.reshape(B * T, D)
    wqk = np.ascontiguousarray(W_attn[:, :2048].reshape(8, 128, 16, 128).transpose(2, 1, 0, 3))
    wv = np.ascontiguousarray(W_attn[:, 2048:].reshape(8, 128, D).transpose(1, 0, 2))
    wp = np.ascontiguousarray(W_proj.reshape(8, 128, D).transpose(1, 0, 2))
    eyes1 = np.ascontiguousarray(np.stack(
        [c * np.eye(64, dtype=np.float32) for c in (7.0, 15.0, 13.0)]).transpose(1, 0, 2))
    eyes = np.concatenate([eyes1, eyes1], axis=0)
    idxi = np.ascontiguousarray(tok.reshape(8, 16, 128).transpose(2, 0, 1))
    wr_r = np.ascontiguousarray(Wr.reshape(8, 128, 8).transpose(1, 0, 2))
    woutT = np.ascontiguousarray(W_out.T.reshape(8, 128, D).transpose(1, 0, 2))
    Wa_r = W_a.reshape(H1, M1, M1 + 1, D)
    in_maps = []
    for c in range(NCORES):
        wc_full = np.empty((D, 8, 18, 16), np.float32)
        for hh in range(8):
            hg = 8 * c + hh
            blk = Wa_r[hg]
            wc_full[:, hh, 0:16, :] = blk.transpose(2, 1, 0)[:, 1:17, :]
            wc_full[:, hh, 16, :] = blk.transpose(2, 1, 0)[:, 0, :]
            wc_full[:, hh, 17, :] = W_v[hg * 16:(hg + 1) * 16].T
        wcr = np.ascontiguousarray(
            wc_full.reshape(D, 2304).reshape(8, 128, 2304).transpose(1, 0, 2))
        in_maps.append({
            "x": np.ascontiguousarray(x.reshape(NCORES, BPC * T // 128, 128, D)[c]),
            "gxr": np.ascontiguousarray(xf[tok[c]].reshape(16, 128, D)),
            "wqk": wqk, "wv": wv, "wp": wp,
            "w1": np.ascontiguousarray(W1[c].reshape(8, 128, 32, 128).transpose(1, 2, 0, 3)),
            "w2": np.ascontiguousarray(W2[c].reshape(32, 128, 8, 128).transpose(2, 1, 0, 3)),
            "b1": np.ascontiguousarray(b1[c].reshape(32, 128).T),
            "b2": np.ascontiguousarray(b2[c].reshape(8, 128).T),
            "wvec": np.ascontiguousarray(cw[tok[c], c].reshape(1, 2048)),
            "idxi": idxi, "nw": nw.reshape(1, D), "wr": wr_r, "br": br.reshape(1, 8),
            "wc": wcr, "woutT": woutT,
            "w1f": W1f.reshape(8192, D), "w2f": W2f.reshape(8192, D),
            "b1f": b1f, "b2f": b2f,
            "eyes": eyes, "eye64": np.concatenate([np.eye(64, dtype=np.float32)] * 2, axis=0),
            "onesf": np.ones((128, 128), np.float32),
            "onesbf": _bf(np.ones((128, 512), np.float32)),
            "iota1024": np.broadcast_to(np.arange(1024, dtype=np.float32), (128, 1024)).copy(),
            "iotacol": np.arange(128, dtype=np.float32).reshape(128, 1),
            "rbase": np.full((128, 1), 1024.0 * c, np.float32),
            "iscore": np.full((128, 1), 1.0 if c == 0 else 0.0, np.float32),
            "eye4r": np.eye(BPC, dtype=np.float32).reshape(1, BPC * BPC),
            "iota8": np.broadcast_to(np.arange(8, dtype=np.float32), (128, 8)).copy(),
        })
    res = _run(nc, in_maps)
    out = np.stack([r["out"] for r in res.results])
    return out.reshape(B, T, D)


kernel = kernel_fused


# version 08
# speedup vs baseline: 1.0145x; 1.0145x over previous
import numpy as np
import ml_dtypes
from contextlib import ExitStack

import concourse.bass as bass
import concourse.mybir as mybir
from concourse.bass_utils import run_bass_kernel_spmd

F32 = mybir.dt.float32
BF16 = mybir.dt.bfloat16
I32 = mybir.dt.int32
AF = mybir.ActivationFunctionType
ALU = mybir.AluOpType
AX = mybir.AxisListType

NCORES = 8
B, T, D = 32, 2048, 1024
BPC = B // NCORES
NH, HS, NL = 16, 64, 64
H1, M1 = 64, 16
EPS = 1e-5

_NC_CACHE = {}


def _bf(x):
    return np.ascontiguousarray(x).astype(ml_dtypes.bfloat16)


class Chain:

    def __init__(self):
        self.ops = []

    def add(self, engine, fn, inc=1):
        self.ops.append((engine, fn, inc))

    def dma(self, engine, out, in_, **kw):
        self.add(engine, lambda e, o=out, i=in_, kw=kw: e.dma_start(out=o, in_=i, **kw), 16)

    def build(self, nc):
        pre = []
        c = 0
        for _, _, inc in self.ops:
            pre.append(c)
            c += inc
        with nc.semaphore("chain") as sem, nc.Block() as block:
            for engname in ("sync", "scalar", "vector", "tensor", "gpsimd"):
                idxs = [i for i, (e, _, _) in enumerate(self.ops) if e == engname]
                if not idxs:
                    continue

                def body(eng, idxs=idxs):
                    for i in idxs:
                        _, fn, inc = self.ops[i]
                        if pre[i] > 0:
                            eng.wait_ge(sem, pre[i])
                        last = fn(eng)
                        last.then_inc(sem, inc)

                getattr(block, engname)(body)
            total = c

            def tail(eng):
                eng.wait_ge(sem, total)
            block.sync(tail)


def _run(nc, in_maps):
    from concourse.aot_env import allow_ici_all_to_all
    with allow_ici_all_to_all():
        return run_bass_kernel_spmd(nc, in_maps, core_ids=list(range(NCORES)))



def _ln_tile(ch, v_eng, s_eng, xt, stats, xnb_out, n=D, extra_outs=()):
    nhalf = n // 2
    ch.add("vector", lambda e: e.bn_stats(out=stats[:, 0:6], in_=xt[:, 0:nhalf]))
    ch.add("vector", lambda e: e.bn_stats(out=stats[:, 6:12], in_=xt[:, nhalf:n]))
    ch.add("vector", lambda e: e.bn_aggr(out=stats[:, 12:14], in_=stats[:, 0:12]))
    ch.add("vector", lambda e: e.tensor_scalar(out=stats[:, 13:14], in0=stats[:, 13:14],
                                               scalar1=EPS, scalar2=None, op0=ALU.add))
    ch.add("scalar", lambda e: e.activation(out=stats[:, 14:15], in_=stats[:, 13:14],
                                            func=AF.Sqrt))
    ch.add("vector", lambda e: e.reciprocal(out=stats[:, 15:16], in_=stats[:, 14:15]))
    ch.add("vector", lambda e: e.tensor_scalar(out=stats[:, 12:13], in0=stats[:, 12:13],
                                               scalar1=-1.0, scalar2=None, op0=ALU.mult))
    for outap in (xnb_out,) + tuple(extra_outs):
        ch.add("vector", lambda e, o=outap: e.tensor_scalar(
            out=o, in0=xt, scalar1=stats[:, 12:13], scalar2=stats[:, 15:16],
            op0=ALU.add, op1=ALU.mult))


def build_ln0():
    nc = bass.Bass("TRN2", target_bir_lowering=False)
    x = nc.dram_tensor("x", [BPC * T // 128, 128, D], F32, kind="ExternalInput").ap()
    xn0 = nc.dram_tensor("xn0", [BPC * T // 128, 128, D], BF16, kind="ExternalOutput").ap()
    ntiles = BPC * T // 128
    with ExitStack() as ctx:
        e = ctx.enter_context
        xt = e(nc.sbuf_tensor("xt", [128, D], F32))
        xnb = e(nc.sbuf_tensor("xnb", [128, D], BF16))
        stats = e(nc.sbuf_tensor("stats", [128, 16], F32))
        ch = Chain()
        for i in range(ntiles):
            ch.dma("sync", xt[:], x[i])
            _ln_tile(ch, None, None, xt[:], stats, xnb[:])
            ch.dma("sync", xn0[i], xnb[:])
        ch.build(nc)
    return nc


def stage_ln0(x_full):
    nc = _NC_CACHE.setdefault("ln0", build_ln0())
    shards = x_full.reshape(NCORES, BPC * T // 128, 128, D)
    res = _run(nc, [{"x": np.ascontiguousarray(shards[c])} for c in range(NCORES)])
    out = np.stack([r["xn0"] for r in res.results])
    return out.reshape(B, T, D)



def build_attn0():
    nc = bass.Bass("TRN2", target_bir_lowering=False)
    xn0T = nc.dram_tensor("xn0T", [BPC, 128, 8, T], BF16, kind="ExternalInput").ap()
    wqk = nc.dram_tensor("wqk", [16, 128, 8, 128], F32, kind="ExternalInput").ap()
    wv = nc.dram_tensor("wv", [128, 8, D], F32, kind="ExternalInput").ap()
    wp = nc.dram_tensor("wp", [128, 8, D], F32, kind="ExternalInput").ap()
    eyes = nc.dram_tensor("eyes", [64, 3, 64], F32, kind="ExternalInput").ap()
    eye64 = nc.dram_tensor("eye64", [64, 64], F32, kind="ExternalInput").ap()
    onesf_i = nc.dram_tensor("onesf", [128, 128], F32, kind="ExternalInput").ap()
    onesbf_i = nc.dram_tensor("onesbf", [128, 8], BF16, kind="ExternalInput").ap()
    s0_o = nc.dram_tensor("s0", [BPC, 16, 128, D], F32, kind="ExternalOutput").ap()

    with ExitStack() as ctx:
        e = ctx.enter_context
        qT = e(nc.sbuf_tensor("qT", [128, 8, T], BF16))
        kT = e(nc.sbuf_tensor("kT", [128, 8, T], BF16))
        vb = e(nc.sbuf_tensor("vb", [128, 16, D], BF16))
        yT = e(nc.sbuf_tensor("yT", [128, 8, T], BF16))
        xb = e(nc.sbuf_tensor("xb", [128, 8, T], BF16))
        wres = e(nc.sbuf_tensor("wres", [128, 8, D], BF16))
        wstrip = e(nc.sbuf_tensor("wstrip", [128, 8, 128], BF16))
        expNT = e(nc.sbuf_tensor("expNT", [128, 16, 64], BF16))
        expLT = e(nc.sbuf_tensor("expLT", [64, T], F32))
        repsb = e(nc.sbuf_tensor("repsb", [128, 512], F32))
        rzlsb = e(nc.sbuf_tensor("rzlsb", [128, 512], F32))
        eyesb = e(nc.sbuf_tensor("eyesb", [64, 3, 64], F32))
        eye64b = e(nc.sbuf_tensor("eye64b", [64, 64], F32))
        onesf = e(nc.sbuf_tensor("onesfb", [128, 128], F32))
        onesbf = e(nc.sbuf_tensor("onesbfb", [128, 8], BF16))
        sc_ql = e(nc.sbuf_tensor("sc_ql", [128, 64], F32))
        sc_kl = e(nc.sbuf_tensor("sc_kl", [128, 64], F32))
        sc_qlbf = e(nc.sbuf_tensor("sc_qlbf", [128, 64], BF16))
        sc_klbf = e(nc.sbuf_tensor("sc_klbf", [128, 64], BF16))
        Ksb = e(nc.sbuf_tensor("Ksb", [64, 64], F32))
        Kt_sb = e(nc.sbuf_tensor("Kt_sb", [64, 64], F32))
        V_sb = e(nc.sbuf_tensor("V_sb", [64, 64], F32))
        W_sb = e(nc.sbuf_tensor("W_sb", [64, 64], F32))
        T_sb = e(nc.sbuf_tensor("T_sb", [64, 64], F32))
        KVt_sb = e(nc.sbuf_tensor("KVt_sb", [64, 64], F32))
        z1sb = e(nc.sbuf_tensor("z1sb", [64, 64], F32))
        z2aug = e(nc.sbuf_tensor("z2aug", [64, 65], F32))
        smalls = e(nc.sbuf_tensor("smalls", [64, 8], F32))
        s0t = e(nc.sbuf_tensor("s0t", [128, D], F32))
        P0 = e(nc.psum_tensor("P0", [128, 512], F32))
        P1 = e(nc.psum_tensor("P1", [128, 512], F32))
        P2 = e(nc.psum_tensor("P2", [128, 512], F32))
        PBIG = e(nc.psum_tensor("PBIG", [128, 1024], F32))

        ch = Chain()
        ch.dma("sync", eyesb[:], eyes)
        ch.dma("sync", eye64b[:], eye64)
        ch.dma("sync", onesf[:], onesf_i)
        ch.dma("sync", onesbf[:], onesbf_i)
        ch.add("vector", lambda e: e.memset(z2aug[:, 64:65], 1.0))

        def mm(ps, lhsT, rhs, start=True, stop=True):
            return lambda e: e.matmul(out=ps, lhsT=lhsT, rhs=rhs, start=start, stop=stop)

        for b in range(BPC):
            ch.dma("gpsimd", xb[:], xn0T[b])
            for part, dst in ((0, qT), (1, kT)):
                for cc in range(8):
                    ch.dma("gpsimd", wstrip[:], wqk[part * 8 + cc])
                    for tc in range(4):
                        def qmm(e, cc=cc, tc=tc, dst=dst):
                            for dc in range(8):
                                r = e.matmul(out=P0[:, 0:512],
                                             lhsT=wstrip[:, dc, :],
                                             rhs=xb[:, dc, tc * 512:(tc + 1) * 512],
                                             start=(dc == 0), stop=(dc == 7))
                            return r
                        ch.add("tensor", qmm)
                        ch.add("scalar", lambda e, cc=cc, tc=tc, dst=dst: e.copy(
                            out=dst[:, cc, tc * 512:(tc + 1) * 512], in_=P0[:, 0:512]))
            ch.dma("gpsimd", wres[:], wv)
            for tt in range(16):
                for fh in range(2):
                    def vmm(e, tt=tt, fh=fh):
                        for dc in range(8):
                            r = e.matmul(out=P0[:, 0:512],
                                         lhsT=xb[:, dc, tt * 128:(tt + 1) * 128],
                                         rhs=wres[:, dc, fh * 512:(fh + 1) * 512],
                                         start=(dc == 0), stop=(dc == 7))
                        return r
                    ch.add("tensor", vmm)
                    ch.add("scalar", lambda e, tt=tt, fh=fh: e.copy(
                        out=vb[:, tt, fh * 512:(fh + 1) * 512], in_=P0[:, 0:512]))

            for h in range(NH):
                bb = 64 * (h % 2)
                cc = h // 2
                qTh = qT[bb:bb + 64, cc, :]
                kTh = kT[bb:bb + 64, cc, :]
                ch.add("vector", lambda e, qTh=qTh, bb=bb: e.tensor_reduce(
                    out=sc_ql[bb:bb + 64, :], in_=qTh.rearrange("p (l g) -> p l g", g=32),
                    axis=AX.X, op=ALU.add))
                ch.add("vector", lambda e, kTh=kTh, bb=bb: e.tensor_reduce(
                    out=sc_kl[bb:bb + 64, :], in_=kTh.rearrange("p (l g) -> p l g", g=32),
                    axis=AX.X, op=ALU.add))
                ch.add("scalar", lambda e, bb=bb: e.copy(out=sc_qlbf[bb:bb + 64, :], in_=sc_ql[bb:bb + 64, :]))
                ch.add("scalar", lambda e, bb=bb: e.copy(out=sc_klbf[bb:bb + 64, :], in_=sc_kl[bb:bb + 64, :]))
                ch.add("tensor", mm(P0[0:64, 0:64], sc_ql[bb:bb + 64, :], sc_kl[bb:bb + 64, :]))
                ch.add("scalar", lambda e: e.activation(out=Ksb[:, :], in_=P0[0:64, 0:64],
                                                        func=AF.Exp, scale=1.0 / 1024.0))
                ch.add("vector", lambda e: e.tensor_reduce(out=smalls[:, 0:1], in_=Ksb[:, :],
                                                           axis=AX.X, op=ALU.add))
                ch.add("vector", lambda e: e.reciprocal(out=smalls[:, 1:2], in_=smalls[:, 0:1]))
                ch.add("vector", lambda e: e.tensor_scalar(out=Ksb[:, :], in0=Ksb[:, :],
                                                           scalar1=smalls[:, 1:2], scalar2=None, op0=ALU.mult))
                ch.add("tensor", mm(P1[0:1, 0:64], onesf[0:64, 0:1], Ksb[:, :]))
                ch.add("vector", lambda e: e.reduce_max(out=smalls[0:1, 2:3], in_=P1[0:1, 0:64], axis=AX.X))
                ch.add("vector", lambda e: e.reciprocal(out=smalls[0:1, 3:4], in_=smalls[0:1, 2:3]))
                ch.add("tensor", lambda e: e.matmul(out=P2[0:64, 0:1], lhsT=onesf[0:1, 0:64],
                                                    rhs=smalls[0:1, 3:4], start=True, stop=True))
                ch.add("tensor", lambda e: e.transpose(out=P0[0:64, 0:64], in_=Ksb[:, :], identity=eye64b[:]))
                ch.add("vector", lambda e: e.tensor_scalar(out=V_sb[:], in0=P0[0:64, 0:64],
                                                           scalar1=P2[0:64, 0:1], scalar2=None, op0=ALU.mult))
                ch.add("vector", lambda e: e.tensor_scalar(out=W_sb[:], in0=Ksb[:, :],
                                                           scalar1=P2[0:64, 0:1], scalar2=None, op0=ALU.mult))
                ch.add("scalar", lambda e: e.copy(out=Kt_sb[:], in_=P0[0:64, 0:64]))
                for _ in range(6):
                    ch.add("tensor", mm(P0[0:64, 0:64], Kt_sb[:], V_sb[:]))
                    ch.add("vector", lambda e: e.tensor_tensor(out=T_sb[:], in0=eyesb[:, 0, :],
                                                               in1=P0[0:64, 0:64], op=ALU.subtract))
                    ch.add("tensor", mm(P1[0:64, 0:64], V_sb[:], Kt_sb[:]))
                    ch.add("scalar", lambda e: e.copy(out=KVt_sb[:], in_=P1[0:64, 0:64]))
                    ch.add("tensor", mm(P0[0:64, 0:64], KVt_sb[:], T_sb[:]))
                    ch.add("vector", lambda e: e.tensor_tensor(out=T_sb[:], in0=eyesb[:, 1, :],
                                                               in1=P0[0:64, 0:64], op=ALU.subtract))
                    ch.add("tensor", mm(P0[0:64, 0:64], KVt_sb[:], T_sb[:]))
                    ch.add("vector", lambda e: e.tensor_tensor(out=T_sb[:], in0=eyesb[:, 2, :],
                                                               in1=P0[0:64, 0:64], op=ALU.subtract))
                    ch.add("tensor", mm(P1[0:64, 0:64], W_sb[:], T_sb[:]))
                    ch.add("tensor", mm(P2[0:64, 0:64], T_sb[:], W_sb[:]))
                    ch.add("scalar", lambda e: e.mul(V_sb[:], P1[0:64, 0:64], 0.25))
                    ch.add("scalar", lambda e: e.mul(W_sb[:], P2[0:64, 0:64], 0.25))
                for tc in range(16):
                    ch.add("tensor", lambda e, tc=tc, bb=bb, cc=cc: e.matmul(
                        out=P0[0:128, 0:64], lhsT=kT[bb:bb + 64, cc, tc * 128:(tc + 1) * 128],
                        rhs=sc_qlbf[bb:bb + 64, :], start=True, stop=True))
                    ch.add("scalar", lambda e, tc=tc: e.activation(
                        out=expNT[:, tc, :], in_=P0[0:128, 0:64], func=AF.Exp, scale=1.0 / 32.0))
                def z1mm(e, h=h):
                    for tc in range(16):
                        e.matmul(out=P2[0:64, 0:64], lhsT=expNT[:, tc, :],
                                 rhs=vb[:, tc, h * 64:(h + 1) * 64],
                                 start=(tc == 0), stop=(tc == 15))
                        r = e.matmul(out=P1[0:64, 0:1], lhsT=expNT[:, tc, :],
                                     rhs=onesbf[:, 0:1], start=(tc == 0), stop=(tc == 15))
                    return r
                ch.add("tensor", z1mm)
                ch.add("vector", lambda e: e.reciprocal(out=smalls[:, 4:5], in_=P1[0:64, 0:1]))
                ch.add("vector", lambda e: e.tensor_scalar(out=z1sb[:], in0=P2[0:64, 0:64],
                                                           scalar1=smalls[:, 4:5], scalar2=None, op0=ALU.mult))
                ch.add("tensor", mm(P0[0:64, 0:64], W_sb[:], z1sb[:]))
                ch.add("scalar", lambda e: e.copy(out=z2aug[:, 0:64], in_=P0[0:64, 0:64]))
                for tc in range(4):
                    ch.add("tensor", lambda e, tc=tc, bb=bb, cc=cc: e.matmul(
                        out=P1[0:64, 0:512], lhsT=sc_klbf[bb:bb + 64, :],
                        rhs=qT[bb:bb + 64, cc, tc * 512:(tc + 1) * 512], start=True, stop=True))
                    ch.add("scalar", lambda e, tc=tc: e.activation(
                        out=expLT[:, tc * 512:(tc + 1) * 512], in_=P1[0:64, 0:512],
                        func=AF.Exp, scale=1.0 / 32.0))
                for tc in range(4):
                    ch.add("tensor", lambda e, tc=tc: e.matmul(
                        out=PBIG[0:65, 0:512], lhsT=z2aug[:, :],
                        rhs=expLT[:, tc * 512:(tc + 1) * 512], start=True, stop=True))
                    ch.add("vector", lambda e: e.reciprocal(out=rzlsb[64:65, 0:512], in_=PBIG[64:65, 0:512]))
                    ch.add("tensor", lambda e: e.matmul(out=P2[0:64, 0:512], lhsT=onesf[64:65, 0:64],
                                                        rhs=rzlsb[64:65, 0:512], start=True, stop=True))
                    ch.add("scalar", lambda e: e.copy(out=repsb[0:64, 0:512], in_=P2[0:64, 0:512]))
                    ch.add("vector", lambda e, tc=tc, bb=bb, cc=cc: e.tensor_tensor(
                        out=yT[bb:bb + 64, cc, tc * 512:(tc + 1) * 512],
                        in0=PBIG[0:64, 0:512], in1=repsb[0:64, 0:512], op=ALU.mult))

            ch.dma("gpsimd", wres[:], wp)
            for tt in range(16):
                for fh in range(2):
                    def pmm(e, tt=tt, fh=fh):
                        for dc in range(8):
                            r = e.matmul(out=P0[:, 0:512],
                                         lhsT=yT[:, dc, tt * 128:(tt + 1) * 128],
                                         rhs=wres[:, dc, fh * 512:(fh + 1) * 512],
                                         start=(dc == 0), stop=(dc == 7))
                        return r
                    ch.add("tensor", pmm)
                    ch.add("scalar", lambda e, tt=tt, fh=fh: e.copy(
                        out=s0t[:, fh * 512:(fh + 1) * 512], in_=P0[:, 0:512]))
                ch.dma("sync", s0_o[b, tt], s0t[:])
        ch.build(nc)
    return nc


def stage_attn0(xn0, W_attn, W_proj):
    nc = _NC_CACHE.setdefault("attn0", build_attn0())
    wqk = np.ascontiguousarray(
        W_attn[:, :2048].reshape(8, 128, 16, 128).transpose(2, 1, 0, 3))
    wv = np.ascontiguousarray(W_attn[:, 2048:].reshape(8, 128, 1024).transpose(1, 0, 2))
    wp = np.ascontiguousarray(W_proj.reshape(8, 128, 1024).transpose(1, 0, 2))
    eyes1 = np.ascontiguousarray(np.stack(
        [c * np.eye(64, dtype=np.float32) for c in (7.0, 15.0, 13.0)]).transpose(1, 0, 2))
    eyes = np.concatenate([eyes1, eyes1], axis=0)
    eye64 = np.eye(64, dtype=np.float32)
    onesf = np.ones((128, 128), np.float32)
    onesbf = _bf(np.ones((128, 8), np.float32))
    in_maps = []
    for c in range(NCORES):
        xcT = np.ascontiguousarray(
            np.asarray(xn0[c * BPC:(c + 1) * BPC]).transpose(0, 2, 1))
        in_maps.append({
            "xn0T": np.ascontiguousarray(xcT.reshape(BPC, 8, 128, T).transpose(0, 2, 1, 3)),
            "wqk": wqk, "wv": wv, "wp": wp, "eyes": eyes, "eye64": eye64,
            "onesf": onesf, "onesbf": onesbf,
        })
    res = _run(nc, in_maps)
    s0 = np.stack([r["s0"] for r in res.results])
    return s0.reshape(B, T, D)



def build_ffn0():
    nc = bass.Bass("TRN2", target_bir_lowering=False)
    gxnT = nc.dram_tensor("gxnT", [128, 8, 2048], BF16, kind="ExternalInput").ap()
    w1 = nc.dram_tensor("w1", [128, 32, 8, 128], F32, kind="ExternalInput").ap()
    w2 = nc.dram_tensor("w2", [8, 128, 32, 128], F32, kind="ExternalInput").ap()
    b1i = nc.dram_tensor("b1", [128, 32], F32, kind="ExternalInput").ap()
    b2i = nc.dram_tensor("b2", [128, 8], F32, kind="ExternalInput").ap()
    wvec = nc.dram_tensor("wvec", [1, 2048], F32, kind="ExternalInput").ap()
    onesf_i = nc.dram_tensor("onesf", [128, 128], F32, kind="ExternalInput").ap()
    wT_o = nc.dram_tensor("wT", [8, 128, 2048], BF16, kind="ExternalOutput").ap()

    C_GELU = 0.7978845608028654
    with ExitStack() as ctx:
        e = ctx.enter_context
        gx = e(nc.sbuf_tensor("gx", [128, 8, 2048], BF16))
        w1sb = e(nc.sbuf_tensor("w1sb", [128, 32, 8, 128], BF16))
        w2sb = e(nc.sbuf_tensor("w2sb", [128, 32, 128], BF16))
        b1sb = e(nc.sbuf_tensor("b1sb", [128, 32], F32))
        b2sb = e(nc.sbuf_tensor("b2sb", [128, 8], F32))
        onesf = e(nc.sbuf_tensor("onesfb", [128, 128], F32))
        wrow = e(nc.sbuf_tensor("wrow", [1, 2048], F32))
        wrep = e(nc.sbuf_tensor("wrep", [128, 2048], F32))
        hTq = e(nc.sbuf_tensor("hTq", [128, 32, 512], BF16))
        ga = e(nc.sbuf_tensor("ga", [128, 512], F32))
        gb = e(nc.sbuf_tensor("gb", [128, 512], F32))
        gc = e(nc.sbuf_tensor("gc", [128, 512], F32))
        wout = e(nc.sbuf_tensor("wout", [128, 512], BF16))
        PH = e(nc.psum_tensor("PH", [128, 512], F32))
        PO = e(nc.psum_tensor("PO", [128, 512], F32))
        PR = e(nc.psum_tensor("PR", [128, 512], F32))

        ch = Chain()
        ch.dma("gpsimd", gx[:], gxnT)
        ch.dma("gpsimd", w1sb[:], w1)
        ch.dma("sync", b1sb[:], b1i)
        ch.dma("sync", b2sb[:], b2i)
        ch.dma("sync", onesf[:], onesf_i)
        ch.dma("sync", wrow[:], wvec)
        for tc in range(4):
            ch.add("tensor", lambda e, tc=tc: e.matmul(
                out=PR[:, 0:512], lhsT=onesf[0:1, 0:128],
                rhs=wrow[0:1, tc * 512:(tc + 1) * 512], start=True, stop=True))
            ch.add("scalar", lambda e, tc=tc: e.copy(
                out=wrep[:, tc * 512:(tc + 1) * 512], in_=PR[:, 0:512]))

        for tq in range(4):
            for ffc in range(32):
                def hmm_(e, tq=tq, ffc=ffc):
                    for dc in range(8):
                        r = e.matmul(out=PH[:, 0:512], lhsT=w1sb[:, ffc, dc, :],
                                     rhs=gx[:, dc, tq * 512:(tq + 1) * 512],
                                     start=(dc == 0), stop=(dc == 7))
                    return r
                ch.add("tensor", hmm_)
                ch.add("vector", lambda e, ffc=ffc: e.tensor_scalar(
                    out=ga[:], in0=PH[:, 0:512], scalar1=b1sb[:, ffc:ffc + 1],
                    scalar2=None, op0=ALU.add))
                ch.add("scalar", lambda e: e.activation(out=gb[:], in_=ga[:], func=AF.Square))
                ch.add("vector", lambda e: e.tensor_scalar(
                    out=gb[:], in0=gb[:], scalar1=0.044715, scalar2=1.0,
                    op0=ALU.mult, op1=ALU.add))
                ch.add("vector", lambda e: e.tensor_tensor(out=gc[:], in0=gb[:], in1=ga[:], op=ALU.mult))
                ch.add("scalar", lambda e: e.activation(out=gc[:], in_=gc[:], func=AF.Tanh, scale=C_GELU))
                ch.add("vector", lambda e: e.tensor_scalar(
                    out=gc[:], in0=gc[:], scalar1=1.0, scalar2=0.5, op0=ALU.add, op1=ALU.mult))
                ch.add("vector", lambda e, ffc=ffc: e.tensor_tensor(
                    out=hTq[:, ffc, :], in0=gc[:], in1=ga[:], op=ALU.mult))
            for dc in range(8):
                ch.dma("gpsimd", w2sb[:], w2[dc])
                def omm(e):
                    for ffc in range(32):
                        r = e.matmul(out=PO[:, 0:512], lhsT=w2sb[:, ffc, :],
                                     rhs=hTq[:, ffc, :], start=(ffc == 0), stop=(ffc == 31))
                    return r
                ch.add("tensor", omm)
                ch.add("vector", lambda e, dc=dc: e.tensor_scalar(
                    out=ga[:], in0=PO[:, 0:512], scalar1=b2sb[:, dc:dc + 1],
                    scalar2=None, op0=ALU.add))
                ch.add("vector", lambda e, tq=tq: e.tensor_tensor(
                    out=wout[:], in0=ga[:], in1=wrep[:, tq * 512:(tq + 1) * 512], op=ALU.mult))
                ch.dma("sync", wT_o[dc, :, tq * 512:(tq + 1) * 512], wout[:])
        ch.build(nc)
    return nc


def stage_ffn0(xn0_bf, top_idx, combine_weights, W1, b1, W2, b2):
    nc = _NC_CACHE.setdefault("ffn0", build_ffn0())
    tok = np.asarray(top_idx).T
    xf = np.asarray(xn0_bf).reshape(B * T, D)
    in_maps = []
    for c in range(NCORES):
        gathered = xf[tok[c]]
        gxnT = np.ascontiguousarray(gathered.T).reshape(8, 128, 2048).transpose(1, 0, 2)
        w1r = W1[c].reshape(8, 128, 32, 128).transpose(1, 2, 0, 3)
        w2r = W2[c].reshape(32, 128, 8, 128).transpose(2, 1, 0, 3)
        b1r = b1[c].reshape(32, 128).T
        b2r = b2[c].reshape(8, 128).T
        wv = combine_weights[tok[c], c].reshape(1, 2048)
        in_maps.append({
            "gxnT": np.ascontiguousarray(gxnT), "w1": np.ascontiguousarray(w1r),
            "w2": np.ascontiguousarray(w2r), "b1": np.ascontiguousarray(b1r),
            "b2": np.ascontiguousarray(b2r), "wvec": np.ascontiguousarray(wv),
            "onesf": np.ones((128, 128), np.float32),
        })
    res = _run(nc, in_maps)
    wT = np.stack([r["wT"] for r in res.results])
    weighted = wT.reshape(NCORES, 8, 128, 2048).transpose(0, 3, 1, 2).reshape(NCORES, 2048, 1024)
    return np.ascontiguousarray(weighted)



def build_scatter():
    nc = bass.Bass("TRN2", target_bir_lowering=False)
    wall = nc.dram_tensor("wall", [8, 16, 128, 1024], BF16, kind="ExternalInput").ap()
    idxi = nc.dram_tensor("idxi", [128, 8, 16], I32, kind="ExternalInput").ap()
    rbase = nc.dram_tensor("rbase", [128, 1], F32, kind="ExternalInput").ap()
    iota_i = nc.dram_tensor("iota", [128, 1024], F32, kind="ExternalInput").ap()
    osc = nc.dram_tensor("osc", [8, 128, 1024], F32, kind="ExternalOutput").ap()

    with ExitStack() as ctx:
        e = ctx.enter_context
        idxsb = e(nc.sbuf_tensor("idxsb", [128, 128], I32))
        idxf = e(nc.sbuf_tensor("idxf", [128, 128], F32))
        iotasb = e(nc.sbuf_tensor("iotasb", [128, 1024], F32))
        rbasesb = e(nc.sbuf_tensor("rbasesb", [128, 1], F32))
        ohsb = e(nc.sbuf_tensor("ohsb", [128, 128], BF16))
        wchunk = e(nc.sbuf_tensor("wchunk", [128, 1024], BF16))
        outsb = e(nc.sbuf_tensor("outsb", [128, 1024], F32))
        PR0 = e(nc.psum_tensor("PR0", [128, 512], F32))
        PR1 = e(nc.psum_tensor("PR1", [128, 512], F32))

        ch = Chain()
        ch.dma("sync", idxsb[:].rearrange("p (e s) -> p e s", e=8), idxi)
        ch.dma("sync", iotasb[:], iota_i)
        ch.dma("sync", rbasesb[:], rbase)
        ch.add("scalar", lambda e: e.copy(out=idxf[:], in_=idxsb[:]))
        ch.add("vector", lambda e: e.tensor_scalar(
            out=idxf[:], in0=idxf[:], scalar1=rbasesb[:, 0:1], scalar2=None,
            op0=ALU.subtract))
        for rt in range(8):
            first = True
            for ee in range(8):
                for sc in range(16):
                    ch.dma("gpsimd", wchunk[:], wall[ee, sc])
                    ch.add("vector", lambda e, rt=rt, ee=ee, sc=sc: e.tensor_scalar(
                        out=ohsb[:], in0=iotasb[:, rt * 128:(rt + 1) * 128],
                        scalar1=idxf[:, ee * 16 + sc:ee * 16 + sc + 1],
                        scalar2=None, op0=ALU.is_equal))
                    ch.add("tensor", lambda e, first=first: (
                        e.matmul(out=PR0[:, 0:512], lhsT=ohsb[:], rhs=wchunk[:, 0:512],
                                 start=first, stop=False),
                        e.matmul(out=PR1[:, 0:512], lhsT=ohsb[:], rhs=wchunk[:, 512:1024],
                                 start=first, stop=False))[-1])
                    first = False
            ch.add("scalar", lambda e: e.copy(out=outsb[:, 0:512], in_=PR0[:, 0:512]))
            ch.add("scalar", lambda e: e.copy(out=outsb[:, 512:1024], in_=PR1[:, 0:512]))
            ch.dma("sync", osc[rt], outsb[:])
        ch.build(nc)
    return nc


def stage_scatter(weighted, top_idx):
    nc = _NC_CACHE.setdefault("scatter", build_scatter())
    tok = np.asarray(top_idx).T.astype(np.int32)
    wall = np.ascontiguousarray(weighted.reshape(8, 16, 128, 1024))
    idxi = np.ascontiguousarray(tok.reshape(8, 16, 128).transpose(2, 0, 1))
    iota = np.broadcast_to(np.arange(1024, dtype=np.float32), (128, 1024)).copy()
    in_maps = []
    for c in range(NCORES):
        in_maps.append({
            "wall": wall, "idxi": idxi,
            "rbase": np.full((128, 1), 1024.0 * c, np.float32),
            "iota": iota,
        })
    res = _run(nc, in_maps)
    osc = np.stack([r["osc"] for r in res.results])
    return osc.reshape(8192, 1024)



def build_combine0():
    nc = bass.Bass("TRN2", target_bir_lowering=False)
    NTIL = BPC * T // 128
    x_i = nc.dram_tensor("x", [NTIL, 128, D], F32, kind="ExternalInput").ap()
    s0_i = nc.dram_tensor("s0", [NTIL, 128, D], F32, kind="ExternalInput").ap()
    g0_i = nc.dram_tensor("g0pre", [NTIL, 128, D], F32, kind="ExternalInput").ap()
    nw_i = nc.dram_tensor("nw", [1, D], F32, kind="ExternalInput").ap()
    wr_i = nc.dram_tensor("wr", [128, 8, 8], F32, kind="ExternalInput").ap()
    br_i = nc.dram_tensor("br", [1, 8], F32, kind="ExternalInput").ap()
    eye128 = nc.dram_tensor("eye128", [128, 128], F32, kind="ExternalInput").ap()
    eye4r_i = nc.dram_tensor("eye4r", [1, BPC * BPC], F32, kind="ExternalInput").ap()
    onesf_i = nc.dram_tensor("onesf", [128, 128], F32, kind="ExternalInput").ap()
    iota8_i = nc.dram_tensor("iota8", [128, 8], F32, kind="ExternalInput").ap()
    x1_o = nc.dram_tensor("x1", [NTIL, 128, D], F32, kind="ExternalOutput").ap()
    xn1_o = nc.dram_tensor("xn1", [NTIL, 128, D], F32, kind="ExternalOutput").ap()
    xn1b_o = nc.dram_tensor("xn1b", [NTIL, 128, D], BF16, kind="ExternalOutput").ap()
    hb_o = nc.dram_tensor("hb", [NTIL, 128, D], BF16, kind="ExternalOutput").ap()
    idx_o = nc.dram_tensor("idx", [BPC, 2], F32, kind="ExternalOutput").ap()

    with ExitStack() as ctx:
        e = ctx.enter_context
        xt = e(nc.sbuf_tensor("xt", [128, D], F32))
        s0t = e(nc.sbuf_tensor("s0t", [128, D], F32))
        g0t = e(nc.sbuf_tensor("g0t", [128, D], F32))
        x1t = e(nc.sbuf_tensor("x1t", [128, D], F32))
        xn1t = e(nc.sbuf_tensor("xn1t", [128, D], F32))
        xn1bt = e(nc.sbuf_tensor("xn1bt", [128, D], BF16))
        hbt = e(nc.sbuf_tensor("hbt", [128, D], BF16))
        sq = e(nc.sbuf_tensor("sq", [128, D], F32))
        stats = e(nc.sbuf_tensor("stats", [128, 16], F32))
        nwrep = e(nc.sbuf_tensor("nwrep", [128, D], F32))
        nwrow = e(nc.sbuf_tensor("nwrow", [1, D], F32))
        onesf = e(nc.sbuf_tensor("onesfb", [128, 128], F32))
        eyesb = e(nc.sbuf_tensor("eyesb", [128, 128], F32))
        eye4r = e(nc.sbuf_tensor("eye4rsb", [1, BPC * BPC], F32))
        wrsb = e(nc.sbuf_tensor("wrsb", [128, 8, 8], F32))
        brsb = e(nc.sbuf_tensor("brsb", [1, 8], F32))
        iota8 = e(nc.sbuf_tensor("iota8sb", [128, 8], F32))
        xrows = e(nc.sbuf_tensor("xrows", [1, BPC, D], F32))
        xrT = e(nc.sbuf_tensor("xrT", [128, 8, BPC], F32))
        scsb = e(nc.sbuf_tensor("scsb", [BPC, 8], F32))
        m1 = e(nc.sbuf_tensor("m1", [BPC, 8], F32))
        m2 = e(nc.sbuf_tensor("m2", [BPC, 8], F32))
        sm = e(nc.sbuf_tensor("sm", [BPC, 8], F32))
        idxsb = e(nc.sbuf_tensor("idxsb", [BPC, 2], F32))
        P0 = e(nc.psum_tensor("P0", [128, 512], F32))
        P1 = e(nc.psum_tensor("P1", [128, 512], F32))

        ch = Chain()
        ch.dma("sync", nwrow[:], nw_i)
        ch.dma("sync", onesf[:], onesf_i)
        ch.dma("sync", eyesb[:], eye128)
        ch.dma("sync", eye4r[:], eye4r_i)
        ch.dma("sync", wrsb[:], wr_i)
        ch.dma("sync", brsb[:], br_i)
        ch.dma("sync", iota8[:], iota8_i)
        for fh in range(2):
            ch.add("tensor", lambda e, fh=fh: e.matmul(
                out=P0[:, 0:512], lhsT=onesf[0:1, 0:128],
                rhs=nwrow[0:1, fh * 512:(fh + 1) * 512], start=True, stop=True))
            ch.add("scalar", lambda e, fh=fh: e.copy(
                out=nwrep[:, fh * 512:(fh + 1) * 512], in_=P0[:, 0:512]))

        NTIL = BPC * T // 128
        for i in range(NTIL):
            ch.dma("sync", xt[:], x_i[i])
            ch.dma("sync", s0t[:], s0_i[i])
            ch.dma("sync", g0t[:], g0_i[i])
            ch.add("scalar", lambda e: e.activation(out=g0t[:], in_=g0t[:], func=AF.Silu))
            ch.add("vector", lambda e: e.tensor_tensor(out=s0t[:], in0=s0t[:], in1=g0t[:], op=ALU.mult))
            ch.add("vector", lambda e: e.tensor_tensor(out=x1t[:], in0=xt[:], in1=s0t[:], op=ALU.add))
            ch.dma("sync", x1_o[i], x1t[:])
            _ln_tile(ch, None, None, x1t[:], stats, xn1t[:], extra_outs=(xn1bt[:],))
            ch.dma("sync", xn1_o[i], xn1t[:])
            ch.dma("sync", xn1b_o[i], xn1bt[:])
            ch.add("scalar", lambda e: e.activation(out=sq[:], in_=xn1t[:], func=AF.Square))
            ch.add("vector", lambda e: e.tensor_reduce(out=stats[:, 0:1], in_=sq[:], axis=AX.X, op=ALU.add))
            ch.add("vector", lambda e: e.tensor_scalar(
                out=stats[:, 0:1], in0=stats[:, 0:1], scalar1=1.0 / D, scalar2=EPS,
                op0=ALU.mult, op1=ALU.add))
            ch.add("scalar", lambda e: e.activation(out=stats[:, 1:2], in_=stats[:, 0:1], func=AF.Sqrt))
            ch.add("vector", lambda e: e.reciprocal(out=stats[:, 2:3], in_=stats[:, 1:2]))
            ch.add("vector", lambda e: e.tensor_scalar(
                out=sq[:], in0=xn1t[:], scalar1=stats[:, 2:3], scalar2=None, op0=ALU.mult))
            ch.add("vector", lambda e: e.tensor_tensor(out=hbt[:], in0=sq[:], in1=nwrep[:], op=ALU.mult))
            ch.dma("sync", hb_o[i], hbt[:])
            if i % 16 == 0:
                bidx = i // 16
                ch.add("vector", lambda e, bidx=bidx: e.tensor_copy(
                    out=xrows[0:1, bidx, :], in_=xn1t[0:1, :]))
        for dc in range(8):
            def trx(e, dc=dc):
                for bb in range(BPC):
                    r = e.matmul(out=P0[0:128, 0:BPC],
                                 lhsT=xrows[0:1, bb, dc * 128:(dc + 1) * 128],
                                 rhs=eye4r[0:1, bb * BPC:(bb + 1) * BPC],
                                 start=(bb == 0), stop=(bb == BPC - 1))
                return r
            ch.add("tensor", trx)
            ch.add("scalar", lambda e, dc=dc: e.copy(out=xrT[:, dc, :], in_=P0[0:128, 0:BPC]))
        def smm(e):
            for dc in range(8):
                e.matmul(out=P1[0:BPC, 0:8], lhsT=xrT[:, dc, :], rhs=wrsb[:, dc, :],
                         start=(dc == 0), stop=False)
            return e.matmul(out=P1[0:BPC, 0:8], lhsT=onesf[0:1, 0:BPC], rhs=brsb[:],
                            start=False, stop=True)
        ch.add("tensor", smm)
        ch.add("scalar", lambda e: e.copy(out=scsb[:], in_=P1[0:BPC, 0:8]))
        ch.add("vector", lambda e: e.reduce_max(out=sm[:, 0:1], in_=scsb[:], axis=AX.X))
        ch.add("vector", lambda e: e.tensor_scalar(
            out=m1[:], in0=scsb[:], scalar1=sm[:, 0:1], scalar2=None, op0=ALU.is_ge))
        ch.add("vector", lambda e: e.tensor_scalar(
            out=m2[:], in0=m1[:], scalar1=1e30, scalar2=None, op0=ALU.mult))
        ch.add("vector", lambda e: e.tensor_tensor(out=m2[:], in0=scsb[:], in1=m2[:], op=ALU.subtract))
        ch.add("vector", lambda e: e.reduce_max(out=sm[:, 1:2], in_=m2[:], axis=AX.X))
        ch.add("vector", lambda e: e.tensor_scalar(
            out=m2[:], in0=m2[:], scalar1=sm[:, 1:2], scalar2=None, op0=ALU.is_ge))
        ch.add("vector", lambda e: e.tensor_tensor(out=m1[:], in0=m1[:], in1=iota8[0:BPC, :], op=ALU.mult))
        ch.add("vector", lambda e: e.tensor_tensor(out=m2[:], in0=m2[:], in1=iota8[0:BPC, :], op=ALU.mult))
        ch.add("vector", lambda e: e.tensor_reduce(out=idxsb[:, 0:1], in_=m1[:], axis=AX.X, op=ALU.add))
        ch.add("vector", lambda e: e.tensor_reduce(out=idxsb[:, 1:2], in_=m2[:], axis=AX.X, op=ALU.add))
        ch.dma("sync", idx_o[:, :], idxsb[:])
        ch.build(nc)
    return nc


def stage_combine0(x_full, s0, scat_region, norm_w, Wr, br):
    nc = _NC_CACHE.setdefault("combine0", build_combine0())
    NTIL = BPC * T // 128
    g0pre = np.zeros((B * T, D), np.float32)
    g0pre[:8192] = scat_region
    g0pre = g0pre.reshape(NCORES, NTIL, 128, D)
    xs = x_full.reshape(NCORES, NTIL, 128, D)
    s0s = s0.reshape(NCORES, NTIL, 128, D)
    wr_r = np.ascontiguousarray(Wr.reshape(8, 128, 8).transpose(1, 0, 2))
    in_maps = []
    for c in range(NCORES):
        in_maps.append({
            "x": np.ascontiguousarray(xs[c]), "s0": np.ascontiguousarray(s0s[c]),
            "g0pre": np.ascontiguousarray(g0pre[c]),
            "nw": norm_w.reshape(1, D), "wr": wr_r, "br": br.reshape(1, 8),
            "eye128": np.eye(128, dtype=np.float32),
            "eye4r": np.eye(BPC, dtype=np.float32).reshape(1, BPC * BPC),
            "onesf": np.ones((128, 128), np.float32),
            "iota8": np.broadcast_to(np.arange(8, dtype=np.float32), (128, 8)).copy(),
        })
    res = _run(nc, in_maps)
    x1 = np.stack([r["x1"] for r in res.results]).reshape(B, T, D)
    xn1 = np.stack([r["xn1"] for r in res.results]).reshape(B, T, D)
    xn1b = np.stack([r["xn1b"] for r in res.results]).reshape(B, T, D)
    hb = np.stack([r["hb"] for r in res.results]).reshape(B, T, D)
    idx = np.stack([r["idx"] for r in res.results]).reshape(B, 2).astype(np.int64)
    return x1, xn1, xn1b, hb, idx



def build_gates():
    nc = bass.Bass("TRN2", target_bir_lowering=False)
    hT_i = nc.dram_tensor("hT", [128, 128, 8, 512], BF16, kind="ExternalInput").ap()
    wc_i = nc.dram_tensor("wc", [128, 8, 2304], F32, kind="ExternalInput").ap()
    gn_o = nc.dram_tensor("gn", [512, 128, 2176], BF16, kind="ExternalOutput").ap()
    v_o = nc.dram_tensor("v", [512, 128, 128], F32, kind="ExternalOutput").ap()

    with ExitStack() as ctx:
        e = ctx.enter_context
        wcsb = e(nc.sbuf_tensor("wcsb", [128, 8, 2304], BF16))
        strip = e(nc.sbuf_tensor("strip", [128, 8, 512], BF16))
        egl = e(nc.sbuf_tensor("egl", [128, 2176], F32))
        vsb = e(nc.sbuf_tensor("vsb", [128, 128], F32))
        rz = e(nc.sbuf_tensor("rz", [128, 128], F32))
        gnt = e(nc.sbuf_tensor("gnt", [128, 2176], BF16))
        PT = [e(nc.psum_tensor(f"PT{i}", [128, 512], F32)) for i in range(4)]
        PT4 = e(nc.psum_tensor("PT4", [128, 256], F32))

        ch = Chain()
        ch.dma("gpsimd", wcsb[:], wc_i)
        for tt in range(512):
            if tt % 4 == 0:
                ch.dma("gpsimd", strip[:], hT_i[tt // 4])
            ts = (tt % 4) * 128
            for p5 in range(5):
                off = p5 * 512
                sz = 256 if p5 == 4 else 512
                ps = PT4[:, 0:256] if p5 == 4 else PT[p5][:, 0:512]
                def gmm(e, ts=ts, off=off, sz=sz, ps=ps):
                    for dc in range(8):
                        r = e.matmul(out=ps, lhsT=strip[:, dc, ts:ts + 128],
                                     rhs=wcsb[:, dc, off:off + sz],
                                     start=(dc == 0), stop=(dc == 7))
                    return r
                ch.add("tensor", gmm)
            ch.add("scalar", lambda e: e.copy(out=vsb[:], in_=PT[0][:, 0:128]))
            ch.add("scalar", lambda e: e.activation(out=egl[:, 0:384], in_=PT[0][:, 128:512], func=AF.Exp))
            for p5 in range(1, 4):
                ch.add("scalar", lambda e, p5=p5: e.activation(
                    out=egl[:, p5 * 512 - 128:p5 * 512 + 384], in_=PT[p5][:, 0:512], func=AF.Exp))
            ch.add("scalar", lambda e: e.activation(out=egl[:, 1920:2176], in_=PT4[:, 0:256], func=AF.Exp))
            ch.add("vector", lambda e: e.tensor_reduce(
                out=rz[:], in_=egl[:].rearrange("p (m g) -> p m g", g=17), axis=AX.X, op=ALU.add))
            ch.add("vector", lambda e: e.reciprocal(out=rz[:], in_=rz[:]))
            ch.add("vector", lambda e: e.tensor_tensor(
                out=gnt[:].rearrange("p (m g) -> p m g", g=17),
                in0=egl[:].rearrange("p (m g) -> p m g", g=17),
                in1=rz[:].unsqueeze(2).broadcast_to((128, 128, 17)), op=ALU.mult))
            ch.dma("sync", gn_o[tt], gnt[:])
            ch.dma("sync", v_o[tt], vsb[:])
        ch.build(nc)
    return nc


def stage_gates(hb, W_v, W_a):
    nc = _NC_CACHE.setdefault("gates", build_gates())
    hT = np.ascontiguousarray(np.asarray(hb).reshape(B * T, D).T)
    hTr = np.ascontiguousarray(
        hT.reshape(8, 128, 128, 512).transpose(2, 1, 0, 3))
    Wa_r = W_a.reshape(H1, M1, M1 + 1, D)
    in_maps = []
    for c in range(NCORES):
        heads = range(8 * c, 8 * c + 8)
        wc = np.empty((D, 8, 16, 18), np.float32)
        for hh, hg in enumerate(heads):
            wc[:, hh, :, 17] = 0.0
        wcv = np.stack([W_v[hg * 16:(hg + 1) * 16].T for hg in heads], 1)
        wc_full = np.empty((D, 2304), np.float32)
        for hh, hg in enumerate(heads):
            wc_full[:, hh * 16:(hh + 1) * 16] = W_v[hg * 16:(hg + 1) * 16].T
        for hh, hg in enumerate(heads):
            blk = Wa_r[hg]
            wc_full[:, 128 + hh * 272:128 + (hh + 1) * 272] = \
                blk.transpose(2, 0, 1).reshape(D, 272)
        wcr = np.ascontiguousarray(wc_full.reshape(8, 128, 2304).transpose(1, 0, 2))
        in_maps.append({"hT": hTr, "wc": wcr})
    res = _run(nc, in_maps)
    gn = np.stack([r["gn"] for r in res.results])
    vv = np.stack([r["v"] for r in res.results])
    return gn, vv



def build_scan():
    nc = bass.Bass("TRN2", target_bir_lowering=False)
    CH = 64
    NCH = T // CH
    g_i = nc.dram_tensor("g", [NCH, 128, CH, 2, 16, 18], BF16, kind="ExternalInput").ap()
    v_i = nc.dram_tensor("v", [NCH, 128, CH, 2, 16], F32, kind="ExternalInput").ap()
    ys_o = nc.dram_tensor("ys", [NCH, 128, CH, 2, 16], F32, kind="ExternalOutput").ap()

    with ExitStack() as ctx:
        e = ctx.enter_context
        gb = e(nc.sbuf_tensor("gb", [128, CH, 2, 16, 18], BF16))
        vbuf = e(nc.sbuf_tensor("vbuf", [128, CH, 2, 16], F32))
        ub = e(nc.sbuf_tensor("ub", [128, CH, 2, 16], F32))
        ysb = [e(nc.sbuf_tensor(f"ysb{i}", [128, CH, 2, 16], F32)) for i in range(2)]
        st0 = e(nc.sbuf_tensor("st0", [128, 2, 16], F32))
        tmpb = e(nc.sbuf_tensor("tmpb", [128, 2, 16, 16], F32))
        red = e(nc.sbuf_tensor("red", [128, 2, 16], F32))

        ch = Chain()
        ch.add("vector", lambda e: e.memset(st0[:], 0.0))
        for ci in range(NCH):
            ys = ysb[ci % 2]
            ysp = ysb[(ci + 1) % 2]
            ch.dma("sync", gb[:], g_i[ci])
            ch.dma("sync", vbuf[:], v_i[ci])
            ch.add("vector", lambda e: e.tensor_tensor(
                out=ub[:], in0=vbuf[:], in1=gb[:, :, :, :, 16], op=ALU.mult))
            for t in range(CH):
                if ci == 0 and t == 0:
                    state = st0[:]
                elif t == 0:
                    state = ysp[:, CH - 1, :, :]
                else:
                    state = ys[:, t - 1, :, :]
                sb = state.unsqueeze(2).broadcast_to((128, 2, 16, 16))
                A = gb[:, t, :, :, 0:16]
                ch.add("vector", lambda e, A=A, sb=sb: e.tensor_tensor(
                    out=tmpb[:], in0=A, in1=sb, op=ALU.mult))
                ch.add("vector", lambda e: e.tensor_reduce(
                    out=red[:], in_=tmpb[:], axis=AX.X, op=ALU.add))
                ch.add("vector", lambda e, ys=ys, t=t: e.tensor_tensor(
                    out=ys[:, t, :, :], in0=red[:], in1=ub[:, t, :, :], op=ALU.add))
            ch.dma("sync", ys_o[ci], ys[:])
        ch.build(nc)
    return nc


def stage_scan(gn, vv):
    nc = _NC_CACHE.setdefault("scan", build_scan())
    A_all = np.asarray(gn).reshape(8, B * T, 8, 16, 17)
    v_all = np.asarray(vv).reshape(8, B * T, 8, 16)
    CH = 64
    NCH = T // CH
    in_maps = []
    seqs = np.arange(B * H1)
    bs, hgs = seqs // H1, seqs % H1
    gcs, hhs = hgs // 8, hgs % 8
    g_seq = np.zeros((B * H1, T, 16, 18), ml_dtypes.bfloat16)
    taus = bs[:, None] * T + np.arange(T)[None, :]
    for s in range(B * H1):
        blk = A_all[gcs[s], taus[s], hhs[s]]
        g_seq[s, :, :, 0:16] = blk[:, :, 1:17]
        g_seq[s, :, :, 16] = blk[:, :, 0]
    v_seq = np.empty((B * H1, T, 16), np.float32)
    for s in range(B * H1):
        v_seq[s] = v_all[gcs[s], taus[s], hhs[s]]
    for c in range(NCORES):
        sl = slice(256 * c, 256 * (c + 1))
        gss = g_seq[sl].reshape(128, 2, T, 16, 18).transpose(0, 2, 1, 3, 4)
        vss = v_seq[sl].reshape(128, 2, T, 16).transpose(0, 2, 1, 3)
        gss = np.ascontiguousarray(gss.reshape(128, NCH, CH, 2, 16, 18).transpose(1, 0, 2, 3, 4, 5))
        vss = np.ascontiguousarray(vss.reshape(128, NCH, CH, 2, 16).transpose(1, 0, 2, 3, 4))
        in_maps.append({"g": gss, "v": vss})
    res = _run(nc, in_maps)
    ys = np.stack([r["ys"] for r in res.results])
    ys = ys.transpose(0, 2, 4, 1, 3, 5).reshape(B * H1, T, 16)
    h_out = ys.reshape(B, H1, T, 16).transpose(0, 2, 1, 3).reshape(B, T, D)
    return np.ascontiguousarray(h_out)



def build_final():
    nc = bass.Bass("TRN2", target_bir_lowering=False)
    NTIL = T // 128
    x1_i = nc.dram_tensor("x1", [BPC, NTIL, 128, D], F32, kind="ExternalInput").ap()
    xn1_i = nc.dram_tensor("xn1", [BPC, NTIL, 128, D], F32, kind="ExternalInput").ap()
    xn1T_i = nc.dram_tensor("xn1T", [BPC, 128, 8, T], BF16, kind="ExternalInput").ap()
    ysT_i = nc.dram_tensor("ysT", [BPC, 128, 8, T], F32, kind="ExternalInput").ap()
    woutT_i = nc.dram_tensor("woutT", [128, 8, D], F32, kind="ExternalInput").ap()
    w1g_i = nc.dram_tensor("w1g", [BPC, 2, 8, 128, 8, 128], F32, kind="ExternalInput").ap()
    b1g_i = nc.dram_tensor("b1g", [128, BPC, 2, 8], F32, kind="ExternalInput").ap()
    w2g_i = nc.dram_tensor("w2g", [BPC, 128, 2, 8, 1024], F32, kind="ExternalInput").ap()
    b2g_i = nc.dram_tensor("b2g", [1, BPC, 2, 1024], F32, kind="ExternalInput").ap()
    onesf_i = nc.dram_tensor("onesf", [128, 128], F32, kind="ExternalInput").ap()
    out_o = nc.dram_tensor("out", [BPC, NTIL, 128, D], F32, kind="ExternalOutput").ap()

    with ExitStack() as ctx:
        e = ctx.enter_context
        xn1T = e(nc.sbuf_tensor("xn1Tb", [128, 8, T], BF16))
        ysT = e(nc.sbuf_tensor("ysTb", [128, 8, T], BF16))
        woutsb = e(nc.sbuf_tensor("woutsb", [128, 8, D], BF16))
        h1T = e(nc.sbuf_tensor("h1T", [128, 2, 8, T], BF16))
        w2sb = e(nc.sbuf_tensor("w2sb", [128, 2, 8, D], BF16))
        w1strip = e(nc.sbuf_tensor("w1strip", [128, 8, 128], BF16))
        b1sb = e(nc.sbuf_tensor("b1sb", [128, BPC, 2, 8], F32))
        b2row = e(nc.sbuf_tensor("b2row", [1, 2, 1024], F32))
        b2sum = e(nc.sbuf_tensor("b2sum", [1, 1024], F32))
        b2rep = e(nc.sbuf_tensor("b2rep", [128, 1024], F32))
        onesf = e(nc.sbuf_tensor("onesfb", [128, 128], F32))
        x1t = e(nc.sbuf_tensor("x1t", [128, D], F32))
        sg = e(nc.sbuf_tensor("sg", [128, D], F32))
        g1t = e(nc.sbuf_tensor("g1t", [128, D], F32))
        PS = [e(nc.psum_tensor(f"PS{i}", [128, 512], F32)) for i in range(2)]
        PO = [e(nc.psum_tensor(f"PO{i}", [128, 512], F32)) for i in range(2)]

        ch = Chain()
        ch.dma("gpsimd", woutsb[:], woutT_i)
        ch.dma("sync", b1sb[:], b1g_i)
        ch.dma("sync", onesf[:], onesf_i)
        for b in range(BPC):
            ch.dma("sync", b2row[:], b2g_i[0:1, b])
            ch.dma("gpsimd", xn1T[:], xn1T_i[b])
            ch.dma("gpsimd", ysT[:], ysT_i[b])
            ch.dma("gpsimd", w2sb[:], w2g_i[b])
            ch.add("vector", lambda e: e.tensor_tensor(
                out=b2sum[:], in0=b2row[0:1, 0, :], in1=b2row[0:1, 1, :], op=ALU.add))
            for fh in range(2):
                ch.add("tensor", lambda e, fh=fh: e.matmul(
                    out=PS[0][:, 0:512], lhsT=onesf[0:1, 0:128],
                    rhs=b2sum[0:1, fh * 512:(fh + 1) * 512], start=True, stop=True))
                ch.add("scalar", lambda e, fh=fh: e.copy(
                    out=b2rep[:, fh * 512:(fh + 1) * 512], in_=PS[0][:, 0:512]))
            for k in range(2):
                for ffc in range(8):
                    ch.dma("gpsimd", w1strip[:], w1g_i[b, k, ffc])
                    for tc in range(4):
                        def hmm_(e, tc=tc):
                            for dc in range(8):
                                r = e.matmul(out=PS[0][:, 0:512], lhsT=w1strip[:, dc, :],
                                             rhs=xn1T[:, dc, tc * 512:(tc + 1) * 512],
                                             start=(dc == 0), stop=(dc == 7))
                            return r
                        ch.add("tensor", hmm_)
                        ch.add("scalar", lambda e, k=k, ffc=ffc, tc=tc, b=b: e.activation(
                            out=h1T[:, k, ffc, tc * 512:(tc + 1) * 512], in_=PS[0][:, 0:512],
                            func=AF.Gelu, bias=b1sb[:, b, k, ffc:ffc + 1]))
            for tt in range(16):
                def smm(e, tt=tt):
                    for fh in range(2):
                        for dc in range(8):
                            r = e.matmul(out=PS[fh][:, 0:512],
                                         lhsT=ysT[:, dc, tt * 128:(tt + 1) * 128],
                                         rhs=woutsb[:, dc, fh * 512:(fh + 1) * 512],
                                         start=(dc == 0), stop=(dc == 7))
                    return r
                ch.add("tensor", smm)
                ch.dma("sync", g1t[:], xn1_i[b, tt])
                ch.dma("sync", x1t[:], x1_i[b, tt])
                for fh in range(2):
                    ch.add("vector", lambda e, fh=fh: e.tensor_tensor(
                        out=sg[:, fh * 512:(fh + 1) * 512], in0=PS[fh][:, 0:512],
                        in1=g1t[:, fh * 512:(fh + 1) * 512], op=ALU.add))
                def omm(e, tt=tt):
                    for fh in range(2):
                        n = 0
                        for k in range(2):
                            for ffc in range(8):
                                r = e.matmul(out=PO[fh][:, 0:512],
                                             lhsT=h1T[:, k, ffc, tt * 128:(tt + 1) * 128],
                                             rhs=w2sb[:, k, ffc, fh * 512:(fh + 1) * 512],
                                             start=(n == 0), stop=(n == 15))
                                n += 1
                    return r
                ch.add("tensor", omm)
                for fh in range(2):
                    ch.add("vector", lambda e, fh=fh: e.tensor_tensor(
                        out=g1t[:, fh * 512:(fh + 1) * 512], in0=PO[fh][:, 0:512],
                        in1=b2rep[:, fh * 512:(fh + 1) * 512], op=ALU.add))
                ch.add("scalar", lambda e: e.activation(out=g1t[:], in_=g1t[:], func=AF.Silu, scale=0.5))
                ch.add("vector", lambda e: e.tensor_tensor(out=sg[:], in0=sg[:], in1=g1t[:], op=ALU.mult))
                ch.add("vector", lambda e: e.tensor_tensor(out=x1t[:], in0=x1t[:], in1=sg[:], op=ALU.add))
                ch.dma("sync", out_o[b, tt], x1t[:])
        ch.build(nc)
    return nc


def stage_final(x1, xn1, xn1b, h_out, idx, W_out, W1f, b1f, W2f, b2f):
    nc = _NC_CACHE.setdefault("final", build_final())
    NTIL = T // 128
    woutT = np.ascontiguousarray(W_out.T.reshape(8, 128, D).transpose(1, 0, 2))
    in_maps = []
    for c in range(NCORES):
        bsl = slice(c * BPC, (c + 1) * BPC)
        xn1Tc = np.asarray(xn1b[bsl]).transpose(0, 2, 1)
        ysTc = np.ascontiguousarray(h_out[bsl].transpose(0, 2, 1))
        idxc = idx[bsl]
        w1g = W1f[idxc]
        b1g = b1f[idxc]
        w2g = W2f[idxc]
        b2g = b2f[idxc]
        in_maps.append({
            "x1": np.ascontiguousarray(x1[bsl].reshape(BPC, NTIL, 128, D)),
            "xn1": np.ascontiguousarray(xn1[bsl].reshape(BPC, NTIL, 128, D)),
            "xn1T": np.ascontiguousarray(xn1Tc.reshape(BPC, 8, 128, T).transpose(0, 2, 1, 3)),
            "ysT": np.ascontiguousarray(ysTc.reshape(BPC, 8, 128, T).transpose(0, 2, 1, 3)),
            "woutT": woutT,
            "w1g": np.ascontiguousarray(
                w1g.reshape(BPC, 2, 8, 128, 8, 128).transpose(0, 1, 4, 3, 2, 5)),
            "b1g": np.ascontiguousarray(
                b1g.reshape(BPC, 2, 8, 128).transpose(3, 0, 1, 2)),
            "w2g": np.ascontiguousarray(
                w2g.reshape(BPC, 2, 8, 128, 1024).transpose(0, 3, 1, 2, 4)),
            "b2g": b2g.reshape(1, BPC, 2, 1024),
            "onesf": np.ones((128, 128), np.float32),
        })
    res = _run(nc, in_maps)
    out = np.stack([r["out"] for r in res.results])
    return out.reshape(B, T, D)



def kernel_staged(**inputs):
    x = np.asarray(inputs["L0_attn_x"], np.float32)
    xn0 = stage_ln0(x)
    s0 = stage_attn0(xn0, inputs["L0_attn_W_attn"], inputs["L0_attn_W_proj"])
    weighted = stage_ffn0(xn0, inputs["L0_ffn_top_idx"], inputs["L0_ffn_combine_weights"],
                          inputs["L0_ffn_W1"], inputs["L0_ffn_b1"],
                          inputs["L0_ffn_W2"], inputs["L0_ffn_b2"])
    scat = stage_scatter(weighted, inputs["L0_ffn_top_idx"])
    x1, xn1, xn1b, hb, idx = stage_combine0(x, s0, scat, inputs["L1_attn_norm_w"],
                                            inputs["L1_ffn_Wr"], inputs["L1_ffn_br"])
    gn, vv = stage_gates(hb, inputs["L1_attn_W_v"], inputs["L1_attn_W_a"])
    h_out = stage_scan(gn, vv)
    out = stage_final(x1, xn1, xn1b, h_out, idx, inputs["L1_attn_W_out"],
                      inputs["L1_ffn_W1"], inputs["L1_ffn_b1"],
                      inputs["L1_ffn_W2"], inputs["L1_ffn_b2"])
    return out



class ChainSeg(Chain):
    SEG = 50000

    def __init__(self):
        self.groups = []

    def add(self, engine, fn, inc=1):
        self.groups.append([(engine, fn, inc)])

    def add_parallel(self, ops):
        self.groups.append(list(ops))

    def dma(self, engine, out, in_, **kw):
        self.add(engine, lambda e, o=out, i=in_, kw=kw: e.dma_start(out=o, in_=i, **kw), 16)

    def build(self, nc):
        segs = []
        cur = []
        c = 0
        for g in self.groups:
            ginc = sum(o[2] for o in g)
            if c + ginc > self.SEG and cur:
                segs.append(cur)
                cur = []
                c = 0
            cur.append(g)
            c += ginc
        if cur:
            segs.append(cur)
        with ExitStack() as st:
            sems = [st.enter_context(nc.semaphore(f"chseg{i}")) for i in range(len(segs))]
            block = st.enter_context(nc.Block())
            seg_totals = []
            plans = {}
            for si, seg in enumerate(segs):
                c = 0
                for g in seg:
                    for engname, fn, inc in g:
                        plans.setdefault(engname, []).append((si, c, fn, inc))
                    c += sum(o[2] for o in g)
                seg_totals.append(c)
            for engname, plan in plans.items():
                def body(eng, plan=plan):
                    cur_seg = 0
                    for si, pre, fn, inc in plan:
                        while cur_seg < si:
                            eng.wait_ge(sems[cur_seg], seg_totals[cur_seg])
                            cur_seg += 1
                        if pre > 0:
                            eng.wait_ge(sems[si], pre)
                        last = fn(eng)
                        last.then_inc(sems[si], inc)
                getattr(block, engname)(body)

            def tail(eng):
                for si in range(len(segs)):
                    eng.wait_ge(sems[si], seg_totals[si])
            block.sync(tail)


class ChainP:
    NSEM = 6

    def __init__(self):
        self.ops = []
        self.counts = [0] * self.NSEM
        self.cur = 0
        self.tile_no = -1
        self.tile_ends = {}
        self.pending_waits = []

    def begin_tile(self):
        if self.tile_no >= 0:
            self.tile_ends[self.tile_no] = (self.cur, self.counts[self.cur])
        self.tile_no += 1
        self.cur = self.tile_no % self.NSEM
        return self.tile_no

    def wait_tile_end(self, k):
        if k >= 0 and k in self.tile_ends:
            self.pending_waits.append(self.tile_ends[k])

    def mark(self):
        return (self.cur, self.counts[self.cur])

    def wait_mark(self, m):
        if m is not None:
            self.pending_waits.append(m)

    def add(self, engine, fn, inc=1, raw=False):
        pre = self.counts[self.cur]
        ew = self.pending_waits
        self.pending_waits = []
        self.ops.append((engine, fn, inc, self.cur, pre, ew, raw))
        self.counts[self.cur] += inc

    def dma(self, engine, out, in_, **kw):
        self.add(engine, lambda e, o=out, i=in_, kw=kw: e.dma_start(out=o, in_=i, **kw), 16)

    def finish(self):
        if self.tile_no >= 0:
            self.tile_ends[self.tile_no] = (self.cur, self.counts[self.cur])

    def emit(self, ch, nc, name):
        self.finish()
        sems = [nc.semaphore(f"{name}{i}").__enter__() for i in range(self.NSEM)]
        totals = list(self.counts)
        by_engine = {}
        for op in self.ops:
            by_engine.setdefault(op[0], []).append(op)

        def make_region(engine_ops, sems=sems):
            def region(eng):
                last = None
                for _, fn, inc, sidx, pre, ew, raw in engine_ops:
                    for (wsem, wcount) in ew:
                        if wcount > 0:
                            eng.wait_ge(sems[wsem], wcount)
                    if pre > 0:
                        eng.wait_ge(sems[sidx], pre)
                    if raw:
                        last = fn(eng, sems[sidx])
                    else:
                        last = fn(eng)
                        last.then_inc(sems[sidx], inc)
                return last
            return region

        engines = list(by_engine)
        group = []
        for engname in engines:
            def fn(eng, engname=engname):
                make_region(by_engine[engname])(eng)
                for i in range(self.NSEM):
                    if totals[i] > 0:
                        eng.wait_ge(sems[i], totals[i])
                return eng.nop()
            group.append((engname, fn, 1))
        ch.add_parallel(group)


def _collective_issue(ch, kind, op, ins, outs, sem_name, nc):
    cc_sem = nc.semaphore(sem_name).__enter__()

    def fn(e):
        from concourse.aot_env import allow_ici_all_to_all
        from contextlib import nullcontext
        cm = allow_ici_all_to_all() if kind == "AllToAll" else nullcontext()
        with cm:
            e.collective_compute(kind, op, replica_groups=[list(range(NCORES))],
                                 ins=ins, outs=outs).then_inc(cc_sem)
        return e.nop()
    ch.add("gpsimd", fn)
    return cc_sem


def _collective_wait(ch, cc_sem):
    def fn(e):
        e.wait_ge(cc_sem, 1)
        return e.nop()
    ch.add("gpsimd", fn)


def _collective(ch, kind, op, ins, outs, sem_name, nc):
    cc_sem = nc.semaphore(sem_name).__enter__()

    def fn(e):
        from concourse.aot_env import allow_ici_all_to_all
        from contextlib import nullcontext
        cm = allow_ici_all_to_all() if kind == "AllToAll" else nullcontext()
        with cm:
            e.collective_compute(kind, op, replica_groups=[list(range(NCORES))],
                                 ins=ins, outs=outs).then_inc(cc_sem)
        e.wait_ge(cc_sem, 1)
        return e.nop()
    ch.add("gpsimd", fn)


def build_fused():
    nc = bass.Bass("TRN2", target_bir_lowering=False)
    NTIL = BPC * T // 128
    x_i = nc.dram_tensor("x", [NTIL, 128, D], F32, kind="ExternalInput").ap()
    gxr_i = nc.dram_tensor("gxr", [16, 128, D], F32, kind="ExternalInput").ap()
    wqk_i = nc.dram_tensor("wqk", [16, 128, 8, 128], F32, kind="ExternalInput").ap()
    wv_i = nc.dram_tensor("wv", [128, 8, D], F32, kind="ExternalInput").ap()
    wp_i = nc.dram_tensor("wp", [128, 8, D], F32, kind="ExternalInput").ap()
    w1_i = nc.dram_tensor("w1", [128, 32, 8, 128], F32, kind="ExternalInput").ap()
    w2_i = nc.dram_tensor("w2", [8, 128, 32, 128], F32, kind="ExternalInput").ap()
    b1_i = nc.dram_tensor("b1", [128, 32], F32, kind="ExternalInput").ap()
    b2_i = nc.dram_tensor("b2", [128, 8], F32, kind="ExternalInput").ap()
    wvec_i = nc.dram_tensor("wvec", [1, 2048], F32, kind="ExternalInput").ap()
    idxi_i = nc.dram_tensor("idxi", [128, 8, 16], I32, kind="ExternalInput").ap()
    nw_i = nc.dram_tensor("nw", [1, D], F32, kind="ExternalInput").ap()
    wr_i = nc.dram_tensor("wr", [128, 8, 8], F32, kind="ExternalInput").ap()
    br_i = nc.dram_tensor("br", [1, 8], F32, kind="ExternalInput").ap()
    wc_i = nc.dram_tensor("wc", [128, 8, 2304], F32, kind="ExternalInput").ap()
    woutT_i = nc.dram_tensor("woutT", [128, 8, D], F32, kind="ExternalInput").ap()
    w1f_i = nc.dram_tensor("w1f", [8192, D], F32, kind="ExternalInput").ap()
    w2f_i = nc.dram_tensor("w2f", [8192, D], F32, kind="ExternalInput").ap()
    b1f_i = nc.dram_tensor("b1f", [8, D], F32, kind="ExternalInput").ap()
    b2f_i = nc.dram_tensor("b2f", [8, D], F32, kind="ExternalInput").ap()
    eyes_i = nc.dram_tensor("eyes", [128, 3, 64], F32, kind="ExternalInput").ap()
    eye64_i = nc.dram_tensor("eye64", [128, 64], F32, kind="ExternalInput").ap()
    onesf_i = nc.dram_tensor("onesf", [128, 128], F32, kind="ExternalInput").ap()
    onesbf_i = nc.dram_tensor("onesbf", [128, 512], BF16, kind="ExternalInput").ap()
    iota1024_i = nc.dram_tensor("iota1024", [128, 1024], F32, kind="ExternalInput").ap()
    iotacol_i = nc.dram_tensor("iotacol", [128, 1], F32, kind="ExternalInput").ap()
    rbase_i = nc.dram_tensor("rbase", [128, 1], F32, kind="ExternalInput").ap()
    iscore_i = nc.dram_tensor("iscore", [128, 1], F32, kind="ExternalInput").ap()
    out_o = nc.dram_tensor("out", [BPC, 16, 128, D], F32, kind="ExternalOutput").ap()

    d_xn0T = nc.dram_tensor("d_xn0T", [BPC, 128, 8, T], BF16)
    d_gxnT = nc.dram_tensor("d_gxnT", [128, 8, 2048], BF16)
    d_s0 = nc.dram_tensor("d_s0", [BPC, 16, 128, D], F32)
    agw_in = nc.dram_tensor("agw_in", [2048, D], BF16)
    agw_out = nc.dram_tensor("agw_out", [8, 2048, D], BF16)
    agsc_in = nc.dram_tensor("agsc_in", [8, 128, D], F32)
    agsc_out = nc.dram_tensor("agsc_out", [64, 128, D], F32)
    d_x1 = nc.dram_tensor("d_x1", [BPC, 16, 128, D], F32)
    d_xn1 = nc.dram_tensor("d_xn1", [BPC, 16, 128, D], F32)
    d_xn1T = nc.dram_tensor("d_xn1T", [BPC, 128, 8, T], BF16)
    agh_inB = [nc.dram_tensor(f"agh_in{b}", [D, T], BF16) for b in range(BPC)]
    agh_outB = [nc.dram_tensor(f"agh_out{b}", [8, D, T], BF16) for b in range(BPC)]
    d_idx = nc.dram_tensor("d_idx", [BPC, 2], F32)
    d_w1g = nc.dram_tensor("d_w1g", [BPC, 2, 8, 128, D], BF16)
    d_w2g = nc.dram_tensor("d_w2g", [BPC, 2, 8, 128, D], BF16)
    d_b1g = nc.dram_tensor("d_b1g", [BPC, 2, D], BF16)
    d_b2s = nc.dram_tensor("d_b2s", [BPC, D], F32)
    d_gseqH = [nc.dram_tensor(f"d_gseq{i}", [256, T // 2, 19, 16], BF16) for i in range(2)]
    d_ys = nc.dram_tensor("d_ys", [256, 16, T], BF16)
    a2y_in = nc.dram_tensor("a2y_in", [8, 8, 4, 16, T], BF16)
    a2y_out = nc.dram_tensor("a2y_out", [8, 8, 4, 16, T], BF16)

    ch = ChainSeg()
    C_GELU = 0.7978845608028654

    def mmop(ps, lhsT, rhs, start=True, stop=True):
        return lambda e: e.matmul(out=ps, lhsT=lhsT, rhs=rhs, start=start, stop=stop)

    with ExitStack() as ctx:
        e_ = ctx.enter_context
        onesf = e_(nc.sbuf_tensor("onesfb", [128, 128], F32))
        onesbf = e_(nc.sbuf_tensor("onesbfb", [128, 512], BF16))
        eyesb = e_(nc.sbuf_tensor("eyesb", [128, 3, 64], F32))
        eye64b = e_(nc.sbuf_tensor("eye64b", [128, 64], F32))
        iotacol = e_(nc.sbuf_tensor("iotacolb", [128, 1], F32))
        ch.dma("sync", onesf[:], onesf_i)
        ch.dma("sync", onesbf[:], onesbf_i)
        ch.dma("sync", eyesb[:], eyes_i)
        ch.dma("sync", eye64b[:], eye64_i)
        ch.dma("sync", iotacol[:], iotacol_i)

        def _ph():
            with ExitStack() as pa:
                ea = pa.enter_context
                xts = [ea(nc.sbuf_tensor(f"A_xt{i}", [128, D], F32)) for i in range(2)]
                xnbs = [ea(nc.sbuf_tensor(f"A_xnb{i}", [128, D], BF16)) for i in range(2)]
                xTts = [ea(nc.sbuf_tensor(f"A_xTt{i}", [128, 8, 128], BF16)) for i in range(2)]
                statss = [ea(nc.sbuf_tensor(f"A_stats{i}", [128, 16], F32)) for i in range(2)]
                cp = ChainP()
                for i in range(NTIL + 16):
                    tno = cp.begin_tile()
                    cp.wait_tile_end(tno - 2)
                    cp.wait_tile_end(tno - 3)
                    si = i % 2
                    xt, xnb, xTt, stats = xts[si], xnbs[si], xTts[si], statss[si]
                    if i < NTIL:
                        cp.add("sync", lambda e, i=i, xt=xt: e.dma_start(out=xt[:], in_=x_i[i]), 16)
                    else:
                        cp.add("sync", lambda e, i=i, xt=xt: e.dma_start(out=xt[:], in_=gxr_i[i - NTIL]), 16)
                    _ln_tile(cp, None, None, xt[:], stats, xnb[:])
                    def xbars(e, sem, xnb=xnb, xTt=xTt):
                        for dc in range(8):
                            r = e.dma_start_transpose(
                                xTt[:, dc, :], xnb[:, dc * 128:(dc + 1) * 128]).then_inc(sem, 16)
                        return r
                    cp.add("scalar" if i % 2 == 0 else "sync", xbars, 128, raw=True)
                    if i < NTIL:
                        b, tl = i // 16, i % 16
                        cp.add("sync", lambda e, b=b, tl=tl, xTt=xTt: e.dma_start(
                            out=d_xn0T.ap()[b, :, :, tl * 128:(tl + 1) * 128], in_=xTt[:]), 16)
                    else:
                        j = i - NTIL
                        cp.add("sync", lambda e, j=j, xTt=xTt: e.dma_start(
                            out=d_gxnT.ap()[:, :, j * 128:(j + 1) * 128], in_=xTt[:]), 16)
                cp.emit(ch, nc, "Ap")
        _ph()
        def _ph():
            with ExitStack() as pb:
                eb = pb.enter_context
                qT = eb(nc.sbuf_tensor("qT", [128, 8, T], BF16))
                kT = eb(nc.sbuf_tensor("kT", [128, 8, T], BF16))
                vb = eb(nc.sbuf_tensor("vb", [128, 16, D], BF16))
                yT = eb(nc.sbuf_tensor("yT", [128, 8, T], BF16))
                xb = eb(nc.sbuf_tensor("xb", [128, 8, T], BF16))
                wres = eb(nc.sbuf_tensor("wres", [128, 8, D], BF16))
                wstrips = [eb(nc.sbuf_tensor(f"wstrip{i}", [128, 8, 128], BF16)) for i in range(2)]
                expNT = eb(nc.sbuf_tensor("expNT", [128, 16, 128], BF16))
                expLT = eb(nc.sbuf_tensor("expLT", [128, T], BF16))
                repsb = eb(nc.sbuf_tensor("repsb", [128, 512], F32))
                rzlsb = eb(nc.sbuf_tensor("rzlsb", [1, 1024], F32))
                sc_ql = eb(nc.sbuf_tensor("sc_ql", [128, 64], F32))
                sc_kl = eb(nc.sbuf_tensor("sc_kl", [128, 64], F32))
                sc_qlbf = eb(nc.sbuf_tensor("sc_qlbf", [128, 64], BF16))
                sc_klbf = eb(nc.sbuf_tensor("sc_klbf", [128, 64], BF16))
                Ksb = eb(nc.sbuf_tensor("Ksb", [128, 64], F32))
                Kt_sb = eb(nc.sbuf_tensor("Kt_sb", [128, 64], F32))
                V_sb = eb(nc.sbuf_tensor("V_sb", [128, 64], F32))
                W_sb = eb(nc.sbuf_tensor("W_sb", [128, 64], F32))
                T_sb = eb(nc.sbuf_tensor("T_sb", [128, 64], F32))
                KVt_sb = eb(nc.sbuf_tensor("KVt_sb", [128, 64], F32))
                z1sb = eb(nc.sbuf_tensor("z1sb", [128, 64], F32))
                z2aug = eb(nc.sbuf_tensor("z2aug", [128, 65], BF16))
                smalls = eb(nc.sbuf_tensor("smalls", [128, 8], F32))
                s0ts = [eb(nc.sbuf_tensor(f"s0t{i}", [128, D], F32)) for i in range(2)]
                P0s = [eb(nc.psum_tensor(f"P0{i}", [128, 512], F32)) for i in range(2)]
                P1s = [eb(nc.psum_tensor(f"P1{i}", [128, 512], F32)) for i in range(2)]
                P2s = [eb(nc.psum_tensor(f"P2{i}", [128, 512], F32)) for i in range(2)]
                PBs = [eb(nc.psum_tensor(f"PB{i}", [128, 512], F32)) for i in range(2)]
                cp = ChainP()
                cp.begin_tile()
                cp.add("vector", lambda e: e.memset(z2aug[:, 64:65], 1.0))
                prev_b_end = 0
                for b in range(BPC):
                    t_xb = cp.begin_tile()
                    cp.wait_tile_end(prev_b_end)
                    cp.add("gpsimd", lambda e, b=b: e.dma_start(out=xb[:], in_=d_xn0T.ap()[b]), 16)
                    k_mm = 0
                    for part, dst in ((0, qT), (1, kT)):
                        for cc in range(8):
                            for tc in range(4):
                                tno = cp.begin_tile()
                                cp.wait_tile_end(tno - 2)
                                cp.wait_tile_end(tno - 3)
                                if part == 0 and cc == 0 and tc == 0:
                                    cp.wait_tile_end(t_xb)
                                ws = wstrips[(part * 8 + cc) % 2]
                                if tc == 0:
                                    cp.add("gpsimd", lambda e, part=part, cc=cc, ws=ws: e.dma_start(
                                        out=ws[:], in_=wqk_i[part * 8 + cc]), 16)
                                ps = P0s[k_mm % 2]
                                def qmm(e, cc=cc, tc=tc, ws=ws, ps=ps):
                                    for dc in range(8):
                                        r = e.matmul(out=ps[:, 0:512], lhsT=ws[:, dc, :],
                                                     rhs=xb[:, dc, tc * 512:(tc + 1) * 512],
                                                     start=(dc == 0), stop=(dc == 7))
                                    return r
                                cp.add("tensor", qmm)
                                cp.add("scalar", lambda e, cc=cc, tc=tc, dst=dst, ps=ps: e.copy(
                                    out=dst[:, cc, tc * 512:(tc + 1) * 512], in_=ps[:, 0:512]))
                                k_mm += 1
                    t_wv = cp.begin_tile()
                    cp.wait_tile_end(t_wv - 2)
                    cp.wait_tile_end(t_wv - 3)
                    cp.add("gpsimd", lambda e: e.dma_start(out=wres[:], in_=wv_i), 16)
                    last_v = t_wv
                    for tt in range(16):
                        for fh in range(2):
                            tno = cp.begin_tile()
                            cp.wait_tile_end(tno - 2)
                            cp.wait_tile_end(tno - 3)
                            if tt == 0 and fh == 0:
                                cp.wait_tile_end(t_wv)
                            ps = P0s[k_mm % 2]
                            def vmm(e, tt=tt, fh=fh, ps=ps):
                                for dc in range(8):
                                    r = e.matmul(out=ps[:, 0:512],
                                                 lhsT=xb[:, dc, tt * 128:(tt + 1) * 128],
                                                 rhs=wres[:, dc, fh * 512:(fh + 1) * 512],
                                                 start=(dc == 0), stop=(dc == 7))
                                return r
                            cp.add("tensor", vmm)
                            cp.add("scalar", lambda e, tt=tt, fh=fh, ps=ps: e.copy(
                                out=vb[:, tt, fh * 512:(fh + 1) * 512], in_=ps[:, 0:512]))
                            last_v = tno
                            k_mm += 1
                    last_head = last_v
                    for h in range(NH):
                        s = h % 2
                        bb = 64 * s
                        cc = h // 2
                        B0, B1 = bb, bb + 64
                        tno = cp.begin_tile()
                        cp.wait_tile_end(tno - 2)
                        cp.wait_tile_end(tno - 3)
                        if h == 0:
                            cp.wait_tile_end(last_v)
                        P0, P1, P2, PB = P0s[s], P1s[s], P2s[s], PBs[s]
                        qTh = qT[B0:B1, cc, :]
                        kTh = kT[B0:B1, cc, :]
                        cp.add("vector", lambda e, qTh=qTh, B0=B0, B1=B1: e.tensor_reduce(
                            out=sc_ql[B0:B1, :], in_=qTh.rearrange("p (l g) -> p l g", g=32),
                            axis=AX.X, op=ALU.add))
                        cp.add("vector", lambda e, kTh=kTh, B0=B0, B1=B1: e.tensor_reduce(
                            out=sc_kl[B0:B1, :], in_=kTh.rearrange("p (l g) -> p l g", g=32),
                            axis=AX.X, op=ALU.add))
                        cp.add("scalar", lambda e, B0=B0, B1=B1: e.copy(out=sc_qlbf[B0:B1, :], in_=sc_ql[B0:B1, :]))
                        cp.add("scalar", lambda e, B0=B0, B1=B1: e.copy(out=sc_klbf[B0:B1, :], in_=sc_kl[B0:B1, :]))
                        cp.add("tensor", lambda e, B0=B0, B1=B1, P0=P0: e.matmul(
                            out=P0[B0:B1, 0:64], lhsT=sc_ql[B0:B1, :],
                            rhs=sc_kl[B0:B1, :], start=True, stop=True))
                        cp.add("scalar", lambda e, B0=B0, B1=B1, P0=P0: e.activation(
                            out=Ksb[B0:B1, :], in_=P0[B0:B1, 0:64], func=AF.Exp, scale=1.0 / 1024.0))
                        cp.add("vector", lambda e, B0=B0, B1=B1: e.tensor_reduce(
                            out=smalls[B0:B1, 0:1], in_=Ksb[B0:B1, :], axis=AX.X, op=ALU.add))
                        cp.add("vector", lambda e, B0=B0, B1=B1: e.reciprocal(
                            out=smalls[B0:B1, 1:2], in_=smalls[B0:B1, 0:1]))
                        cp.add("vector", lambda e, B0=B0, B1=B1: e.tensor_scalar(
                            out=Ksb[B0:B1, :], in0=Ksb[B0:B1, :],
                            scalar1=smalls[B0:B1, 1:2], scalar2=None, op0=ALU.mult))
                        cp.add("tensor", lambda e, B0=B0, B1=B1, P1=P1: e.matmul(
                            out=P1[0:1, 0:64], lhsT=onesf[B0:B1, 0:1], rhs=Ksb[B0:B1, :],
                            start=True, stop=True))
                        cp.add("vector", lambda e, s=s, P1=P1: e.reduce_max(
                            out=smalls[0:1, 2 + 4 * s:3 + 4 * s], in_=P1[0:1, 0:64], axis=AX.X))
                        cp.add("vector", lambda e, s=s: e.reciprocal(
                            out=smalls[0:1, 3 + 4 * s:4 + 4 * s], in_=smalls[0:1, 2 + 4 * s:3 + 4 * s]))
                        cp.add("tensor", lambda e, B0=B0, B1=B1, s=s, P2=P2: e.matmul(
                            out=P2[B0:B1, 0:1], lhsT=onesf[0:1, B0:B1],
                            rhs=smalls[0:1, 3 + 4 * s:4 + 4 * s], start=True, stop=True))
                        cp.add("tensor", lambda e, B0=B0, B1=B1, P0=P0: e.matmul(
                            out=P0[B0:B1, 0:64], lhsT=Ksb[B0:B1, :], rhs=eye64b[B0:B1, :],
                            start=True, stop=True))
                        cp.add("vector", lambda e, B0=B0, B1=B1, P0=P0, P2=P2: e.tensor_scalar(
                            out=V_sb[B0:B1, :], in0=P0[B0:B1, 0:64],
                            scalar1=P2[B0:B1, 0:1], scalar2=None, op0=ALU.mult))
                        cp.add("vector", lambda e, B0=B0, B1=B1, P2=P2: e.tensor_scalar(
                            out=W_sb[B0:B1, :], in0=Ksb[B0:B1, :],
                            scalar1=P2[B0:B1, 0:1], scalar2=None, op0=ALU.mult))
                        cp.add("scalar", lambda e, B0=B0, B1=B1, P0=P0: e.copy(
                            out=Kt_sb[B0:B1, :], in_=P0[B0:B1, 0:64]))
                        for _ in range(6):
                            cp.add("tensor", lambda e, B0=B0, B1=B1, P0=P0: e.matmul(
                                out=P0[B0:B1, 0:64], lhsT=Kt_sb[B0:B1, :], rhs=V_sb[B0:B1, :],
                                start=True, stop=True))
                            cp.add("vector", lambda e, B0=B0, B1=B1, P0=P0: e.tensor_tensor(
                                out=T_sb[B0:B1, :], in0=eyesb[B0:B1, 0, :],
                                in1=P0[B0:B1, 0:64], op=ALU.subtract))
                            cp.add("tensor", lambda e, B0=B0, B1=B1, P1=P1: e.matmul(
                                out=P1[B0:B1, 0:64], lhsT=V_sb[B0:B1, :], rhs=Kt_sb[B0:B1, :],
                                start=True, stop=True))
                            cp.add("scalar", lambda e, B0=B0, B1=B1, P1=P1: e.copy(
                                out=KVt_sb[B0:B1, :], in_=P1[B0:B1, 0:64]))
                            cp.add("tensor", lambda e, B0=B0, B1=B1, P0=P0: e.matmul(
                                out=P0[B0:B1, 0:64], lhsT=KVt_sb[B0:B1, :], rhs=T_sb[B0:B1, :],
                                start=True, stop=True))
                            cp.add("vector", lambda e, B0=B0, B1=B1, P0=P0: e.tensor_tensor(
                                out=T_sb[B0:B1, :], in0=eyesb[B0:B1, 1, :],
                                in1=P0[B0:B1, 0:64], op=ALU.subtract))
                            cp.add("tensor", lambda e, B0=B0, B1=B1, P0=P0: e.matmul(
                                out=P0[B0:B1, 0:64], lhsT=KVt_sb[B0:B1, :], rhs=T_sb[B0:B1, :],
                                start=True, stop=True))
                            cp.add("vector", lambda e, B0=B0, B1=B1, P0=P0: e.tensor_tensor(
                                out=T_sb[B0:B1, :], in0=eyesb[B0:B1, 2, :],
                                in1=P0[B0:B1, 0:64], op=ALU.subtract))
                            cp.add("tensor", lambda e, B0=B0, B1=B1, P1=P1: e.matmul(
                                out=P1[B0:B1, 0:64], lhsT=W_sb[B0:B1, :], rhs=T_sb[B0:B1, :],
                                start=True, stop=True))
                            cp.add("tensor", lambda e, B0=B0, B1=B1, P2=P2: e.matmul(
                                out=P2[B0:B1, 0:64], lhsT=T_sb[B0:B1, :], rhs=W_sb[B0:B1, :],
                                start=True, stop=True))
                            cp.add("scalar", lambda e, B0=B0, B1=B1, P1=P1: e.mul(
                                V_sb[B0:B1, :], P1[B0:B1, 0:64], 0.25))
                            cp.add("scalar", lambda e, B0=B0, B1=B1, P2=P2: e.mul(
                                W_sb[B0:B1, :], P2[B0:B1, 0:64], 0.25))
                        for tc in range(16):
                            cp.add("tensor", lambda e, tc=tc, B0=B0, B1=B1, cc=cc, P0=P0: e.matmul(
                                out=P0[0:128, 0:64], lhsT=kT[B0:B1, cc, tc * 128:(tc + 1) * 128],
                                rhs=sc_qlbf[B0:B1, :], start=True, stop=True))
                            cp.add("scalar", lambda e, tc=tc, s=s, P0=P0: e.activation(
                                out=expNT[:, tc, s * 64:(s + 1) * 64], in_=P0[0:128, 0:64],
                                func=AF.Exp, scale=1.0 / 32.0))
                        def z1mm(e, h=h, s=s, B0=B0, B1=B1, P1=P1, P2=P2):
                            for tc in range(16):
                                e.matmul(out=P2[B0:B1, 0:64], lhsT=expNT[:, tc, s * 64:(s + 1) * 64],
                                         rhs=vb[:, tc, h * 64:(h + 1) * 64],
                                         start=(tc == 0), stop=(tc == 15))
                                r = e.matmul(out=P1[B0:B1, 0:1], lhsT=expNT[:, tc, s * 64:(s + 1) * 64],
                                             rhs=onesbf[:, 0:1], start=(tc == 0), stop=(tc == 15))
                            return r
                        cp.add("tensor", z1mm)
                        cp.add("vector", lambda e, B0=B0, B1=B1, P1=P1: e.reciprocal(
                            out=smalls[B0:B1, 4:5], in_=P1[B0:B1, 0:1]))
                        cp.add("vector", lambda e, B0=B0, B1=B1, P2=P2: e.tensor_scalar(
                            out=z1sb[B0:B1, :], in0=P2[B0:B1, 0:64],
                            scalar1=smalls[B0:B1, 4:5], scalar2=None, op0=ALU.mult))
                        cp.add("tensor", lambda e, B0=B0, B1=B1, P0=P0: e.matmul(
                            out=P0[B0:B1, 0:64], lhsT=W_sb[B0:B1, :], rhs=z1sb[B0:B1, :],
                            start=True, stop=True))
                        cp.add("scalar", lambda e, B0=B0, B1=B1, P0=P0: e.copy(
                            out=z2aug[B0:B1, 0:64], in_=P0[B0:B1, 0:64]))
                        for tc in range(4):
                            cp.add("tensor", lambda e, tc=tc, B0=B0, B1=B1, cc=cc, P1=P1: e.matmul(
                                out=P1[B0:B1, 0:512], lhsT=sc_klbf[B0:B1, :],
                                rhs=qT[B0:B1, cc, tc * 512:(tc + 1) * 512], start=True, stop=True))
                            cp.add("scalar", lambda e, tc=tc, B0=B0, B1=B1, P1=P1: e.activation(
                                out=expLT[B0:B1, tc * 512:(tc + 1) * 512], in_=P1[B0:B1, 0:512],
                                func=AF.Exp, scale=1.0 / 32.0))
                        for tc in range(4):
                            def ymm(e, tc=tc, B0=B0, B1=B1, PB=PB, P1=P1):
                                e.matmul(out=PB[B0:B1, 0:512], lhsT=z2aug[B0:B1, 0:64],
                                         rhs=expLT[B0:B1, tc * 512:(tc + 1) * 512],
                                         start=True, stop=True)
                                return e.matmul(out=P1[0:1, 0:512], lhsT=z2aug[B0:B1, 64:65],
                                                rhs=expLT[B0:B1, tc * 512:(tc + 1) * 512],
                                                start=True, stop=True)
                            cp.add("tensor", ymm)
                            cp.add("vector", lambda e, s=s, P1=P1: e.reciprocal(
                                out=rzlsb[0:1, s * 512:(s + 1) * 512], in_=P1[0:1, 0:512]))
                            cp.add("tensor", lambda e, B0=B0, B1=B1, s=s, P2=P2: e.matmul(
                                out=P2[B0:B1, 0:512], lhsT=onesf[0:1, B0:B1],
                                rhs=rzlsb[0:1, s * 512:(s + 1) * 512], start=True, stop=True))
                            cp.add("scalar", lambda e, B0=B0, B1=B1, P2=P2: e.copy(
                                out=repsb[B0:B1, 0:512], in_=P2[B0:B1, 0:512]))
                            cp.add("vector", lambda e, tc=tc, B0=B0, B1=B1, cc=cc, PB=PB: e.tensor_tensor(
                                out=yT[B0:B1, cc, tc * 512:(tc + 1) * 512],
                                in0=PB[B0:B1, 0:512], in1=repsb[B0:B1, 0:512],
                                op=ALU.mult))
                        last_head = tno
                    t_wp = cp.begin_tile()
                    cp.wait_tile_end(t_wp - 2)
                    cp.wait_tile_end(t_wp - 3)
                    cp.wait_tile_end(last_head)
                    cp.add("gpsimd", lambda e: e.dma_start(out=wres[:], in_=wp_i), 16)
                    for tt in range(16):
                        tno = cp.begin_tile()
                        cp.wait_tile_end(tno - 2)
                        cp.wait_tile_end(tno - 3)
                        if tt == 0:
                            cp.wait_tile_end(t_wp)
                        s0t = s0ts[tt % 2]
                        for fh in range(2):
                            ps = (P0s if fh == 0 else P1s)[tt % 2]
                            def pmm(e, tt=tt, fh=fh, ps=ps):
                                for dc in range(8):
                                    r = e.matmul(out=ps[:, 0:512],
                                                 lhsT=yT[:, dc, tt * 128:(tt + 1) * 128],
                                                 rhs=wres[:, dc, fh * 512:(fh + 1) * 512],
                                                 start=(dc == 0), stop=(dc == 7))
                                return r
                            cp.add("tensor", pmm)
                            cp.add("scalar", lambda e, fh=fh, ps=ps, s0t=s0t: e.copy(
                                out=s0t[:, fh * 512:(fh + 1) * 512], in_=ps[:, 0:512]))
                        cp.add("sync", lambda e, b=b, tt=tt, s0t=s0t: e.dma_start(
                            out=d_s0.ap()[b, tt], in_=s0t[:]), 16)
                        prev_b_end = tno
                cp.emit(ch, nc, "Bp")
        _ph()
        def _ph():
            with ExitStack() as pc:
                ec = pc.enter_context
                gx = ec(nc.sbuf_tensor("C_gx", [128, 8, 2048], BF16))
                w1sb = ec(nc.sbuf_tensor("C_w1sb", [128, 32, 8, 128], BF16))
                w2sbs = [ec(nc.sbuf_tensor(f"C_w2sb{i}", [128, 32, 128], BF16)) for i in range(2)]
                b1sb = ec(nc.sbuf_tensor("C_b1sb", [128, 32], F32))
                b2sb = ec(nc.sbuf_tensor("C_b2sb", [128, 8], F32))
                wrow = ec(nc.sbuf_tensor("C_wrow", [1, 2048], F32))
                wrep = ec(nc.sbuf_tensor("C_wrep", [128, 2048], F32))
                hTq = ec(nc.sbuf_tensor("C_hTq", [128, 32, 512], BF16))
                gas = [ec(nc.sbuf_tensor(f"C_ga{i}", [128, 512], F32)) for i in range(2)]
                gbs = [ec(nc.sbuf_tensor(f"C_gb{i}", [128, 512], F32)) for i in range(2)]
                gcs = [ec(nc.sbuf_tensor(f"C_gc{i}", [128, 512], F32)) for i in range(2)]
                wouts = [ec(nc.sbuf_tensor(f"C_wout{i}", [128, 512], BF16)) for i in range(2)]
                wtps = [ec(nc.sbuf_tensor(f"C_wtp{i}", [128, 128], BF16)) for i in range(2)]
                PHs = [ec(nc.psum_tensor(f"PH{i}", [128, 512], F32)) for i in range(2)]
                POs = [ec(nc.psum_tensor(f"PO{i}", [128, 512], F32)) for i in range(2)]
                ch.dma("gpsimd", gx[:], d_gxnT.ap())
                ch.dma("gpsimd", w1sb[:], w1_i)
                ch.dma("sync", b1sb[:], b1_i)
                ch.dma("sync", b2sb[:], b2_i)
                ch.dma("sync", wrow[:], wvec_i)
                cp = ChainP()
                t0_ = cp.begin_tile()
                for tc in range(4):
                    cp.add("tensor", lambda e, tc=tc: e.matmul(
                        out=POs[0][:, 0:512], lhsT=onesf[0:1, 0:128],
                        rhs=wrow[0:1, tc * 512:(tc + 1) * 512], start=True, stop=True))
                    cp.add("scalar", lambda e, tc=tc: e.copy(
                        out=wrep[:, tc * 512:(tc + 1) * 512], in_=POs[0][:, 0:512]))
                kt = 0
                last_omm = -1
                for tq in range(4):
                    for ffc in range(32):
                        tno = cp.begin_tile()
                        cp.wait_tile_end(tno - 2)
                        cp.wait_tile_end(tno - 3)
                        if ffc == 0 and tq > 0:
                            cp.wait_tile_end(last_omm)
                        s = kt % 2
                        PH, ga, gb_, gc_ = PHs[s], gas[s], gbs[s], gcs[s]
                        def hmm_(e, tq=tq, ffc=ffc, PH=PH):
                            for dc in range(8):
                                r = e.matmul(out=PH[:, 0:512], lhsT=w1sb[:, ffc, dc, :],
                                             rhs=gx[:, dc, tq * 512:(tq + 1) * 512],
                                             start=(dc == 0), stop=(dc == 7))
                            return r
                        cp.add("tensor", hmm_)
                        cp.add("vector", lambda e, ffc=ffc, PH=PH, ga=ga: e.tensor_scalar(
                            out=ga[:], in0=PH[:, 0:512], scalar1=b1sb[:, ffc:ffc + 1],
                            scalar2=None, op0=ALU.add))
                        cp.add("scalar", lambda e, ga=ga, gb_=gb_: e.activation(out=gb_[:], in_=ga[:], func=AF.Square))
                        cp.add("vector", lambda e, gb_=gb_: e.tensor_scalar(
                            out=gb_[:], in0=gb_[:], scalar1=0.044715, scalar2=1.0,
                            op0=ALU.mult, op1=ALU.add))
                        cp.add("vector", lambda e, gb_=gb_, gc_=gc_, ga=ga: e.tensor_tensor(
                            out=gc_[:], in0=gb_[:], in1=ga[:], op=ALU.mult))
                        cp.add("scalar", lambda e, gc_=gc_: e.activation(out=gc_[:], in_=gc_[:], func=AF.Tanh, scale=C_GELU))
                        cp.add("vector", lambda e, gc_=gc_: e.tensor_scalar(
                            out=gc_[:], in0=gc_[:], scalar1=1.0, scalar2=0.5, op0=ALU.add, op1=ALU.mult))
                        cp.add("vector", lambda e, ffc=ffc, gc_=gc_, ga=ga: e.tensor_tensor(
                            out=hTq[:, ffc, :], in0=gc_[:], in1=ga[:], op=ALU.mult))
                        kt += 1
                        last_h = tno
                    for dc in range(8):
                        tno = cp.begin_tile()
                        cp.wait_tile_end(tno - 2)
                        cp.wait_tile_end(tno - 3)
                        if dc == 0:
                            cp.wait_tile_end(last_h)
                        s = kt % 2
                        w2sb, ga, wout, wtp = w2sbs[s], gas[s], wouts[s], gcs[s]
                        PO = POs[s]
                        cp.add("gpsimd", lambda e, dc=dc, w2sb=w2sb: e.dma_start(
                            out=w2sb[:], in_=w2_i[dc]), 16)
                        def omm(e, PO=PO, w2sb=w2sb):
                            for ffc in range(32):
                                r = e.matmul(out=PO[:, 0:512], lhsT=w2sb[:, ffc, :],
                                             rhs=hTq[:, ffc, :], start=(ffc == 0), stop=(ffc == 31))
                            return r
                        cp.add("tensor", omm)
                        cp.add("vector", lambda e, dc=dc, PO=PO, ga=ga: e.tensor_scalar(
                            out=ga[:], in0=PO[:, 0:512], scalar1=b2sb[:, dc:dc + 1],
                            scalar2=None, op0=ALU.add))
                        cp.add("vector", lambda e, tq=tq, ga=ga, wout=wout: e.tensor_tensor(
                            out=wout[:], in0=ga[:], in1=wrep[:, tq * 512:(tq + 1) * 512], op=ALU.mult))
                        for sc in range(4):
                            wtpb = wtps[sc % 2]
                            cp.add("scalar", lambda e, sc=sc, wout=wout, wtpb=wtpb: e.dma_start_transpose(
                                wtpb[:], wout[:, sc * 128:(sc + 1) * 128]), 16)
                            cp.add("sync", lambda e, tq=tq, sc=sc, dc=dc, wtpb=wtpb: e.dma_start(
                                out=agw_in.ap()[tq * 512 + sc * 128:tq * 512 + (sc + 1) * 128,
                                                dc * 128:(dc + 1) * 128], in_=wtpb[:]), 16)
                        kt += 1
                        last_omm = tno
                cp.emit(ch, nc, "Cp")
            _collective(ch, "AllGather", ALU.bypass, [agw_in.ap()], [agw_out.ap()],
                        "ccw", nc)
        _ph()
        def _ph():
            with ExitStack() as pd:
                ed = pd.enter_context
                idxsb = ed(nc.sbuf_tensor("D_idxsb", [128, 128], I32))
                idxf = ed(nc.sbuf_tensor("D_idxf", [128, 128], F32))
                iotasb = ed(nc.sbuf_tensor("D_iotasb", [128, 1024], F32))
                rbasesb = ed(nc.sbuf_tensor("D_rbasesb", [128, 1], F32))
                ohsb = ed(nc.sbuf_tensor("D_ohsb", [128, 1024], BF16))
                wchunk = ed(nc.sbuf_tensor("D_wchunk", [128, 512], BF16))
                outsb = ed(nc.sbuf_tensor("D_outsb", [128, 512], F32))
                PRs = [ed(nc.psum_tensor(f"PR{i}", [128, 512], F32)) for i in range(8)]
                ch.dma("sync", idxsb[:].rearrange("p (e s) -> p e s", e=8), idxi_i)
                ch.dma("sync", iotasb[:], iota1024_i)
                ch.dma("sync", rbasesb[:], rbase_i)
                ch.add("scalar", lambda e: e.copy(out=idxf[:], in_=idxsb[:]))
                ch.add("vector", lambda e: e.tensor_scalar(
                    out=idxf[:], in0=idxf[:], scalar1=rbasesb[:, 0:1], scalar2=None,
                    op0=ALU.subtract))
                for dh in range(2):
                    n = 0
                    for ee2 in range(8):
                        for sc in range(16):
                            def wld(e, ee2=ee2, sc=sc, dh=dh):
                                ap = bass.AP(agw_out, ee2 * 2048 * D + sc * 128 * D + dh * 512,
                                             [[D, 128], [1, 512]])
                                return e.dma_start(out=wchunk[:], in_=ap)
                            ch.add("gpsimd", wld, 16)
                            ch.add("vector", lambda e, ee2=ee2, sc=sc: e.tensor_scalar(
                                out=ohsb[:], in0=iotasb[:],
                                scalar1=idxf[:, ee2 * 16 + sc:ee2 * 16 + sc + 1],
                                scalar2=None, op0=ALU.is_equal))
                            def dmm(e, n=n):
                                for rt in range(8):
                                    r = e.matmul(out=PRs[rt][:, 0:512],
                                                 lhsT=ohsb[:, rt * 128:(rt + 1) * 128],
                                                 rhs=wchunk[:], start=(n == 0),
                                                 stop=(n == 127))
                                return r
                            ch.add("tensor", dmm)
                            n += 1
                    for rt in range(8):
                        ch.add("scalar", lambda e, rt=rt: e.copy(out=outsb[:], in_=PRs[rt][:, 0:512]))
                        ch.dma("sync", agsc_in.ap()[rt, :, dh * 512:(dh + 1) * 512], outsb[:])
                _collective(ch, "AllGather", ALU.bypass, [agsc_in.ap()], [agsc_out.ap()],
                            "ccs", nc)
        _ph()
        agh_sems = []
        def _ph():
            with ExitStack() as pe:
                ee = pe.enter_context
                xts = [ee(nc.sbuf_tensor(f"E_xt{i}", [128, D], F32)) for i in range(2)]
                s0ts = [ee(nc.sbuf_tensor(f"E_s0t{i}", [128, D], F32)) for i in range(2)]
                g0ts = [ee(nc.sbuf_tensor(f"E_g0t{i}", [128, D], F32)) for i in range(2)]
                x1ts = [ee(nc.sbuf_tensor(f"E_x1t{i}", [128, D], F32)) for i in range(2)]
                xn1ts = [ee(nc.sbuf_tensor(f"E_xn1t{i}", [128, D], F32)) for i in range(2)]
                xn1bts = [ee(nc.sbuf_tensor(f"E_xn1bt{i}", [128, D], BF16)) for i in range(2)]
                hbts = [ee(nc.sbuf_tensor(f"E_hbt{i}", [128, D], BF16)) for i in range(2)]
                sqs = [ee(nc.sbuf_tensor(f"E_sq{i}", [128, D], F32)) for i in range(2)]
                statss = [ee(nc.sbuf_tensor(f"E_stats{i}", [128, 16], F32)) for i in range(2)]
                xTts = [ee(nc.sbuf_tensor(f"E_xTt{i}", [128, 8, 128], BF16)) for i in range(2)]
                nwrep = ee(nc.sbuf_tensor("E_nwrep", [128, D], F32))
                nwrow = ee(nc.sbuf_tensor("E_nwrow", [1, D], F32))
                iscoresb = ee(nc.sbuf_tensor("E_iscore", [128, 1], F32))
                wrsb = ee(nc.sbuf_tensor("E_wrsb", [128, 8, 8], F32))
                brsb = ee(nc.sbuf_tensor("E_brsb", [1, 8], F32))
                iota8 = ee(nc.sbuf_tensor("E_iota8", [128, 8], F32))
                xrows = ee(nc.sbuf_tensor("E_xrows", [1, BPC, D], F32))
                eye4r = ee(nc.sbuf_tensor("E_eye4r", [1, BPC * BPC], F32))
                xrT = ee(nc.sbuf_tensor("E_xrT", [128, 8, BPC], F32))
                scsb = ee(nc.sbuf_tensor("E_scsb", [BPC, 8], F32))
                m1 = ee(nc.sbuf_tensor("E_m1", [BPC, 8], F32))
                m2 = ee(nc.sbuf_tensor("E_m2", [BPC, 8], F32))
                sm = ee(nc.sbuf_tensor("E_sm", [BPC, 8], F32))
                idxsb = ee(nc.sbuf_tensor("E_idxsb", [BPC, 2], F32))
                idxflat = ee(nc.sbuf_tensor("E_idxflat", [1, 2 * BPC], F32))
                basecol = ee(nc.sbuf_tensor("E_basecol", [128, 1], F32))
                base1024 = ee(nc.sbuf_tensor("E_base1024", [128, 1], F32))
                u32col = ee(nc.sbuf_tensor("E_u32col", [128, 1], mybir.dt.uint32))
                wgsb = ee(nc.sbuf_tensor("E_wgsb", [128, D], BF16))
                b1fsb = ee(nc.sbuf_tensor("E_b1fsb", [8, D], F32))
                b2fsb = ee(nc.sbuf_tensor("E_b2fsb", [8, D], F32))
                mcol = ee(nc.sbuf_tensor("E_mcol", [8, 1], F32))
                brow = ee(nc.sbuf_tensor("E_brow", [1, D], BF16))
                b2acc = ee(nc.sbuf_tensor("E_b2acc", [1, D], F32))
                P0e = ee(nc.psum_tensor("P0e", [128, 512], F32))
                P1e = ee(nc.psum_tensor("P1e", [128, 512], F32))
                ch.dma("sync", nwrow[:], nw_i)
                ch.dma("sync", iscoresb[:], iscore_i)
                ch.dma("sync", wrsb[:], wr_i)
                ch.dma("sync", brsb[:], br_i)
                ch.dma("sync", eye4r[:],
                       nc.dram_tensor("eye4r", [1, BPC * BPC], F32, kind="ExternalInput").ap())
                ch.dma("sync", iota8[:],
                       nc.dram_tensor("iota8", [128, 8], F32, kind="ExternalInput").ap())
                ch.dma("sync", b1fsb[:], b1f_i)
                ch.dma("sync", b2fsb[:], b2f_i)
                for fh in range(2):
                    ch.add("tensor", lambda e, fh=fh: e.matmul(
                        out=P0e[:, 0:512], lhsT=onesf[0:1, 0:128],
                        rhs=nwrow[0:1, fh * 512:(fh + 1) * 512], start=True, stop=True))
                    ch.add("scalar", lambda e, fh=fh: e.copy(
                        out=nwrep[:, fh * 512:(fh + 1) * 512], in_=P0e[:, 0:512]))
                cp = ChainP()
                for i in range(NTIL):
                    b, tl = i // 16, i % 16
                    tno = cp.begin_tile()
                    cp.wait_tile_end(tno - 2)
                    cp.wait_tile_end(tno - 3)
                    si = i % 2
                    xt, s0t, g0t, x1t = xts[si], s0ts[si], g0ts[si], x1ts[si]
                    xn1t, xn1bt, hbt, sq = xn1ts[si], xn1bts[si], hbts[si], sqs[si]
                    stats, xTt = statss[si], xTts[si]
                    def lds(e, sem, i=i, b=b, tl=tl, xt=xt, s0t=s0t, g0t=g0t):
                        e.dma_start(out=xt[:], in_=x_i[i]).then_inc(sem, 16)
                        e.dma_start(out=s0t[:], in_=d_s0.ap()[b, tl]).then_inc(sem, 16)
                        return e.dma_start(out=g0t[:], in_=agsc_out.ap()[i]).then_inc(sem, 16)
                    cp.add("sync", lds, 48, raw=True)
                    cp.add("vector", lambda e, g0t=g0t: e.tensor_scalar(
                        out=g0t[:], in0=g0t[:], scalar1=iscoresb[:, 0:1], scalar2=None, op0=ALU.mult))
                    cp.add("scalar", lambda e, g0t=g0t: e.activation(out=g0t[:], in_=g0t[:], func=AF.Silu))
                    cp.add("vector", lambda e, s0t=s0t, g0t=g0t: e.tensor_tensor(
                        out=s0t[:], in0=s0t[:], in1=g0t[:], op=ALU.mult))
                    cp.add("vector", lambda e, x1t=x1t, xt=xt, s0t=s0t: e.tensor_tensor(
                        out=x1t[:], in0=xt[:], in1=s0t[:], op=ALU.add))
                    cp.add("sync", lambda e, b=b, tl=tl, x1t=x1t: e.dma_start(
                        out=d_x1.ap()[b, tl], in_=x1t[:]), 16)
                    _ln_tile(cp, None, None, x1t[:], stats, xn1t[:], extra_outs=(xn1bt[:],))
                    cp.add("sync", lambda e, b=b, tl=tl, xn1t=xn1t: e.dma_start(
                        out=d_xn1.ap()[b, tl], in_=xn1t[:]), 16)
                    def xbars1(e, sem, xn1bt=xn1bt, xTt=xTt):
                        for dc in range(8):
                            r = e.dma_start_transpose(
                                xTt[:, dc, :], xn1bt[:, dc * 128:(dc + 1) * 128]).then_inc(sem, 16)
                        return r
                    cp.add("scalar" if i % 2 == 0 else "sync", xbars1, 128, raw=True)
                    cp.add("sync", lambda e, b=b, tl=tl, xTt=xTt: e.dma_start(
                        out=d_xn1T.ap()[b, :, :, tl * 128:(tl + 1) * 128], in_=xTt[:]), 16)
                    cp.add("scalar", lambda e, xn1t=xn1t, sq=sq: e.activation(out=sq[:], in_=xn1t[:], func=AF.Square))
                    cp.add("vector", lambda e, sq=sq, stats=stats: e.tensor_reduce(
                        out=stats[:, 0:1], in_=sq[:], axis=AX.X, op=ALU.add))
                    cp.add("vector", lambda e, stats=stats: e.tensor_scalar(
                        out=stats[:, 0:1], in0=stats[:, 0:1], scalar1=1.0 / D, scalar2=EPS,
                        op0=ALU.mult, op1=ALU.add))
                    cp.add("scalar", lambda e, stats=stats: e.activation(out=stats[:, 1:2], in_=stats[:, 0:1], func=AF.Sqrt))
                    cp.add("vector", lambda e, stats=stats: e.reciprocal(out=stats[:, 2:3], in_=stats[:, 1:2]))
                    cp.add("vector", lambda e, sq=sq, xn1t=xn1t, stats=stats: e.tensor_scalar(
                        out=sq[:], in0=xn1t[:], scalar1=stats[:, 2:3], scalar2=None, op0=ALU.mult))
                    cp.add("vector", lambda e, hbt=hbt, sq=sq: e.tensor_tensor(
                        out=hbt[:], in0=sq[:], in1=nwrep[:], op=ALU.mult))
                    def xbars2(e, sem, hbt=hbt, xTt=xTt):
                        for dc in range(8):
                            r = e.dma_start_transpose(
                                xTt[:, dc, :], hbt[:, dc * 128:(dc + 1) * 128]).then_inc(sem, 16)
                        return r
                    cp.add("sync" if i % 2 == 0 else "scalar", xbars2, 128, raw=True)
                    def hdma(e, b=b, tl=tl, xTt=xTt):
                        ap = bass.AP(agh_inB[b], tl * 128,
                                     [[T, 128], [128 * T, 8], [1, 128]])
                        return e.dma_start(out=ap, in_=xTt[:])
                    cp.add("sync", hdma, 16)
                    if tl == 15:
                        cp.wait_tile_end(tno - 1)
                        agh_sems.append(_collective_issue(
                            cp, "AllGather", ALU.bypass, [agh_inB[b].ap()],
                            [agh_outB[b].ap()], f"cch{b}", nc))
                    if tl == 0:
                        cp.add("vector", lambda e, b=b, xn1t=xn1t: e.tensor_copy(
                            out=xrows[0:1, b, :], in_=xn1t[0:1, :]))
                cp.emit(ch, nc, "Ep")
                for dc in range(8):
                    def trx(e, dc=dc):
                        for bb in range(BPC):
                            r = e.matmul(out=P0e[0:128, 0:BPC],
                                         lhsT=xrows[0:1, bb, dc * 128:(dc + 1) * 128],
                                         rhs=eye4r[0:1, bb * BPC:(bb + 1) * BPC],
                                         start=(bb == 0), stop=(bb == BPC - 1))
                        return r
                    ch.add("tensor", trx)
                    ch.add("scalar", lambda e, dc=dc: e.copy(out=xrT[:, dc, :], in_=P0e[0:128, 0:BPC]))
                def smm(e):
                    for dc in range(8):
                        e.matmul(out=P1e[0:BPC, 0:8], lhsT=xrT[:, dc, :], rhs=wrsb[:, dc, :],
                                 start=(dc == 0), stop=False)
                    return e.matmul(out=P1e[0:BPC, 0:8], lhsT=onesf[0:1, 0:BPC], rhs=brsb[:],
                                    start=False, stop=True)
                ch.add("tensor", smm)
                ch.add("scalar", lambda e: e.copy(out=scsb[:], in_=P1e[0:BPC, 0:8]))
                ch.add("vector", lambda e: e.reduce_max(out=sm[:, 0:1], in_=scsb[:], axis=AX.X))
                ch.add("vector", lambda e: e.tensor_scalar(
                    out=m1[:], in0=scsb[:], scalar1=sm[:, 0:1], scalar2=None, op0=ALU.is_ge))
                ch.add("vector", lambda e: e.tensor_scalar(
                    out=m2[:], in0=m1[:], scalar1=1e30, scalar2=None, op0=ALU.mult))
                ch.add("vector", lambda e: e.tensor_tensor(out=m2[:], in0=scsb[:], in1=m2[:], op=ALU.subtract))
                ch.add("vector", lambda e: e.reduce_max(out=sm[:, 1:2], in_=m2[:], axis=AX.X))
                ch.add("vector", lambda e: e.tensor_scalar(
                    out=m2[:], in0=m2[:], scalar1=sm[:, 1:2], scalar2=None, op0=ALU.is_ge))
                ch.add("vector", lambda e: e.tensor_tensor(out=m1[:], in0=m1[:], in1=iota8[0:BPC, :], op=ALU.mult))
                ch.add("vector", lambda e: e.tensor_tensor(out=m2[:], in0=m2[:], in1=iota8[0:BPC, :], op=ALU.mult))
                ch.add("vector", lambda e: e.tensor_reduce(out=idxsb[:, 0:1], in_=m1[:], axis=AX.X, op=ALU.add))
                ch.add("vector", lambda e: e.tensor_reduce(out=idxsb[:, 1:2], in_=m2[:], axis=AX.X, op=ALU.add))
                ch.dma("sync", d_idx.ap()[:, :], idxsb[:])
                ch.dma("sync", idxflat[:], d_idx.ap().rearrange("b k -> (b k)").unsqueeze(0))
                for s in range(2 * BPC):
                    b, k = s // 2, s % 2
                    ch.add("tensor", lambda e, s=s: e.matmul(
                        out=P0e[0:128, 0:1], lhsT=onesf[0:1, 0:128],
                        rhs=idxflat[0:1, s:s + 1], start=True, stop=True))
                    ch.add("scalar", lambda e: e.copy(out=basecol[:], in_=P0e[0:128, 0:1]))
                    ch.add("vector", lambda e: e.tensor_scalar(
                        out=base1024[:], in0=basecol[:], scalar1=1024.0, scalar2=None, op0=ALU.mult))
                    for c8 in range(8):
                        ch.add("vector", lambda e, c8=c8: e.tensor_scalar(
                            out=u32col[:], in0=iotacol[:], scalar1=base1024[:, 0:1],
                            scalar2=float(128 * c8), op0=ALU.add, op1=ALU.add))
                        for w_src, w_dst in ((w1f_i, d_w1g), (w2f_i, d_w2g)):
                            ch.add("gpsimd", lambda e, w_src=w_src: e.indirect_dma_start(
                                out=wgsb[:], out_offset=None, in_=w_src,
                                in_offset=bass.IndirectOffsetOnAxis(ap=u32col[:], axis=0)), 16)
                            ch.dma("sync", w_dst.ap()[b, k, c8], wgsb[:])
                for b in range(BPC):
                    for k in range(2):
                        s = b * 2 + k
                        ch.add("tensor", lambda e, s=s: e.matmul(
                            out=P0e[0:8, 0:1], lhsT=onesf[0:1, 0:8],
                            rhs=idxflat[0:1, s:s + 1], start=True, stop=True))
                        ch.add("vector", lambda e: e.tensor_scalar(
                            out=mcol[:], in0=iotacol[0:8, 0:1], scalar1=P0e[0:8, 0:1],
                            scalar2=None, op0=ALU.is_equal))
                        def b1mm(e):
                            e.matmul(out=P0e[0:1, 0:512], lhsT=mcol[:], rhs=b1fsb[:, 0:512],
                                     start=True, stop=True)
                            return e.matmul(out=P1e[0:1, 0:512], lhsT=mcol[:], rhs=b1fsb[:, 512:1024],
                                            start=True, stop=True)
                        ch.add("tensor", b1mm)
                        ch.add("scalar", lambda e: e.copy(out=brow[0:1, 0:512], in_=P0e[0:1, 0:512]))
                        ch.add("scalar", lambda e: e.copy(out=brow[0:1, 512:1024], in_=P1e[0:1, 0:512]))
                        ch.dma("sync", d_b1g.ap()[b, k].unsqueeze(0), brow[:])
                        def b2mm(e):
                            e.matmul(out=P0e[0:1, 0:512], lhsT=mcol[:], rhs=b2fsb[:, 0:512],
                                     start=True, stop=True)
                            return e.matmul(out=P1e[0:1, 0:512], lhsT=mcol[:], rhs=b2fsb[:, 512:1024],
                                            start=True, stop=True)
                        ch.add("tensor", b2mm)
                        if k == 0:
                            ch.add("scalar", lambda e: e.copy(out=b2acc[0:1, 0:512], in_=P0e[0:1, 0:512]))
                            ch.add("scalar", lambda e: e.copy(out=b2acc[0:1, 512:1024], in_=P1e[0:1, 0:512]))
                        else:
                            ch.add("vector", lambda e: e.tensor_tensor(
                                out=b2acc[0:1, 0:512], in0=b2acc[0:1, 0:512],
                                in1=P0e[0:1, 0:512], op=ALU.add))
                            ch.add("vector", lambda e: e.tensor_tensor(
                                out=b2acc[0:1, 512:1024], in0=b2acc[0:1, 512:1024],
                                in1=P1e[0:1, 0:512], op=ALU.add))
                    ch.dma("sync", d_b2s.ap()[b].unsqueeze(0), b2acc[:])
        _ph()
        def _ph():
            with ExitStack() as pf:
                ef = pf.enter_context
                wcsb = ef(nc.sbuf_tensor("F_wcsb", [128, 8, 2304], BF16))
                strips = [ef(nc.sbuf_tensor(f"F_strip{i}", [128, 8, 512], BF16)) for i in range(2)]
                egls = [ef(nc.sbuf_tensor(f"F_egl{i}", [128, 2176], F32)) for i in range(2)]
                rzs = [ef(nc.sbuf_tensor(f"F_rz{i}", [128, 128], F32)) for i in range(2)]
                gnts = [ef(nc.sbuf_tensor(f"F_gnt{i}", [128, 2432], BF16)) for i in range(2)]
                PG = ef(nc.psum_tensor("PG", [128, 2304], F32))
                ch.dma("gpsimd", wcsb[:], wc_i)
                for s_ in agh_sems:
                    _collective_wait(ch, s_)
                cp = ChainP()
                pg_free = None
                for tt in range(512):
                    b, tl = tt // 16, tt % 16
                    tno = cp.begin_tile()
                    cp.wait_tile_end(tno - 3)
                    strip = strips[(tt // 4) % 2]
                    if tt % 4 == 0:
                        g = tt // 4
                        r, bp, t0 = g // 16, (g // 4) % 4, (g % 4) * 512
                        cp.wait_tile_end(4 * (g - 2) + 3)
                        def sdma(e, r=r, bp=bp, t0=t0, strip=strip):
                            ap = bass.AP(agh_outB[bp], r * D * T + t0,
                                         [[T, 128], [128 * T, 8], [1, 512]])
                            return e.dma_start(out=strip[:], in_=ap)
                        cp.add("sync", sdma, 16)
                    egl = egls[tt % 2]
                    rz = rzs[tt % 2]
                    gnt = gnts[tt % 2]
                    cp.wait_tile_end(tno - 2)
                    cp.wait_mark(pg_free)
                    ts = (tt % 4) * 128
                    def gmm(e, ts=ts, strip=strip):
                        for dc in range(8):
                            for p5 in range(5):
                                off = p5 * 512
                                sz = 256 if p5 == 4 else 512
                                r = e.matmul(out=PG[:, off:off + sz],
                                             lhsT=strip[:, dc, ts:ts + 128],
                                             rhs=wcsb[:, dc, off:off + sz],
                                             start=(dc == 0), stop=(dc == 7))
                        return r
                    cp.add("tensor", gmm)
                    pg_g = bass.AP(PG, 0, [[2304, 128], [288, 8], [18, 16], [1, 17]])
                    cp.add("scalar", lambda e, pg_g=pg_g, egl=egl: e.activation(
                        out=egl[:].rearrange("p (h i c) -> p h i c", h=8, i=16),
                        in_=pg_g, func=AF.Exp))
                    pg_v = bass.AP(PG, 17, [[2304, 128], [288, 8], [18, 16]])
                    gnt_vhi = bass.AP(gnt[:].tensor, 17, [[2432, 128], [304, 8], [19, 16]])
                    gnt_vlo = bass.AP(gnt[:].tensor, 18, [[2432, 128], [304, 8], [19, 16]])
                    cp.add("scalar", lambda e, pg_v=pg_v, gnt_vhi=gnt_vhi: e.copy(
                        out=gnt_vhi, in_=pg_v))
                    cp.add("vector", lambda e, pg_v=pg_v, gnt_vhi=gnt_vhi, gnt_vlo=gnt_vlo:
                           e.tensor_tensor(out=gnt_vlo, in0=pg_v, in1=gnt_vhi, op=ALU.subtract))
                    pg_free = cp.mark()
                    cp.add("vector", lambda e, egl=egl, rz=rz: e.tensor_reduce(
                        out=rz[:].rearrange("p (h i) -> p h i", h=8),
                        in_=bass.AP(egl[:].tensor, 0, [[2176, 128], [272, 8], [17, 16], [1, 17]]),
                        axis=AX.X, op=ALU.add))
                    cp.add("vector", lambda e, rz=rz: e.reciprocal(out=rz[:], in_=rz[:]))
                    gnt_A = bass.AP(gnt[:].tensor, 0, [[2432, 128], [304, 8], [19, 16], [1, 17]])
                    rz_b = bass.AP(rz[:].tensor, 0, [[128, 128], [16, 8], [1, 16], [0, 17]])
                    egl_v = bass.AP(egl[:].tensor, 0, [[2176, 128], [272, 8], [17, 16], [1, 17]])
                    cp.add("vector", lambda e, gnt_A=gnt_A, egl_v=egl_v, rz_b=rz_b: e.tensor_tensor(
                        out=gnt_A, in0=egl_v, in1=rz_b, op=ALU.mult))
                    def gdma(e, b=b, tl=tl, gnt=gnt):
                        TH = T // 2
                        half = tl // 8
                        tl2 = tl % 8
                        ap = bass.AP(d_gseqH[half], b * TH * 304 + tl2 * 128 * 304,
                                     [[304, 128], [32 * TH * 304, 8], [1, 304]])
                        return e.dma_start(out=ap, in_=gnt[:].rearrange("p (h x) -> p h x", h=8))
                    cp.add("sync", gdma, 16)
                cp.emit(ch, nc, "Fp")
        _ph()
        def _ph():
            with ExitStack() as pg_:
                eg = pg_.enter_context
                CH = 64
                NCH = T // CH
                gb = eg(nc.sbuf_tensor("G_gb", [128, 2, CH, 304], BF16))
                ub = eg(nc.sbuf_tensor("G_ub", [128, 2, 16, CH], F32))
                ysb = [eg(nc.sbuf_tensor(f"G_ysb{i}", [128, 2, 16, CH], F32)) for i in range(2)]
                ysbf = eg(nc.sbuf_tensor("G_ysbf", [128, 2, 16, CH], BF16))
                st0 = eg(nc.sbuf_tensor("G_st0", [128, 2, 16], F32))
                tmpc = eg(nc.sbuf_tensor("G_tmpc", [128, 16, 2, 16, 17], F32))
                hB = 2 * CH * 304
                uB = 2 * 16 * CH
                gT = None
                cp = ChainP()
                t0_ = cp.begin_tile()
                cp.add("vector", lambda e: e.memset(st0[:], 0.0))
                last_tile = t0_
                for ci in range(NCH):
                    ys = ysb[ci % 2]
                    ysp = ysb[(ci + 1) % 2]
                    tld = cp.begin_tile()
                    cp.wait_tile_end(last_tile)
                    def ldg(e, ci=ci):
                        TH = T // 2
                        half = (ci * CH) // TH
                        ci2 = ci - half * (TH // CH)
                        ap = bass.AP(d_gseqH[half], ci2 * CH * 304,
                                     [[2 * TH * 304, 128], [TH * 304, 2], [304, CH], [1, 304]])
                        return e.dma_start(out=gb[:], in_=ap)
                    cp.add("sync", ldg, 16)
                    hT_ = gb[:].tensor
                    for hp in range(2):
                        cp.begin_tile()
                        cp.wait_tile_end(tld)
                        vhi = bass.AP(hT_, hp * CH * 304 + 17,
                                      [[hB, 128], [304, CH], [19, 16]])
                        vlo = bass.AP(hT_, hp * CH * 304 + 18,
                                      [[hB, 128], [304, CH], [19, 16]])
                        a0 = bass.AP(hT_, hp * CH * 304 + 16,
                                     [[hB, 128], [304, CH], [19, 16]])
                        ubs = bass.AP(ub[:].tensor, hp * 16 * CH,
                                      [[uB, 128], [16, CH], [1, 16]])
                        cp.add("vector", lambda e, vhi=vhi, vlo=vlo, ubs=ubs: e.tensor_tensor(
                            out=ubs, in0=vhi, in1=vlo, op=ALU.add))
                        cp.add("vector", lambda e, a0=a0, ubs=ubs: e.tensor_tensor(
                            out=ubs, in0=ubs, in1=a0, op=ALU.mult))
                    for t in range(CH):
                        for hp in range(2):
                            tno = cp.begin_tile()
                            cp.wait_tile_end(tno - 2)
                            cp.wait_tile_end(tno - 3)
                            t16 = t % 16
                            if t16 == 0:
                                usl = bass.AP(ub[:].tensor, hp * 16 * CH + t * 16,
                                              [[uB, 128], [16, 16], [1, 16]])
                                ucol = bass.AP(tmpc[:].tensor, hp * 272 + 16,
                                               [[16 * 544, 128], [544, 16], [17, 16]])
                                cp.add("vector", lambda e, usl=usl, ucol=ucol: e.tensor_copy(
                                    out=ucol, in_=usl))
                            if ci == 0 and t == 0:
                                sb = bass.AP(st0[:].tensor, hp * 16,
                                             [[32, 128], [0, 16], [1, 16]])
                            else:
                                yprev = ys if t > 0 else ysp
                                tp = t - 1 if t > 0 else CH - 1
                                sb = bass.AP(yprev[:].tensor, hp * 16 * CH + tp * 16,
                                             [[uB, 128], [0, 16], [1, 16]])
                            A = bass.AP(hT_, hp * CH * 304 + t * 304,
                                        [[hB, 128], [19, 16], [1, 16]])
                            tout = bass.AP(tmpc[:].tensor, t16 * 544 + hp * 272,
                                           [[16 * 544, 128], [17, 16], [1, 16]])
                            cp.add("vector", lambda e, A=A, sb=sb, tout=tout: e.tensor_tensor(
                                out=tout, in0=A, in1=sb, op=ALU.mult))
                            tin = bass.AP(tmpc[:].tensor, t16 * 544 + hp * 272,
                                          [[16 * 544, 128], [17, 16], [1, 17]])
                            yout = bass.AP(ys[:].tensor, hp * 16 * CH + t * 16,
                                           [[uB, 128], [1, 16]])
                            cp.add("vector", lambda e, tin=tin, yout=yout: e.tensor_reduce(
                                out=yout, in_=tin, axis=AX.X, op=ALU.add))
                    tst = cp.begin_tile()
                    cp.wait_tile_end(tst - 1)
                    cp.wait_tile_end(tst - 2)
                    def cpy(e, ys=ys):
                        in_ap = bass.AP(ys[:].tensor, 0,
                                        [[uB, 128], [16 * CH, 2], [1, 16], [16, CH]])
                        return e.tensor_copy(out=ysbf[:], in_=in_ap)
                    cp.add("vector", cpy)
                    def sty(e, ci=ci):
                        ap = bass.AP(d_ys, ci * CH,
                                     [[2 * 16 * T, 128], [16 * T, 2], [T, 16], [1, CH]])
                        return e.dma_start(out=ap, in_=ysbf[:])
                    cp.add("sync", sty, 16)
                    last_tile = tst
                cp.emit(ch, nc, "Gp")
                for r_ in range(8):
                    for hh_ in range(8):
                        def rshp(e, r_=r_, hh_=hh_):
                            src_ap = bass.AP(d_ys, (hh_ * 32 + 4 * r_) * 16 * T,
                                             [[T, 16], [16 * T, 4], [1, T]])
                            dst_ap = bass.AP(a2y_in, r_ * 8 * 16 * 4 * T + hh_ * 16 * 4 * T,
                                             [[4 * T, 16], [T, 4], [1, T]])
                            return e.dma_start(out=dst_ap, in_=src_ap)
                        ch.add("sync", rshp, 16)
                _collective(ch, "AllToAll", ALU.bypass, [a2y_in.ap()], [a2y_out.ap()],
                            "ccy", nc)
        _ph()
        def _ph():
            with ExitStack() as ph:
                eh = ph.enter_context
                xn1T = eh(nc.sbuf_tensor("H_xn1T", [128, 8, T], BF16))
                ysTs = [eh(nc.sbuf_tensor(f"H_ysT{i}", [128, 8, 128], BF16)) for i in range(2)]
                woutsb = eh(nc.sbuf_tensor("H_woutsb", [128, 8, D], BF16))
                h1T = eh(nc.sbuf_tensor("H_h1T", [128, 2, 8, T], BF16))
                w2sb = eh(nc.sbuf_tensor("H_w2sb", [128, 2, 8, D], BF16))
                w1strips = [eh(nc.sbuf_tensor(f"H_w1strip{i}", [128, 8, 128], BF16)) for i in range(2)]
                b1rows = eh(nc.sbuf_tensor("H_b1rows", [1, 2, D], BF16))
                b2sum = eh(nc.sbuf_tensor("H_b2sum", [1, D], F32))
                b2rep = eh(nc.sbuf_tensor("H_b2rep", [128, D], F32))
                x1ts = [eh(nc.sbuf_tensor(f"H_x1t{i}", [128, D], F32)) for i in range(2)]
                sgs = [eh(nc.sbuf_tensor(f"H_sg{i}", [128, D], F32)) for i in range(2)]
                g1ts = [eh(nc.sbuf_tensor(f"H_g1t{i}", [128, D], F32)) for i in range(2)]
                PS = [eh(nc.psum_tensor(f"HPS{i}", [128, 512], F32)) for i in range(2)]
                PO = [eh(nc.psum_tensor(f"HPO{i}", [128, 512], F32)) for i in range(2)]
                ch.dma("gpsimd", woutsb[:], woutT_i)
                cp = ChainP()
                prev_b_end = -1
                for b in range(BPC):
                    t_ld = cp.begin_tile()
                    cp.wait_tile_end(prev_b_end)
                    def lds(e, sem, b=b):
                        e.dma_start(out=xn1T[:], in_=d_xn1T.ap()[b]).then_inc(sem, 16)
                        ap = bass.AP(d_w2g, b * 2 * 8 * 128 * D,
                                     [[D, 128], [8 * 128 * D, 2], [128 * D, 8], [1, D]])
                        e.dma_start(out=w2sb[:], in_=ap).then_inc(sem, 16)
                        e.dma_start(out=b1rows[:], in_=d_b1g.ap()[b].unsqueeze(0)).then_inc(sem, 16)
                        return e.dma_start(out=b2sum[:], in_=d_b2s.ap()[b].unsqueeze(0)).then_inc(sem, 16)
                    cp.add("gpsimd", lds, 64, raw=True)
                    for fh in range(2):
                        cp.add("tensor", lambda e, fh=fh: e.matmul(
                            out=PS[0][:, 0:512], lhsT=onesf[0:1, 0:128],
                            rhs=b2sum[0:1, fh * 512:(fh + 1) * 512], start=True, stop=True))
                        cp.add("scalar", lambda e, fh=fh: e.copy(
                            out=b2rep[:, fh * 512:(fh + 1) * 512], in_=PS[0][:, 0:512]))
                    kt = 0
                    for k in range(2):
                        for ffc in range(8):
                            tno = cp.begin_tile()
                            cp.wait_tile_end(tno - 2)
                            cp.wait_tile_end(tno - 3)
                            if kt == 0:
                                cp.wait_tile_end(t_ld)
                            ws = w1strips[kt % 2]
                            def ldw1(e, b=b, k=k, ffc=ffc, ws=ws):
                                ap = bass.AP(d_w1g, ((b * 2 + k) * 8) * 128 * D + ffc * 128,
                                             [[D, 128], [128 * D, 8], [1, 128]])
                                return e.dma_start(out=ws[:], in_=ap)
                            cp.add("sync", ldw1, 16)
                            for tc in range(4):
                                ps = (PS if kt % 2 == 0 else PO)[tc % 2]
                                def hmm_(e, tc=tc, k=k, ffc=ffc, ws=ws, ps=ps):
                                    for dc in range(8):
                                        e.matmul(out=ps[:, 0:512], lhsT=ws[:, dc, :],
                                                 rhs=xn1T[:, dc, tc * 512:(tc + 1) * 512],
                                                 start=(dc == 0), stop=False)
                                    return e.matmul(out=ps[:, 0:512],
                                                    lhsT=b1rows[0:1, k, ffc * 128:(ffc + 1) * 128],
                                                    rhs=onesbf[0:1, 0:512], start=False, stop=True)
                                cp.add("tensor", hmm_)
                                cp.add("scalar", lambda e, k=k, ffc=ffc, tc=tc, ps=ps: e.activation(
                                    out=h1T[:, k, ffc, tc * 512:(tc + 1) * 512], in_=ps[:, 0:512],
                                    func=AF.Gelu))
                            kt += 1
                            last_h1 = tno
                    for tt in range(16):
                        tno = cp.begin_tile()
                        cp.wait_tile_end(tno - 2)
                        cp.wait_tile_end(tno - 3)
                        if tt == 0:
                            cp.wait_tile_end(last_h1)
                        si = tt % 2
                        ps, po = PS[si], PO[si]
                        x1t, sg, g1t = x1ts[si], sgs[si], g1ts[si]
                        ysT = ysTs[si]
                        def ldy(e, b=b, tt=tt, ysT=ysT):
                            ap2 = bass.AP(a2y_out, b * T + tt * 128,
                                          [[4 * T, 128], [8 * 16 * 4 * T, 8], [1, 128]])
                            return e.dma_start(out=ysT[:], in_=ap2)
                        cp.add("sync", ldy, 16)
                        def smm2(e, tt=tt, ps=ps, po=po, ysT=ysT):
                            for dc in range(8):
                                e.matmul(out=ps[:, 0:512],
                                         lhsT=ysT[:, dc, :],
                                         rhs=woutsb[:, dc, 0:512],
                                         start=(dc == 0), stop=(dc == 7))
                            for dc in range(8):
                                r = e.matmul(out=po[:, 0:512],
                                             lhsT=ysT[:, dc, :],
                                             rhs=woutsb[:, dc, 512:1024],
                                             start=(dc == 0), stop=(dc == 7))
                            return r
                        cp.add("tensor", smm2)
                        def lds2(e, sem, b=b, tt=tt, g1t=g1t, x1t=x1t):
                            e.dma_start(out=g1t[:], in_=d_xn1.ap()[b, tt]).then_inc(sem, 16)
                            return e.dma_start(out=x1t[:], in_=d_x1.ap()[b, tt]).then_inc(sem, 16)
                        cp.add("sync", lds2, 32, raw=True)
                        cp.add("vector", lambda e, sg=sg, g1t=g1t, ps=ps: e.tensor_tensor(
                            out=sg[:, 0:512], in0=ps[:, 0:512], in1=g1t[:, 0:512], op=ALU.add))
                        cp.add("vector", lambda e, sg=sg, g1t=g1t, po=po: e.tensor_tensor(
                            out=sg[:, 512:1024], in0=po[:, 0:512], in1=g1t[:, 512:1024], op=ALU.add))
                        def omm2(e, tt=tt, ps=ps, po=po):
                            for fh in range(2):
                                dst = ps if fh == 0 else po
                                n = 0
                                for k in range(2):
                                    for ffc in range(8):
                                        r = e.matmul(out=dst[:, 0:512],
                                                     lhsT=h1T[:, k, ffc, tt * 128:(tt + 1) * 128],
                                                     rhs=w2sb[:, k, ffc, fh * 512:(fh + 1) * 512],
                                                     start=(n == 0), stop=(n == 15))
                                        n += 1
                            return r
                        cp.add("tensor", omm2)
                        cp.add("vector", lambda e, g1t=g1t, ps=ps: e.tensor_tensor(
                            out=g1t[:, 0:512], in0=ps[:, 0:512],
                            in1=b2rep[:, 0:512], op=ALU.add))
                        cp.add("vector", lambda e, g1t=g1t, po=po: e.tensor_tensor(
                            out=g1t[:, 512:1024], in0=po[:, 0:512],
                            in1=b2rep[:, 512:1024], op=ALU.add))
                        cp.add("scalar", lambda e, g1t=g1t: e.activation(out=g1t[:], in_=g1t[:], func=AF.Silu, scale=0.5))
                        cp.add("vector", lambda e, sg=sg, g1t=g1t: e.tensor_tensor(out=sg[:], in0=sg[:], in1=g1t[:], op=ALU.mult))
                        cp.add("vector", lambda e, x1t=x1t, sg=sg: e.tensor_tensor(out=x1t[:], in0=x1t[:], in1=sg[:], op=ALU.add))
                        cp.add("sync", lambda e, b=b, tt=tt, x1t=x1t: e.dma_start(
                            out=out_o[b, tt], in_=x1t[:]), 16)
                        prev_b_end = tno
                cp.emit(ch, nc, "Hp")
        _ph()
        ch.build(nc)
    return nc


def kernel_fused(**inputs):
    nc = _NC_CACHE.setdefault("fused", build_fused())
    x = np.asarray(inputs["L0_attn_x"], np.float32)
    W_attn = inputs["L0_attn_W_attn"]
    W_proj = inputs["L0_attn_W_proj"]
    top_idx = np.asarray(inputs["L0_ffn_top_idx"])
    cw = inputs["L0_ffn_combine_weights"]
    W1 = inputs["L0_ffn_W1"]; b1 = inputs["L0_ffn_b1"]
    W2 = inputs["L0_ffn_W2"]; b2 = inputs["L0_ffn_b2"]
    nw = inputs["L1_attn_norm_w"]; W_v = inputs["L1_attn_W_v"]
    W_a = inputs["L1_attn_W_a"]; W_out = inputs["L1_attn_W_out"]
    Wr = inputs["L1_ffn_Wr"]; br = inputs["L1_ffn_br"]
    W1f = inputs["L1_ffn_W1"]; b1f = inputs["L1_ffn_b1"]
    W2f = inputs["L1_ffn_W2"]; b2f = inputs["L1_ffn_b2"]

    tok = top_idx.T.astype(np.int32)
    xf = x.reshape(B * T, D)
    wqk = np.ascontiguousarray(W_attn[:, :2048].reshape(8, 128, 16, 128).transpose(2, 1, 0, 3))
    wv = np.ascontiguousarray(W_attn[:, 2048:].reshape(8, 128, D).transpose(1, 0, 2))
    wp = np.ascontiguousarray(W_proj.reshape(8, 128, D).transpose(1, 0, 2))
    eyes1 = np.ascontiguousarray(np.stack(
        [c * np.eye(64, dtype=np.float32) for c in (7.0, 15.0, 13.0)]).transpose(1, 0, 2))
    eyes = np.concatenate([eyes1, eyes1], axis=0)
    idxi = np.ascontiguousarray(tok.reshape(8, 16, 128).transpose(2, 0, 1))
    wr_r = np.ascontiguousarray(Wr.reshape(8, 128, 8).transpose(1, 0, 2))
    woutT = np.ascontiguousarray(W_out.T.reshape(8, 128, D).transpose(1, 0, 2))
    Wa_r = W_a.reshape(H1, M1, M1 + 1, D)
    in_maps = []
    for c in range(NCORES):
        wc_full = np.empty((D, 8, 16, 18), np.float32)
        for hh in range(8):
            hg = 8 * c + hh
            blk = Wa_r[hg]
            wc_full[:, hh, :, 0:16] = blk.transpose(2, 0, 1)[:, :, 1:17]
            wc_full[:, hh, :, 16] = blk.transpose(2, 0, 1)[:, :, 0]
            wc_full[:, hh, :, 17] = W_v[hg * 16:(hg + 1) * 16].T
        wcr = np.ascontiguousarray(
            wc_full.reshape(D, 2304).reshape(8, 128, 2304).transpose(1, 0, 2))
        in_maps.append({
            "x": np.ascontiguousarray(x.reshape(NCORES, BPC * T // 128, 128, D)[c]),
            "gxr": np.ascontiguousarray(xf[tok[c]].reshape(16, 128, D)),
            "wqk": wqk, "wv": wv, "wp": wp,
            "w1": np.ascontiguousarray(W1[c].reshape(8, 128, 32, 128).transpose(1, 2, 0, 3)),
            "w2": np.ascontiguousarray(W2[c].reshape(32, 128, 8, 128).transpose(2, 1, 0, 3)),
            "b1": np.ascontiguousarray(b1[c].reshape(32, 128).T),
            "b2": np.ascontiguousarray(b2[c].reshape(8, 128).T),
            "wvec": np.ascontiguousarray(cw[tok[c], c].reshape(1, 2048)),
            "idxi": idxi, "nw": nw.reshape(1, D), "wr": wr_r, "br": br.reshape(1, 8),
            "wc": wcr, "woutT": woutT,
            "w1f": W1f.reshape(8192, D), "w2f": W2f.reshape(8192, D),
            "b1f": b1f, "b2f": b2f,
            "eyes": eyes, "eye64": np.concatenate([np.eye(64, dtype=np.float32)] * 2, axis=0),
            "onesf": np.ones((128, 128), np.float32),
            "onesbf": _bf(np.ones((128, 512), np.float32)),
            "iota1024": np.broadcast_to(np.arange(1024, dtype=np.float32), (128, 1024)).copy(),
            "iotacol": np.arange(128, dtype=np.float32).reshape(128, 1),
            "rbase": np.full((128, 1), 1024.0 * c, np.float32),
            "iscore": np.full((128, 1), 1.0 if c == 0 else 0.0, np.float32),
            "eye4r": np.eye(BPC, dtype=np.float32).reshape(1, BPC * BPC),
            "iota8": np.broadcast_to(np.arange(8, dtype=np.float32), (128, 8)).copy(),
        })
    res = _run(nc, in_maps)
    out = np.stack([r["out"] for r in res.results])
    return out.reshape(B, T, D)


kernel = kernel_fused
